# version 06
# baseline (speedup 1.0000x reference)
import numpy as np
from contextlib import ExitStack

import concourse.bass as bass
import concourse.mybir as mybir
from concourse.bass_utils import run_bass_kernel_spmd
from concourse.alu_op_type import AluOpType as OP

F32 = mybir.dt.float32
BF16 = mybir.dt.bfloat16
AF = mybir.ActivationFunctionType
AX = mybir.AxisListType

NCORES = 8
TOK = 1024
NCH = 16
LN_EPS = 1e-5
RMS_EPS = 1e-6
NE = 8
IC = 11
SIC = 22

GATE_ORDER = [h * 16 + ty * 4 + q for h in range(4) for q in range(4)
              for ty in (1, 0, 3)] + \
             [h * 16 + 2 * 4 + q for h in range(4) for q in range(4)]


class Sem:
    def __init__(self, ctx, nc, name):
        self.h = ctx.enter_context(nc.semaphore(name))
        self.n = 0

    def inc(self, k=1):
        self.n += k
        return self.n


class Plan:
    ENGS = ("t", "v", "s", "g", "y")

    def __init__(self, nc, ctx):
        self.nc = nc
        self.ops = {e: [] for e in self.ENGS}
        self.last_wait = {}
        self.sem = {e: Sem(ctx, nc, f"prog_{e}") for e in ("t", "v", "s")}
        self.cc = Sem(ctx, nc, "cc")
        self.dirty_w = {}
        self.dirty_r = {}

    def emit(self, eng, fn):
        self.ops[eng].append(fn)

    def wait(self, eng, tok):
        if tok is None:
            return
        sem, val = tok
        if sem is self.sem.get(eng):
            return
        key = (eng, id(sem))
        if self.last_wait.get(key, -1) >= val:
            return
        self.last_wait[key] = val
        self.emit(eng, lambda e, s=sem.h, v=val: e.wait_ge(s, v))

    def waits(self, eng, toks):
        for tk in toks:
            self.wait(eng, tk)

    @staticmethod
    def _names(aps):
        out = []
        for a in aps:
            if a is None or isinstance(a, (int, float)):
                continue
            n = getattr(a, "name", None)
            if n is not None:
                out.append(n)
        return out

    def run(self, eng, fn, waits=(), inc=False, r=(), w=()):
        self.waits(eng, waits)
        if eng in ("v", "s"):
            rn, wn = set(self._names(r)), set(self._names(w))
            dw = self.dirty_w.setdefault(eng, set())
            dr = self.dirty_r.setdefault(eng, set())
            if (rn & dw) or (wn & dw) or (wn & dr):
                self.emit(eng, lambda e: e.drain())
                dw.clear()
                dr.clear()
            dw |= wn
            dr |= rn
        if inc:
            sem = self.sem[eng]
            v = sem.inc(1)

            def wrapped(e, fn=fn, s=sem.h):
                fn(e).then_inc(s, 1)
            self.emit(eng, wrapped)
            return (sem, v)
        self.emit(eng, fn)
        return None


class DmaQueue:
    def __init__(self, plan, eng):
        self.plan = plan
        self.eng = eng

    def dma(self, dst, src, sem, waits=()):
        self.plan.waits(self.eng, waits)
        v = sem.inc(16)
        self.plan.emit(self.eng,
                       lambda e, d=dst, s=src, sm=sem.h: e.dma_start(d, s).then_inc(sm, 16))
        return (sem, v)


class WStream:

    def __init__(self, plan, ctx, nc, nslots=3, cols=2816):
        self.plan = plan
        self.slots = [ctx.enter_context(nc.sbuf_tensor(f"wslot{i}", [128, cols], BF16))
                      for i in range(nslots)]
        self.sems = [Sem(ctx, nc, f"wsem{i}") for i in range(nslots)]
        self.free_tok = [None] * nslots
        self.idx = 0

    def load(self, src_ap, ncols):
        i = self.idx % len(self.slots)
        self.idx += 1
        slot = self.slots[i]
        p = self.plan
        if self.free_tok[i] is not None:
            p.wait("g", self.free_tok[i])
        v = self.sems[i].inc(16)
        p.emit("g", lambda e, d=slot[:, 0:ncols], s=src_ap, sm=self.sems[i].h:
               e.dma_start(d, s).then_inc(sm, 16))
        return i, slot, (self.sems[i], v)

    def free(self, i, tok):
        self.free_tok[i] = tok


class PsumPool:
    def __init__(self, ctx, nc, n=4):
        self.ps = [ctx.enter_context(nc.psum_tensor(f"ps{i}", [128, TOK], F32))
                   for i in range(n)]
        self.free_tok = [None] * n
        self.held = [False] * n
        self.idx = 0

    def acquire(self):
        for _ in range(len(self.ps)):
            i = self.idx % len(self.ps)
            self.idx += 1
            if not self.held[i]:
                self.held[i] = True
                return i, self.ps[i], self.free_tok[i]
        raise RuntimeError("all psum pools held")

    def release(self, i, tok):
        assert self.held[i]
        self.held[i] = False
        self.free_tok[i] = tok


def build(debug_taps=None):
    debug_taps = debug_taps or {}
    nc = bass.Bass("TRN2", target_bir_lowering=False)
    ctx = ExitStack()

    def dram_in(name, shape):
        return nc.dram_tensor(name, list(shape), F32, kind="ExternalInput")

    x_slab = dram_in("x_slab", (NCH, 128, TOK))
    wmod_slab = dram_in("wmod_slab", (12, 128, 2048))
    cT = dram_in("cT", (128, 32))
    w_in_slab = dram_in("w_in_slab", (16, 128, 2048))
    w_gate_slab = dram_in("w_gate_slab", (64, 128, 2048))
    b_gate_t = dram_in("b_gate_t", (128, 64))
    w_out_slab = dram_in("w_out_slab", (16, 128, 2048))
    w1_slab = dram_in("w1_slab", (NE, IC, 128, 2048))
    w3_slab = dram_in("w3_slab", (NE, IC, 128, 2048))
    w2_slab = dram_in("w2_slab", (NE, 16, 128, 1408))
    sw1_slab = dram_in("sw1_slab", (SIC, 128, 2048))
    sw3_slab = dram_in("sw3_slab", (SIC, 128, 2048))
    sw2_slab = dram_in("sw2_slab", (16, 128, 2816))
    gate_w_t = dram_in("gate_w_t", (128, 128))
    rms_t_in = dram_in("rms_t", (128, 4))
    cvec_in = dram_in("cvec", (128, 8))
    ident_in = dram_in("ident", (128, 128))

    y_out = nc.dram_tensor("y_slab", [NCH, 128, TOK], F32, kind="ExternalOutput")

    ag1_in = nc.dram_tensor("ag1_in", [1, 24 * 128], F32)
    ag1_out = nc.dram_tensor("ag1_out", [1, 24 * 128 * NCORES], F32)
    ag2_in = nc.dram_tensor("ag2_in", [1, 32 * 128], F32)
    ag2_out = nc.dram_tensor("ag2_out", [1, 32 * 128 * NCORES], F32)
    agw_in = nc.dram_tensor("agw_in", [1, 16], F32)
    agw_out = nc.dram_tensor("agw_out", [1, 16 * NCORES], F32)
    x2s = nc.dram_tensor("x2s", [NCH, 128, TOK], F32)

    dbg = {}
    for name, (shape, dt) in debug_taps.items():
        dbg[name] = nc.dram_tensor("dbg_" + name, list(shape), dt,
                                   kind="ExternalOutput")

    sb = lambda n, s, d: ctx.enter_context(nc.sbuf_tensor(n, s, d))
    BIGB = sb("BIGB", [128, NCH * TOK], BF16)
    BIGC = sb("BIGC", [128, NCH * TOK], BF16)
    BIGD = sb("BIGD", [128, NCH * TOK], BF16)
    BIGE = sb("BIGE", [128, NCH * TOK], BF16)

    fbuf = [sb(f"fbuf{i}", [128, TOK], F32) for i in range(2)]
    ubuf = [sb(f"ubuf{i}", [128, TOK], BF16) for i in range(2)]
    scr32 = [sb(f"scr32_{i}", [128, TOK], F32) for i in range(4)]
    scrb = [sb(f"scrb_{i}", [128, TOK], BF16) for i in range(6)]
    inv_t = sb("inv_t", [128, TOK], BF16)
    m2_t = sb("m2_t", [128, TOK], BF16)

    modl = sb("modl", [128, 24], F32)
    MG = sb("MG", [128, 192], F32)
    V = sb("V", [128, 96], F32)
    ABx = sb("ABx", [128, 32], F32)
    ABG = sb("ABG", [128, 256], F32)
    SCH = sb("SCH", [128, 128], F32)
    zci = sb("zci", [128, 16], F32)
    bt = sb("bt", [128, 64], F32)
    bth = sb("bth", [128, 64], F32)
    rms_t = sb("rms_sb", [128, 4], F32)
    cvec = sb("cvec_sb", [128, 8], F32)
    cT_sb = sb("cT_sb", [128, 32], F32)
    siluc = sb("siluc", [128, 32], BF16)
    gw_sb = sb("gw_sb", [128, 128], BF16)
    ident = sb("ident_sb", [128, 128], BF16)
    ones_m = sb("ones_m", [128, 128], BF16)
    ones_r = sb("ones_r", [128, 128], BF16)
    onesrow = sb("onesrow", [1, 128], BF16)
    zero1 = sb("zero1", [128, 1], F32)
    one1 = sb("one1", [128, 1], F32)
    eps_ln = sb("eps_ln", [128, 1], F32)
    eps_rms = sb("eps_rms", [128, 1], F32)
    LG = sb("LG", [8, TOK], BF16)
    LTM = sb("LTM", [128, 64], F32)
    WTM = sb("WTM", [128, 64], F32)
    wtb = sb("wtb", [128, 16], BF16)
    WEM = sb("WEM", [8, TOK], BF16)
    WROW1 = sb("WROW1", [1, TOK], BF16)
    WBC = sb("WBC", [128, TOK], BF16)
    m1s = sb("m1s", [128, 1], F32)
    m2s = sb("m2s", [128, 1], F32)
    nm1 = sb("nm1", [128, 1], F32)
    e2s = sb("e2s", [128, 1], F32)
    rds = sb("rds", [128, 1], F32)
    eqs = sb("eqs", [128, 8], F32)
    els = sb("els", [128, 8], F32)
    ges = sb("ges", [128, 8], F32)
    w1s = sb("w1s", [128, 8], F32)

    accD = BIGD.ap().bitcast(F32)
    accE = BIGE.ap().bitcast(F32)

    def acc_chunk(dc):
        base = accD if dc < 8 else accE
        return base[:, (dc % 8) * TOK:(dc % 8 + 1) * TOK]

    def big(t, ch):
        return t[:, ch * TOK:(ch + 1) * TOK]

    plan = Plan(nc, ctx)
    P = plan
    CC = plan.cc

    g_q = DmaQueue(plan, "g")
    y_q = DmaQueue(plan, "y")
    s_cst = Sem(ctx, nc, "s_cst")
    s_gcst = Sem(ctx, nc, "s_gcst")
    s_xbf = [Sem(ctx, nc, f"s_xbf{i}") for i in range(4)]
    s_agw = Sem(ctx, nc, "s_agw")
    s_ag1i = Sem(ctx, nc, "s_ag1i")
    s_ag2i = Sem(ctx, nc, "s_ag2i")
    s_mod = Sem(ctx, nc, "s_mod")
    s_abg = Sem(ctx, nc, "s_abg")
    s_xsl = [Sem(ctx, nc, f"s_xsl{i}") for i in range(3)]
    s_x2w = [Sem(ctx, nc, f"s_x2w{i}") for i in range(2)]
    s_wr = Sem(ctx, nc, "s_wr")
    s_out = [Sem(ctx, nc, f"s_out{i}") for i in range(3)]
    s_dbg = Sem(ctx, nc, "s_dbg")

    ws = WStream(plan, ctx, nc, nslots=3, cols=2816)
    pp = PsumPool(ctx, nc, 4)

    def mm(psum_ap, w_ap, x_ap, start, stop, waits=(), inc=False):
        return P.run("t", lambda e, o=psum_ap, w=w_ap, x=x_ap, st=start, sp=stop:
                     e.matmul(o, w, x, start=st, stop=sp), waits=waits, inc=inc)

    def mm_group(ps, wk_slice, rhs_fn, nk, waits=(), per_k=None):
        tok = None
        for k in range(nk):
            w_ap = wk_slice(k)
            for hh in range(2):
                last = (k == nk - 1 and hh == 1)
                wts = list(waits) if (k == 0 and hh == 0) else []
                if per_k is not None and hh == 0:
                    wts.append(per_k[k])
                tok = mm(ps[:, hh * 512:(hh + 1) * 512], w_ap,
                         rhs_fn(k, hh), start=(k == 0), stop=(k == nk - 1),
                         waits=wts, inc=last)
        return tok

    def se_act(out_ap, in_ap, func, bias=0.0, scale=1.0, waits=(), inc=True):
        return P.run("s", lambda e, o=out_ap, i=in_ap, f=func, b=bias, sc=scale:
                     e.activation(o, i, f, bias=b, scale=sc), waits=waits, inc=inc,
                     r=[in_ap, bias, scale], w=[out_ap])

    def v_tt(out_ap, a_ap, b_ap, op, waits=(), inc=True):
        return P.run("v", lambda e, o=out_ap, a=a_ap, b=b_ap, p=op:
                     e.tensor_tensor(o, a, b, p), waits=waits, inc=inc,
                     r=[a_ap, b_ap], w=[out_ap])

    def v_ts(out_ap, in_ap, sc1, sc2, op0, op1, waits=(), inc=True):
        if op1 is None:
            return P.run("v", lambda e, o=out_ap, i=in_ap, s1=sc1, p0=op0:
                         e.tensor_scalar(o, i, s1, None, p0), waits=waits, inc=inc,
                         r=[in_ap, sc1], w=[out_ap])
        return P.run("v", lambda e, o=out_ap, i=in_ap, s1=sc1, s2=sc2, p0=op0, p1=op1:
                     e.tensor_scalar(o, i, s1, s2, p0, p1), waits=waits, inc=inc,
                     r=[in_ap, sc1, sc2], w=[out_ap])

    def v_stt(out_ap, in0_ap, sc, in1_ap, op0, op1, waits=(), inc=True):
        return P.run("v", lambda e, o=out_ap, i0=in0_ap, s=sc, i1=in1_ap, p0=op0, p1=op1:
                     e.scalar_tensor_tensor(o, i0, s, i1, p0, p1), waits=waits, inc=inc,
                     r=[in0_ap, sc, in1_ap], w=[out_ap])

    def v_copy(out_ap, in_ap, waits=(), inc=True):
        return P.run("v", lambda e, o=out_ap, i=in_ap: e.tensor_copy(o, i),
                     waits=waits, inc=inc, r=[in_ap], w=[out_ap])

    def v_scan(out_ap, d0, d1, init, op0, op1, waits=(), inc=True):
        return P.run("v", lambda e, o=out_ap, a=d0, b=d1, s=init, p0=op0, p1=op1:
                     e.tensor_tensor_scan(o, a, b, s, p0, p1), waits=waits, inc=inc,
                     r=[d0, d1, init], w=[out_ap])

    def v_memset(ap, val, waits=(), inc=False):
        return P.run("v", lambda e, a=ap, v=val: e.memset(a, v), waits=waits,
                     inc=inc, w=[ap])

    def v_reduce(out_ap, in_ap, op, waits=(), inc=True):
        return P.run("v", lambda e, o=out_ap, i=in_ap, p=op:
                     e.tensor_reduce(o, i, AX.X, p), waits=waits, inc=inc,
                     r=[in_ap], w=[out_ap])

    dbg_tok = {}
    _tap_prev = [None]

    def tap(name, ap, waits=()):
        if name in dbg:
            w = list(waits) + ([_tap_prev[0]] if _tap_prev[0] else [])
            dbg_tok[name] = y_q.dma(dbg[name].ap(), ap, s_dbg, waits=w)
            _tap_prev[0] = dbg_tok[name]

    def dtok(name):
        return [dbg_tok[name]] if name in dbg_tok else []

    def collective(in_t, out_t, waits=()):
        P.waits("g", waits)
        P.emit("g", lambda e, a=in_t, b=out_t: e.collective_compute(
            "AllGather", OP.bypass, replica_groups=[list(range(NCORES))],
            ins=[a.ap().opt()], outs=[b.ap().opt()]).then_inc(CC.h, 1))
        return (CC, CC.inc(1))

    tk_z = v_memset(zero1[:], 0.0, inc=True)
    t_agwz = y_q.dma(agw_in.ap(), zero1[0:16, :], s_agw, waits=[tk_z])
    collective(agw_in, agw_out, waits=[t_agwz])

    y_q.dma(cT_sb[:], cT.ap(), s_cst)
    y_q.dma(bt[:], b_gate_t.ap(), s_cst)
    y_q.dma(rms_t[:], rms_t_in.ap(), s_cst)
    t_cst_all = y_q.dma(cvec[:], cvec_in.ap(), s_cst)
    t_cT = t_bt = t_rms = t_cvec = t_cst_all
    g_q.dma(ident[:], ident_in.ap(), s_gcst)
    t_gcst_all = g_q.dma(gw_sb[:], gate_w_t.ap(), s_gcst)
    t_ident = t_gw = t_gcst_all

    v_memset(ones_m[:], 1.0 / 2048.0)
    v_memset(ones_r[:], 1.0 / 512.0)
    v_memset(onesrow[:], 1.0)
    v_memset(eps_ln[:], LN_EPS)
    v_memset(eps_rms[:], RMS_EPS)
    tk_ones = P.run("v", lambda e: e.memset(one1[:], 1.0), inc=True, w=[one1[:]])

    tk_bth = v_ts(bth[:], bt[:], 0.5, None, OP.mult, None, waits=[t_bt])
    tk_siluc = se_act(siluc[:], cT_sb[:], AF.Silu, waits=[t_cT])

    pi_mod, ps_mod, _ = pp.acquire()
    tok_mod = None
    for cm in range(12):
        si, slot, rdy = ws.load(wmod_slab[cm], 2048)
        for k in range(16):
            wts = [rdy, tk_siluc] if k == 0 else ()
            tok_mod = mm(ps_mod[:, cm * 2:(cm + 1) * 2],
                         slot[:, k * 128:(k + 1) * 128],
                         siluc[:, k * 2:(k + 1) * 2], start=(k == 0), stop=(k == 15),
                         waits=wts, inc=(k == 15))
        ws.free(si, tok_mod)
    tk_modl = v_copy(modl[:], ps_mod[:, 0:24], waits=[tok_mod])
    pp.release(pi_mod, tk_modl)
    t_ag1in = y_q.dma(ag1_in.ap(), modl[:], s_ag1i, waits=[tk_modl])
    tok_ag1 = collective(ag1_in, ag1_out, waits=[t_ag1in])

    t_MG = y_q.dma(bass.AP(MG, 0, [[192, 128], [1, 192]]),
                   bass.AP(ag1_out, 0, [[24, 128], [3072, 8], [1, 24]]),
                   s_mod, waits=[tok_ag1])
    mgb = lambda b: bass.AP(MG, b, [[192, 128], [24, 8], [2, 12]])
    vap = bass.AP(V, 0, [[96, 128], [12, 8], [1, 12]])
    v_ts(vap, mgb(0), cvec[:, 0:1], None, OP.mult, None,
         waits=[t_MG, t_cvec], inc=False)
    v_stt(vap, mgb(1), cvec[:, 1:2], vap, OP.mult, OP.add, inc=False)
    v_ts(V[:, 16:32], V[:, 16:32], 1.0, None, OP.add, None, inc=False)
    v_ts(V[:, 64:80], V[:, 64:80], 1.0, None, OP.add, None, inc=False)
    tk_V = None
    for c0 in (32, 80):
        gsl = V[:, c0:c0 + 16]
        v_ts(SCH[:, 0:16], gsl, 0.25, None, OP.max, None, inc=False)
        v_ts(SCH[:, 16:32], gsl, -0.25, None, OP.min, None, inc=False)
        v_ts(SCH[:, 32:48], gsl, 0.0, None, OP.is_ge, None, inc=False)
        v_tt(SCH[:, 48:64], SCH[:, 0:16], SCH[:, 16:32], OP.subtract, inc=False)
        v_tt(SCH[:, 48:64], SCH[:, 32:48], SCH[:, 48:64], OP.mult, inc=False)
        tk_V = v_tt(gsl, SCH[:, 48:64], SCH[:, 16:32], OP.add)
    tap("V", V[:], waits=[tk_V])

    def Vcol(c):
        return V[:, c:c + 1]

    xbf = BIGD
    xbf_grp = []
    for j in range(4):
        xbf_grp.append(g_q.dma(xbf[:, j * 4 * TOK:(j + 1) * 4 * TOK],
                               bass.AP(x_slab, j * 4 * 128 * TOK,
                                       [[TOK, 128], [128 * TOK, 4], [1, TOK]]),
                               s_xbf[j]))
    xbf_chunk = [xbf_grp[k // 4] for k in range(NCH)]

    def ln_stats(src_big, dep_toks, per_chunk=None):
        pc = per_chunk or [None] * NCH
        pi_m, ps_m, ptok_m = pp.acquire()
        pi_s, ps_s, ptok_s = pp.acquire()
        tokm = None
        for k in range(NCH):
            for hh in range(2):
                wts = (list(dep_toks) + [ptok_m, pc[k]]) if hh == 0 else ()
                tokm = mm(ps_m[:, hh * 512:(hh + 1) * 512], ones_m[:],
                          big(src_big, k)[:, hh * 512:(hh + 1) * 512],
                          start=(k == 0), stop=(k == NCH - 1), waits=wts,
                          inc=(k == NCH - 1 and hh == 1))
        toks = None
        mm_toks = []
        for k in range(NCH):
            sl = scrb[k % 2]
            w = [mm_toks[k - 2]] if k >= 2 else []
            tkq = v_tt(sl[:], big(src_big, k), big(src_big, k), OP.mult,
                       waits=(list(dep_toks) if k == 0 else []) + w + [pc[k]])
            for hh in range(2):
                wts = [tkq, ptok_s] if (k == 0 and hh == 0) else [tkq]
                toks = mm(ps_s[:, hh * 512:(hh + 1) * 512], ones_m[:],
                          sl[:, hh * 512:(hh + 1) * 512],
                          start=(k == 0), stop=(k == NCH - 1),
                          waits=wts, inc=(hh == 1))
            mm_toks.append(toks)
        return pi_m, ps_m, pi_s, ps_s, tokm, toks

    def ln_invm2(ps_m, ps_s, tok_m, tok_s, guard=()):
        tk1 = se_act(scr32[0][:], ps_m[:], AF.Square,
                     waits=[tok_m] + list(guard))
        tk2 = v_tt(scr32[1][:], ps_s[:], scr32[0][:], OP.subtract,
                   waits=[tok_s, tk1])
        tk3 = se_act(scr32[2][:], scr32[1][:], AF.Ln, bias=eps_ln[:], waits=[tk2, tk_ones])
        tk4 = se_act(inv_t[:], scr32[2][:], AF.Exp, scale=-0.5, waits=[tk3])
        tk5 = v_tt(m2_t[:], ps_m[:], inv_t[:], OP.mult, waits=[tk4])
        return tk4, tk5

    pi_m, ps_m, pi_s, ps_s, tok_mean, tok_sq = ln_stats(
        xbf, [], per_chunk=xbf_chunk)
    tok_inv, tok_m2 = ln_invm2(ps_m, ps_s, tok_mean, tok_sq)
    pp.release(pi_m, tok_m2)
    pp.release(pi_s, tok_m2)
    tap("inv1", inv_t[:], waits=[tok_inv])

    tok_h1 = None
    for k in range(NCH):
        sl = scrb[2]
        v_tt(sl[:], big(xbf, k), inv_t[:], OP.mult,
             waits=[tok_inv] if k == 0 else (), inc=False)
        v_tt(sl[:], sl[:], m2_t[:], OP.subtract,
             waits=[tok_m2] if k == 0 else (), inc=False)
        tok_h1 = v_ts(big(BIGB, k), sl[:], Vcol(16 + k), Vcol(0 + k),
                      OP.mult, OP.add, waits=[tk_V] if k == 0 else ())
    tap("h1", BIGB.ap(), waits=[tok_h1])

    tok_xp_evac = []
    for c in range(NCH):
        si, slot, rdy = ws.load(w_in_slab[c], 2048)
        pi, ps, ptok = pp.acquire()
        tokm = mm_group(ps, lambda k, s=slot: s[:, k * 128:(k + 1) * 128],
                        lambda k, hh: big(BIGB, k)[:, hh * 512:(hh + 1) * 512],
                        16, waits=[rdy, tok_h1, ptok])
        ws.free(si, tokm)
        tke = se_act(big(BIGC, c), ps[:], AF.Copy, bias=0.0, scale=1.0,
                     waits=[tokm])
        pp.release(pi, tke)
        tok_xp_evac.append(tke)
    tok_xp = tok_xp_evac[-1]
    tap("xp", BIGC.ap(), waits=[tok_xp])

    prev_f = prev_u = None
    tok_last_scan = None
    for ch in range(NCH):
        pos = ch * 3
        f_sl = fbuf[ch % 2]
        u_sl = ubuf[ch % 2]
        si, slot, rdy = ws.load(w_gate_slab[pos], 2048)
        pi, ps, ptok = pp.acquire()
        tokm = mm_group(ps, lambda k, s=slot: s[:, k * 128:(k + 1) * 128],
                        lambda k, hh: big(BIGC, k)[:, hh * 512:(hh + 1) * 512],
                        16, waits=[rdy, tok_xp, ptok])
        ws.free(si, tokm)
        tk_th = se_act(scr32[0][:], ps[:], AF.Tanh, bias=bth[:, pos:pos + 1],
                       scale=0.5,
                       waits=[tokm, tk_bth] + ([prev_f] if prev_f else []))
        tk_f = v_ts(f_sl[:], scr32[0][:], 0.5, 0.5, OP.mult, OP.add,
                    waits=[tk_th])
        prev_f = tk_f
        pp.release(pi, tk_f)
        si, slot, rdy = ws.load(w_gate_slab[pos + 1], 2048)
        pi, ps, ptok = pp.acquire()
        tokm = mm_group(ps, lambda k, s=slot: s[:, k * 128:(k + 1) * 128],
                        lambda k, hh: big(BIGC, k)[:, hh * 512:(hh + 1) * 512],
                        16, waits=[rdy, ptok])
        ws.free(si, tokm)
        tk_i = se_act(scr32[1][:], ps[:], AF.Exp, bias=bt[:, pos + 1:pos + 2],
                      waits=[tokm] + ([prev_u] if prev_u else []))
        pp.release(pi, tk_i)
        si, slot, rdy = ws.load(w_gate_slab[pos + 2], 2048)
        pi, ps, ptok = pp.acquire()
        tokm = mm_group(ps, lambda k, s=slot: s[:, k * 128:(k + 1) * 128],
                        lambda k, hh: big(BIGC, k)[:, hh * 512:(hh + 1) * 512],
                        16, waits=[rdy, ptok])
        ws.free(si, tokm)
        tk_cd = se_act(scr32[2][:], ps[:], AF.Tanh, bias=bt[:, pos + 2:pos + 3],
                       waits=[tokm] + ([prev_u] if prev_u else []))
        tk_u = v_tt(u_sl[:], scr32[1][:], scr32[2][:], OP.mult,
                    waits=[tk_i, tk_cd])
        prev_u = tk_u
        pp.release(pi, tk_u)
        v_scan(big(BIGD, ch), f_sl[:], u_sl[:], zero1[:], OP.mult, OP.add,
               waits=[tk_f, tk_u, tk_ones], inc=False)
        tok_last_scan = v_scan(big(BIGE, ch), f_sl[:], u_sl[:], one1[:],
                               OP.mult, OP.bypass)

    tok_o = []
    for ch in range(NCH):
        pos = 48 + ch
        si, slot, rdy = ws.load(w_gate_slab[pos], 2048)
        pi, ps, ptok = pp.acquire()
        tokm = mm_group(ps, lambda k, s=slot: s[:, k * 128:(k + 1) * 128],
                        lambda k, hh: big(BIGC, k)[:, hh * 512:(hh + 1) * 512],
                        16, waits=[rdy, ptok])
        ws.free(si, tokm)
        tk_o = se_act(big(BIGB, ch), ps[:], AF.Sigmoid, bias=bt[:, pos:pos + 1],
                      waits=[tokm] + dtok("h1"))
        pp.release(pi, tk_o)
        tok_o.append(tk_o)

    tkA = v_copy(ABx[:, 0:16],
                 bass.AP(BIGE, TOK - 1, [[NCH * TOK, 128], [TOK, NCH]]),
                 waits=[tok_last_scan])
    tkB = v_copy(ABx[:, 16:32],
                 bass.AP(BIGD, TOK - 1, [[NCH * TOK, 128], [TOK, NCH]]))
    t_ag2in = y_q.dma(ag2_in.ap(), ABx[:], s_ag2i, waits=[tkB])
    tok_ag2 = collective(ag2_in, ag2_out, waits=[t_ag2in])
    tap("ctil", BIGD.ap(), waits=[tkB])
    tap("Pcum", BIGE.ap(), waits=[tkB])

    t_abg = y_q.dma(bass.AP(ABG, 0, [[256, 128], [1, 256]]),
                    bass.AP(ag2_out, 0, [[32, 128], [4096, 8], [1, 32]]),
                    s_abg, waits=[tok_ag2])

    def abg_A(r):
        return ABG[:, r * 32:r * 32 + 16]

    def abg_B(r):
        return ABG[:, r * 32 + 16:r * 32 + 32]

    def sch(r):
        return SCH[:, r * 16:(r + 1) * 16]

    v_copy(sch(0), abg_B(0), waits=[t_abg], inc=False)
    v_copy(sch(4), abg_B(4), inc=False)
    for r in (1, 2, 3, 5, 6, 7):
        v_tt(sch(r), abg_A(r), sch(r - 1), OP.mult, inc=False)
        v_tt(sch(r), sch(r), abg_B(r), OP.add, inc=False)
    v_memset(zci[:], 0.0)
    tk_ci = None
    for idx, r in enumerate((0, 1, 2, 4, 5, 6)):
        tk_ci = v_stt(zci[:], sch(r), cvec[:, 2 + idx:3 + idx], zci[:],
                      OP.mult, OP.add, waits=[t_cvec] if idx == 0 else ())
    tap("cin", zci[:], waits=[tk_ci])
    tok_fix = []
    for ch in range(NCH):
        tok_fix.append(v_stt(big(BIGD, ch), big(BIGE, ch), zci[:, ch:ch + 1],
                             big(BIGD, ch), OP.mult, OP.add,
                             waits=([tk_ci] + dtok("ctil") + dtok("Pcum"))
                             if ch == 0 else ()))
    tap("cfix", BIGD.ap(), waits=[tok_fix[-1]])

    rsq = [scrb[2 + h] for h in range(4)]
    tok_rsq = []
    csq_mm_tok = [None, None]
    for h in range(4):
        pi, ps, ptok = pp.acquire()
        tokm = None
        for q in range(4):
            ch = h * 4 + q
            sl = scrb[q % 2]
            w = [csq_mm_tok[q % 2]] if csq_mm_tok[q % 2] else []
            tkq = v_tt(sl[:], big(BIGD, ch), big(BIGD, ch), OP.mult,
                       waits=[tok_fix[ch]] + w)
            for hh in range(2):
                wts = [tkq, ptok] if (q == 0 and hh == 0) else [tkq]
                tokm = mm(ps[:, hh * 512:(hh + 1) * 512], ones_r[:],
                          sl[:, hh * 512:(hh + 1) * 512],
                          start=(q == 0), stop=(q == 3),
                          waits=wts, inc=(hh == 1))
            csq_mm_tok[q % 2] = tokm
        tk1 = se_act(scr32[0][:], ps[:], AF.Ln, bias=eps_rms[:], waits=[tokm, tk_ones])
        tk2 = se_act(rsq[h][:], scr32[0][:], AF.Exp, scale=-0.5, waits=[tk1])
        pp.release(pi, tk2)
        tok_rsq.append(tk2)
    fb_bf = [f.ap().bitcast(BF16) for f in fbuf]
    cns = [scrb[0][:], scrb[1][:], fb_bf[0][:, 0:TOK], fb_bf[1][:, 0:TOK]]
    ths = [ubuf[0][:], ubuf[1][:]]
    tkt_l = [None] * NCH
    tkh_l = [None] * NCH
    tkc_l = [None] * NCH

    def emit_tkc(ch):
        h = ch // 4
        cw = [tok_rsq[h]]
        if ch >= 4 and tkt_l[ch - 4] is not None:
            cw.append(tkt_l[ch - 4])
        tkc_l[ch] = v_tt(cns[ch % 4], big(BIGD, ch), rsq[h][:], OP.mult,
                         waits=cw)

    def emit_tkt(ch):
        q = ch % 4
        tw = [tkc_l[ch], t_rms]
        if ch >= 2 and tkh_l[ch - 2] is not None:
            tw.append(tkh_l[ch - 2])
        tkt_l[ch] = se_act(ths[ch % 2], cns[ch % 4], AF.Tanh,
                           scale=rms_t[:, q:q + 1], waits=tw)

    def emit_tkh(ch):
        tkh_l[ch] = v_tt(big(BIGB, ch), big(BIGB, ch), ths[ch % 2], OP.mult,
                         waits=[tkt_l[ch], tok_o[ch]])

    for ch in range(NCH):
        emit_tkc(ch)
        emit_tkt(ch)
        if ch >= 2:
            emit_tkh(ch - 2)
    emit_tkh(NCH - 2)
    emit_tkh(NCH - 1)
    tok_hout = tkh_l[NCH - 1]
    tap("hout", BIGB.ap(), waits=[tok_hout])

    tok_x2bf = []
    for dc in range(NCH):
        si, slot, rdy = ws.load(w_out_slab[dc], 2048)
        pi, ps, ptok = pp.acquire()
        tokm = mm_group(ps, lambda k, s=slot: s[:, k * 128:(k + 1) * 128],
                        lambda k, hh: big(BIGB, k)[:, hh * 512:(hh + 1) * 512],
                        16, waits=[rdy, tok_hout, ptok])
        ws.free(si, tokm)
        xs = scr32[dc % 3]
        xw = [tok_fix[-1], tok_rsq[-1]] if dc < 3 else [tok_x2bf[dc - 3]]
        t_x = y_q.dma(xs[:], x_slab[dc], s_xsl[dc % 3], waits=xw)
        x2w = [tokm, tk_V, t_x]
        if dc == 0:
            x2w += [tok_hout] + dtok("cfix") + dtok("ctil") + dtok("Pcum")
        tk_x2 = v_stt(acc_chunk(dc), ps[:], Vcol(32 + dc), xs[:],
                      OP.mult, OP.add, waits=x2w)
        pp.release(pi, tk_x2)
        tok_x2bf.append(v_copy(big(BIGC, dc), acc_chunk(dc),
                               waits=[tk_x2] + dtok("xp")))
    tok_acc = list(tok_x2bf)
    tok_x2bf_all = tok_x2bf[-1]
    tap("x2bf", BIGC.ap(), waits=[tok_x2bf_all])

    pi_m, ps_m, pi_s, ps_s, tok_mean2, tok_sq2 = ln_stats(
        BIGC, [], per_chunk=tok_x2bf)
    tok_inv2, tok_m22 = ln_invm2(ps_m, ps_s, tok_mean2, tok_sq2,
                                 guard=[tok_x2bf_all] + dtok("inv1"))
    pp.release(pi_m, tok_m22)
    pp.release(pi_s, tok_m22)

    tok_h2_list = []
    for k in range(NCH):
        sl = scrb[2]
        v_tt(sl[:], big(BIGC, k), inv_t[:], OP.mult,
             waits=[tok_inv2] if k == 0 else (), inc=False)
        v_tt(sl[:], sl[:], m2_t[:], OP.subtract,
             waits=[tok_m22] if k == 0 else (), inc=False)
        tok_h2_list.append(v_ts(big(BIGB, k), sl[:], Vcol(64 + k), Vcol(48 + k),
                                OP.mult, OP.add,
                                waits=dtok("hout") if k == 0 else ()))
    tok_h2 = tok_h2_list[-1]
    tap("h2", BIGB.ap(), waits=[tok_h2])

    pi_l, ps_l, ptok_l = pp.acquire()
    tok_lg = None
    for k in range(NCH):
        for hh in range(2):
            wts = [tok_h2_list[k], t_gw, ptok_l] if hh == 0 else ()
            tok_lg = mm(ps_l[0:8, hh * 512:(hh + 1) * 512],
                        gw_sb[:, k * 8:(k + 1) * 8],
                        big(BIGB, k)[:, hh * 512:(hh + 1) * 512],
                        start=(k == 0), stop=(k == NCH - 1), waits=wts,
                        inc=(k == NCH - 1 and hh == 1))
    tk_lgc = v_copy(LG[:], ps_l[0:8, 0:TOK], waits=[tok_lg])
    pp.release(pi_l, tk_lgc)
    tap("logits", LG.ap(), waits=[tk_lgc])

    pi_tr, ps_tr, ptok_tr = pp.acquire()
    pi_tr2, ps_tr2, ptok_tr2 = pp.acquire()
    psb_tr = ps_tr.ap().bitcast(BF16)
    psb_tr2 = ps_tr2.ap().bitcast(BF16)
    _g_state = {"tok_wem": None, "tk_tr2": None}

    def _gate_tt(tt):
        tk_tr = P.run("t", lambda e, o=psb_tr[:, tt * 8:(tt + 1) * 8],
                      i=LG[0:8, tt * 128:(tt + 1) * 128], idn=ident[0:8, 0:8]:
                      e.transpose(o, i, idn),
                      waits=([tk_lgc, t_ident, ptok_tr] if tt == 0 else ()),
                      inc=True)
        l_tm = LTM[:, tt * 8:(tt + 1) * 8]
        v_copy(l_tm, psb_tr[:, tt * 8:(tt + 1) * 8], waits=[tk_tr], inc=False)
        v_reduce(m1s[:], l_tm, OP.max, inc=False)
        v_ts(eqs[:], l_tm, m1s[:], None, OP.is_equal, None, inc=False)
        v_stt(els[:], eqs[:], -1e30, l_tm, OP.mult, OP.add, inc=False)
        v_reduce(m2s[:], els[:], OP.max, inc=False)
        tk_pre = v_ts(e2s[:], m2s[:], m1s[:], None, OP.subtract, None)
        tk_e = se_act(e2s[:], e2s[:], AF.Exp, waits=[tk_pre])
        v_ts(e2s[:], e2s[:], 1.0, None, OP.add, None, waits=[tk_e], inc=False)
        P.run("v", lambda e, o=rds[:], i=e2s[:]: e.reciprocal(o, i), inc=False,
              r=[e2s[:]], w=[rds[:]])
        tk_nm = v_ts(nm1[:], m1s[:], -1.0, None, OP.mult, None)
        tk_el = se_act(els[:], l_tm, AF.Exp, bias=nm1[:], waits=[tk_nm])
        v_ts(ges[:], l_tm, m2s[:], None, OP.is_ge, None, inc=False)
        v_ts(w1s[:], els[:], rds[:], None, OP.mult, None, waits=[tk_el],
             inc=False)
        tk_w = v_tt(WTM[:, tt * 8:(tt + 1) * 8], w1s[:], ges[:], OP.mult)
        tk_wb = v_copy(wtb[:, tt % 2:tt % 2 + 1].bitcast(BF16) if False else
                       wtb[:, (tt % 2) * 8:(tt % 2) * 8 + 8],
                       WTM[:, tt * 8:(tt + 1) * 8],
                       waits=[tk_w] + ([_g_state["tk_tr2"]] if _g_state["tk_tr2"] else []))
        _g_state["tk_wb"] = tk_wb

    def _gate_tt2(tt):
        tk_tr2 = P.run("t", lambda e,
                       o=psb_tr2[0:8, tt * 128:(tt + 1) * 128],
                       i=wtb[:, (tt % 2) * 8:(tt % 2) * 8 + 8], idn=ident[:]:
                       e.transpose(o, i, idn),
                       waits=[_g_state["tk_wb"]] + ([ptok_tr2] if tt == 0 else []),
                       inc=True)
        _g_state["tk_tr2"] = tk_tr2
        _g_state["tok_wem"] = v_copy(WEM[0:8, tt * 128:(tt + 1) * 128],
                                     psb_tr2[0:8, tt * 128:(tt + 1) * 128],
                                     waits=[tk_tr2])
        if tt == 7:
            pp.release(pi_tr, _g_state["tok_wem"])
            pp.release(pi_tr2, _g_state["tok_wem"])
            tap("wem", WEM.ap(), waits=[_g_state["tok_wem"]])

    def _gate_step(j):
        if 1 <= j <= 8:
            _gate_tt2(j - 1)
        if j <= 7:
            _gate_tt(j)

    gate_fillers = [lambda j=j: _gate_step(j) for j in range(9)]

    tok_pe_used = [None] * IC
    experts = [("s", 0)] + [("r", e) for e in range(NE)] + [("s", 1)]
    first = True
    tok_wbc = None
    prev_gp_read = None
    prev_sil_read = [None, None]
    for kind, e in experts:
        routed = kind == "r"
        if routed:
            t_wr = y_q.dma(WROW1[:], WEM[e:e + 1, :], s_wr,
                           waits=[_g_state["tok_wem"]] + ([tok_wbc] if tok_wbc else []))
            pi_b, ps_b, ptok_b = pp.acquire()
            tokb = None
            for hh in range(2):
                wts = [t_wr, ptok_b] if hh == 0 else ()
                tokb = mm(ps_b[:, hh * 512:(hh + 1) * 512], onesrow[:],
                          WROW1[0:1, hh * 512:(hh + 1) * 512],
                          start=True, stop=True, waits=wts, inc=(hh == 1))
            wbw = [tokb]
            if prev_gp_read is not None:
                wbw.append(prev_gp_read)
            tok_wbc = se_act(WBC[:], ps_b[:], AF.Copy, bias=0.0, scale=1.0,
                             waits=wbw)
            pp.release(pi_b, tok_wbc)
        tok_pe_last = None
        for ic in range(IC):
            icg = ic if (routed or e == 0) else ic + IC
            w1src = w1_slab[e][ic] if routed else sw1_slab[icg]
            w3src = w3_slab[e][ic] if routed else sw3_slab[icg]
            si1, slot1, rdy1 = ws.load(w1src, 2048)
            pi_h, ps_h, ptok_h = pp.acquire()
            tok_hm = mm_group(ps_h, lambda k, s=slot1: s[:, k * 128:(k + 1) * 128],
                              lambda k, hh: big(BIGB, k)[:, hh * 512:(hh + 1) * 512],
                              16, waits=[rdy1, tok_h2_list[0], ptok_h],
                              per_k=tok_h2_list)
            ws.free(si1, tok_hm)
            si3, slot3, rdy3 = ws.load(w3src, 2048)
            pi_g, ps_g, ptok_g = pp.acquire()
            tok_gm = mm_group(ps_g, lambda k, s=slot3: s[:, k * 128:(k + 1) * 128],
                              lambda k, hh: big(BIGB, k)[:, hh * 512:(hh + 1) * 512],
                              16, waits=[rdy3, ptok_g])
            ws.free(si3, tok_gm)
            sil = scrb[ic % 2]
            silw = [tok_hm]
            if prev_sil_read[ic % 2] is not None:
                silw.append(prev_sil_read[ic % 2])
            tk_sil = se_act(sil[:], ps_h[:], AF.Silu, waits=silw)
            pp.release(pi_h, tk_sil)
            pe_guard = [tok_pe_used[ic]] if tok_pe_used[ic] else []
            if routed:
                gp = scrb[2 + ic % 2]
                tk_gp = v_tt(gp[:], ps_g[:], WBC[:], OP.mult,
                             waits=[tok_gm, tok_wbc])
                prev_gp_read = tk_gp
                tk_pe = v_tt(big(BIGC, ic), sil[:], gp[:], OP.mult,
                             waits=[tk_sil, tk_gp] + pe_guard + dtok("x2bf"))
            else:
                tk_pe = v_tt(big(BIGC, ic), sil[:], ps_g[:], OP.mult,
                             waits=[tk_sil, tok_gm] + pe_guard)
            prev_sil_read[ic % 2] = tk_pe
            pp.release(pi_g, tk_pe)
            tok_pe_used[ic] = None
            tok_pe_last = tk_pe
            if kind == "s" and e == 0 and ic < 9:
                gate_fillers[ic]()
        for dc in range(NCH):
            w2src = w2_slab[e][dc] if routed else sw2_slab[dc]
            ncols = 1408 if routed else 2816
            coff = 0 if (routed or e == 0) else 1408
            si2, slot2, rdy2 = ws.load(w2src, ncols)
            pi_y, ps_y, ptok_y = pp.acquire()
            toky = None
            for ic in range(IC):
                for hh in range(2):
                    wts = [rdy2, tok_pe_last, ptok_y] if (ic == 0 and hh == 0) else ()
                    toky = mm(ps_y[:, hh * 512:(hh + 1) * 512],
                              slot2[:, coff + ic * 128:coff + (ic + 1) * 128],
                              big(BIGC, ic)[:, hh * 512:(hh + 1) * 512],
                              start=(ic == 0), stop=(ic == IC - 1),
                              waits=wts, inc=(ic == IC - 1 and hh == 1))
            ws.free(si2, toky)
            if dc == NCH - 1:
                for ic2 in range(IC):
                    tok_pe_used[ic2] = toky
            tk_a = v_stt(acc_chunk(dc), ps_y[:], Vcol(80 + dc),
                         acc_chunk(dc), OP.mult, OP.add,
                         waits=[toky, tok_acc[dc]])
            tok_acc[dc] = tk_a
            pp.release(pi_y, tk_a)
        first = False

    for dc in range(NCH):
        y_q.dma(y_out[dc], acc_chunk(dc), s_out[0], waits=[tok_acc[dc]])
    P.emit("y", lambda e, v=s_out[0].n, h=s_out[0].h: e.wait_ge(h, v))
    if dbg:
        P.emit("y", lambda e, v=s_dbg.n: e.wait_ge(s_dbg.h, v))

    with ctx:
        with nc.Block() as block:
            @block.tensor
            def _(e):
                for fn in plan.ops["t"]:
                    fn(e)

            @block.vector
            def _(e):
                for fn in plan.ops["v"]:
                    fn(e)

            @block.scalar
            def _(e):
                for fn in plan.ops["s"]:
                    fn(e)

            @block.gpsimd
            def _(e):
                for fn in plan.ops["g"]:
                    fn(e)

            @block.sync
            def _(e):
                for fn in plan.ops["y"]:
                    fn(e)

    return nc


def _slab(WT):
    din, dout = WT.shape
    nk, C = din // 128, dout // 128
    return np.ascontiguousarray(
        WT.reshape(nk, 128, C, 128).transpose(2, 1, 0, 3).reshape(C, 128, nk * 128))


def prepare_inputs(inputs):
    f32 = lambda k: np.asarray(inputs[k], dtype=np.float32)
    x = f32("L0_attn_x")
    W_in = f32("L0_attn_W_in")
    W_gate = f32("L0_attn_W_gate")
    b_gate = f32("L0_attn_b_gate")
    rms_w = f32("L0_attn_rms_w")
    W_out = f32("L0_attn_W_out")
    gate_w = f32("L0_ffn_gate_w")
    w1 = f32("L0_ffn_w1")
    w3 = f32("L0_ffn_w3")
    w2 = f32("L0_ffn_w2")
    sw1 = f32("L0_ffn_sw1")
    sw3 = f32("L0_ffn_sw3")
    sw2 = f32("L0_ffn_sw2")
    cc = f32("L0_c")
    wmod = f32("L0_wmod")

    shared = {}
    shared["w_in_slab"] = _slab(np.ascontiguousarray(W_in.T))
    wg = _slab(np.ascontiguousarray(W_gate.T))
    shared["w_gate_slab"] = np.ascontiguousarray(wg[GATE_ORDER])
    b2 = np.ascontiguousarray(b_gate.reshape(64, 128).T)
    shared["b_gate_t"] = np.ascontiguousarray(b2[:, GATE_ORDER])
    shared["w_out_slab"] = _slab(np.ascontiguousarray(W_out.T))
    shared["w1_slab"] = np.stack([_slab(w1[e]) for e in range(NE)])
    shared["w3_slab"] = np.stack([_slab(w3[e]) for e in range(NE)])
    shared["w2_slab"] = np.stack([_slab(w2[e]) for e in range(NE)])
    shared["sw1_slab"] = _slab(sw1)
    shared["sw3_slab"] = _slab(sw3)
    shared["sw2_slab"] = _slab(sw2)
    shared["gate_w_t"] = np.ascontiguousarray(
        gate_w.T.reshape(16, 128, 8).transpose(1, 0, 2).reshape(128, 128))
    shared["rms_t"] = np.ascontiguousarray(rms_w.reshape(4, 128).T)
    shared["cT"] = np.ascontiguousarray(
        cc.T.reshape(16, 128, 2).transpose(1, 0, 2).reshape(128, 32))
    shared["ident"] = np.eye(128, dtype=np.float32)

    in_maps = []
    for core in range(NCORES):
        b, q = core // 4, core % 4
        m = dict(shared)
        xs = np.ascontiguousarray(x[b, q * TOK:(q + 1) * TOK, :].T)
        m["x_slab"] = np.ascontiguousarray(xs.reshape(NCH, 128, TOK))
        m["wmod_slab"] = _slab(np.ascontiguousarray(
            wmod[:, core * 1536:(core + 1) * 1536]))
        cv = np.zeros((128, 8), dtype=np.float32)
        cv[:, 0] = 1.0 - b
        cv[:, 1] = float(b)
        for idx, r in enumerate((0, 1, 2, 4, 5, 6)):
            if q != 0 and r == core - 1:
                cv[:, 2 + idx] = 1.0
        m["cvec"] = cv
        in_maps.append(m)
    return in_maps


def kernel(**inputs):
    in_maps = prepare_inputs(inputs)
    nc = build()
    res = run_bass_kernel_spmd(nc, in_maps, core_ids=list(range(NCORES)))
    out = np.empty((2, 4096, 2048), dtype=np.float32)
    for core in range(NCORES):
        b, q = core // 4, core % 4
        y = np.asarray(res.results[core]["y_slab"])
        out[b, q * TOK:(q + 1) * TOK, :] = y.reshape(2048, TOK).T
    return out


# version 07
# speedup vs baseline: 1.0138x; 1.0138x over previous
import numpy as np
from contextlib import ExitStack

import concourse.bass as bass
import concourse.mybir as mybir
from concourse.bass_utils import run_bass_kernel_spmd
from concourse.alu_op_type import AluOpType as OP

F32 = mybir.dt.float32
BF16 = mybir.dt.bfloat16
AF = mybir.ActivationFunctionType
AX = mybir.AxisListType

NCORES = 8
TOK = 1024
NCH = 16
LN_EPS = 1e-5
RMS_EPS = 1e-6
NE = 8
IC = 11
SIC = 22

GATE_ORDER = [h * 16 + ty * 4 + q for h in range(4) for q in range(4)
              for ty in (1, 0, 3)] + \
             [h * 16 + 2 * 4 + q for h in range(4) for q in range(4)]


class Sem:
    def __init__(self, ctx, nc, name):
        self.h = ctx.enter_context(nc.semaphore(name))
        self.n = 0

    def inc(self, k=1):
        self.n += k
        return self.n


class Plan:
    ENGS = ("t", "v", "s", "g", "y")

    def __init__(self, nc, ctx):
        self.nc = nc
        self.ops = {e: [] for e in self.ENGS}
        self.last_wait = {}
        self.sem = {e: Sem(ctx, nc, f"prog_{e}") for e in ("t", "v", "s")}
        self.cc = Sem(ctx, nc, "cc")
        self.dirty_w = {}
        self.dirty_r = {}

    def emit(self, eng, fn):
        self.ops[eng].append(fn)

    def wait(self, eng, tok):
        if tok is None:
            return
        sem, val = tok
        if sem is self.sem.get(eng):
            return
        key = (eng, id(sem))
        if self.last_wait.get(key, -1) >= val:
            return
        self.last_wait[key] = val
        self.emit(eng, lambda e, s=sem.h, v=val: e.wait_ge(s, v))

    def waits(self, eng, toks):
        for tk in toks:
            self.wait(eng, tk)

    @staticmethod
    def _names(aps):
        out = []
        for a in aps:
            if a is None or isinstance(a, (int, float)):
                continue
            n = getattr(a, "name", None)
            if n is not None:
                out.append(n)
        return out

    def run(self, eng, fn, waits=(), inc=False, r=(), w=()):
        self.waits(eng, waits)
        if eng in ("v", "s"):
            rn, wn = set(self._names(r)), set(self._names(w))
            dw = self.dirty_w.setdefault(eng, set())
            dr = self.dirty_r.setdefault(eng, set())
            if (rn & dw) or (wn & dw) or (wn & dr):
                self.emit(eng, lambda e: e.drain())
                dw.clear()
                dr.clear()
            dw |= wn
            dr |= rn
        if inc:
            sem = self.sem[eng]
            v = sem.inc(1)

            def wrapped(e, fn=fn, s=sem.h):
                fn(e).then_inc(s, 1)
            self.emit(eng, wrapped)
            return (sem, v)
        self.emit(eng, fn)
        return None


class DmaQueue:
    def __init__(self, plan, eng):
        self.plan = plan
        self.eng = eng

    def dma(self, dst, src, sem, waits=()):
        self.plan.waits(self.eng, waits)
        v = sem.inc(16)
        self.plan.emit(self.eng,
                       lambda e, d=dst, s=src, sm=sem.h: e.dma_start(d, s).then_inc(sm, 16))
        return (sem, v)


class WStream:

    def __init__(self, plan, ctx, nc, nslots=3, cols=2816):
        self.plan = plan
        self.slots = [ctx.enter_context(nc.sbuf_tensor(f"wslot{i}", [128, cols], BF16))
                      for i in range(nslots)]
        self.sems = [Sem(ctx, nc, f"wsem{i}") for i in range(nslots)]
        self.free_tok = [None] * nslots
        self.idx = 0

    def load(self, src_ap, ncols):
        i = self.idx % len(self.slots)
        self.idx += 1
        slot = self.slots[i]
        p = self.plan
        if self.free_tok[i] is not None:
            p.wait("g", self.free_tok[i])
        v = self.sems[i].inc(16)
        p.emit("g", lambda e, d=slot[:, 0:ncols], s=src_ap, sm=self.sems[i].h:
               e.dma_start(d, s).then_inc(sm, 16))
        return i, slot, (self.sems[i], v)

    def free(self, i, tok):
        self.free_tok[i] = tok


class PsumPool:
    def __init__(self, ctx, nc, n=4):
        self.ps = [ctx.enter_context(nc.psum_tensor(f"ps{i}", [128, TOK], F32))
                   for i in range(n)]
        self.free_tok = [None] * n
        self.held = [False] * n
        self.idx = 0

    def acquire(self):
        for _ in range(len(self.ps)):
            i = self.idx % len(self.ps)
            self.idx += 1
            if not self.held[i]:
                self.held[i] = True
                return i, self.ps[i], self.free_tok[i]
        raise RuntimeError("all psum pools held")

    def release(self, i, tok):
        assert self.held[i]
        self.held[i] = False
        self.free_tok[i] = tok


def build(debug_taps=None):
    debug_taps = debug_taps or {}
    nc = bass.Bass("TRN2", target_bir_lowering=False)
    ctx = ExitStack()

    def dram_in(name, shape):
        return nc.dram_tensor(name, list(shape), F32, kind="ExternalInput")

    x_slab = dram_in("x_slab", (NCH, 128, TOK))
    wmod_slab = dram_in("wmod_slab", (12, 128, 2048))
    cT = dram_in("cT", (128, 32))
    w_in_slab = dram_in("w_in_slab", (16, 128, 2048))
    w_gate_slab = dram_in("w_gate_slab", (64, 128, 2048))
    b_gate_t = dram_in("b_gate_t", (128, 64))
    w_out_slab = dram_in("w_out_slab", (16, 128, 2048))
    w1_slab = dram_in("w1_slab", (NE, IC, 128, 2048))
    w3_slab = dram_in("w3_slab", (NE, IC, 128, 2048))
    w2_slab = dram_in("w2_slab", (NE, 16, 128, 1408))
    sw1_slab = dram_in("sw1_slab", (SIC, 128, 2048))
    sw3_slab = dram_in("sw3_slab", (SIC, 128, 2048))
    sw2_slab = dram_in("sw2_slab", (16, 128, 2816))
    gate_w_t = dram_in("gate_w_t", (128, 128))
    rms_t_in = dram_in("rms_t", (128, 4))
    cvec_in = dram_in("cvec", (128, 8))
    ident_in = dram_in("ident", (128, 128))

    y_out = nc.dram_tensor("y_slab", [NCH, 128, TOK], F32, kind="ExternalOutput")

    ag1_in = nc.dram_tensor("ag1_in", [1, 24 * 128], F32)
    ag1_out = nc.dram_tensor("ag1_out", [1, 24 * 128 * NCORES], F32)
    ag2_in = nc.dram_tensor("ag2_in", [1, 32 * 128], F32)
    ag2_out = nc.dram_tensor("ag2_out", [1, 32 * 128 * NCORES], F32)
    agw_in = nc.dram_tensor("agw_in", [1, 16], F32)
    agw_out = nc.dram_tensor("agw_out", [1, 16 * NCORES], F32)
    x2s = nc.dram_tensor("x2s", [NCH, 128, TOK], F32)

    dbg = {}
    for name, (shape, dt) in debug_taps.items():
        dbg[name] = nc.dram_tensor("dbg_" + name, list(shape), dt,
                                   kind="ExternalOutput")

    sb = lambda n, s, d: ctx.enter_context(nc.sbuf_tensor(n, s, d))
    BIGB = sb("BIGB", [128, NCH * TOK], BF16)
    BIGC = sb("BIGC", [128, NCH * TOK], BF16)
    BIGD = sb("BIGD", [128, NCH * TOK], BF16)
    BIGE = sb("BIGE", [128, NCH * TOK], BF16)

    fbuf = [sb(f"fbuf{i}", [128, TOK], F32) for i in range(2)]
    ubuf = [sb(f"ubuf{i}", [128, TOK], BF16) for i in range(2)]
    scr32 = [sb(f"scr32_{i}", [128, TOK], F32) for i in range(4)]
    scrb = [sb(f"scrb_{i}", [128, TOK], BF16) for i in range(6)]
    inv_t = sb("inv_t", [128, TOK], BF16)
    m2_t = sb("m2_t", [128, TOK], BF16)

    modl = sb("modl", [128, 24], F32)
    MG = sb("MG", [128, 192], F32)
    V = sb("V", [128, 96], F32)
    ABx = sb("ABx", [128, 32], F32)
    ABG = sb("ABG", [128, 256], F32)
    SCH = sb("SCH", [128, 128], F32)
    zci = sb("zci", [128, 16], F32)
    bt = sb("bt", [128, 64], F32)
    bth = sb("bth", [128, 64], F32)
    rms_t = sb("rms_sb", [128, 4], F32)
    cvec = sb("cvec_sb", [128, 8], F32)
    cT_sb = sb("cT_sb", [128, 32], F32)
    siluc = sb("siluc", [128, 32], BF16)
    gw_sb = sb("gw_sb", [128, 128], BF16)
    ident = sb("ident_sb", [128, 128], BF16)
    ones_m = sb("ones_m", [128, 128], BF16)
    ones_r = sb("ones_r", [128, 128], BF16)
    onesrow = sb("onesrow", [1, 128], BF16)
    zero1 = sb("zero1", [128, 1], F32)
    one1 = sb("one1", [128, 1], F32)
    eps_ln = sb("eps_ln", [128, 1], F32)
    eps_rms = sb("eps_rms", [128, 1], F32)
    LG = sb("LG", [8, TOK], BF16)
    LTM = sb("LTM", [128, 64], F32)
    WTM = sb("WTM", [128, 64], F32)
    wtb = sb("wtb", [128, 16], BF16)
    WEM = sb("WEM", [8, TOK], BF16)
    WROW1 = sb("WROW1", [1, TOK], BF16)
    WBC = sb("WBC", [128, TOK], BF16)
    m1s = sb("m1s", [128, 1], F32)
    m2s = sb("m2s", [128, 1], F32)
    nm1 = sb("nm1", [128, 1], F32)
    e2s = sb("e2s", [128, 1], F32)
    rds = sb("rds", [128, 1], F32)
    eqs = sb("eqs", [128, 8], F32)
    els = sb("els", [128, 8], F32)
    ges = sb("ges", [128, 8], F32)
    w1s = sb("w1s", [128, 8], F32)

    accD = BIGD.ap().bitcast(F32)
    accE = BIGE.ap().bitcast(F32)

    def acc_chunk(dc):
        base = accD if dc < 8 else accE
        return base[:, (dc % 8) * TOK:(dc % 8 + 1) * TOK]

    def big(t, ch):
        return t[:, ch * TOK:(ch + 1) * TOK]

    plan = Plan(nc, ctx)
    P = plan
    CC = plan.cc

    g_q = DmaQueue(plan, "g")
    y_q = DmaQueue(plan, "y")
    s_cst = Sem(ctx, nc, "s_cst")
    s_gcst = Sem(ctx, nc, "s_gcst")
    s_xbf = [Sem(ctx, nc, f"s_xbf{i}") for i in range(4)]
    s_agw = Sem(ctx, nc, "s_agw")
    s_ag1i = Sem(ctx, nc, "s_ag1i")
    s_ag2i = Sem(ctx, nc, "s_ag2i")
    s_mod = Sem(ctx, nc, "s_mod")
    s_abg = Sem(ctx, nc, "s_abg")
    s_xsl = [Sem(ctx, nc, f"s_xsl{i}") for i in range(3)]
    s_x2w = [Sem(ctx, nc, f"s_x2w{i}") for i in range(2)]
    s_wr = Sem(ctx, nc, "s_wr")
    s_out = [Sem(ctx, nc, f"s_out{i}") for i in range(3)]
    s_dbg = Sem(ctx, nc, "s_dbg")

    ws = WStream(plan, ctx, nc, nslots=3, cols=2816)
    pp = PsumPool(ctx, nc, 4)

    def mm(psum_ap, w_ap, x_ap, start, stop, waits=(), inc=False):
        return P.run("t", lambda e, o=psum_ap, w=w_ap, x=x_ap, st=start, sp=stop:
                     e.matmul(o, w, x, start=st, stop=sp), waits=waits, inc=inc)

    def mm_group(ps, wk_slice, rhs_fn, nk, waits=(), per_k=None):
        tok = None
        for k in range(nk):
            w_ap = wk_slice(k)
            for hh in range(2):
                last = (k == nk - 1 and hh == 1)
                wts = list(waits) if (k == 0 and hh == 0) else []
                if per_k is not None and hh == 0:
                    wts.append(per_k[k])
                tok = mm(ps[:, hh * 512:(hh + 1) * 512], w_ap,
                         rhs_fn(k, hh), start=(k == 0), stop=(k == nk - 1),
                         waits=wts, inc=last)
        return tok

    def mm_super(cs, w_srcs, ncols, rhs_big, per_k, out_fn, extra_first=()):
        slots = []
        for c in cs:
            slots.append(ws.load(w_srcs[c], ncols))
        pools = [pp.acquire() for _ in cs]
        tokm = None
        for k in range(16):
            for j, c in enumerate(cs):
                si, slot, rdy = slots[j]
                pi, ps, ptok = pools[j]
                for hh in range(2):
                    wts = []
                    if hh == 0 and j == 0:
                        wts.append(per_k[k])
                    if k == 0 and hh == 0:
                        wts += [rdy, ptok] + (list(extra_first) if j == 0 else [])
                    tokm = mm(ps[:, hh * 512:(hh + 1) * 512],
                              slot[:, k * 128:(k + 1) * 128],
                              big(rhs_big, k)[:, hh * 512:(hh + 1) * 512],
                              start=(k == 0), stop=(k == 15), waits=wts,
                              inc=(k == 15 and hh == 1))
        for j, c in enumerate(cs):
            si, slot, rdy = slots[j]
            pi, ps, ptok = pools[j]
            ws.free(si, tokm)
            tke = out_fn(c, ps, tokm)
            pp.release(pi, tke)

    def se_act(out_ap, in_ap, func, bias=0.0, scale=1.0, waits=(), inc=True):
        return P.run("s", lambda e, o=out_ap, i=in_ap, f=func, b=bias, sc=scale:
                     e.activation(o, i, f, bias=b, scale=sc), waits=waits, inc=inc,
                     r=[in_ap, bias, scale], w=[out_ap])

    def v_tt(out_ap, a_ap, b_ap, op, waits=(), inc=True):
        return P.run("v", lambda e, o=out_ap, a=a_ap, b=b_ap, p=op:
                     e.tensor_tensor(o, a, b, p), waits=waits, inc=inc,
                     r=[a_ap, b_ap], w=[out_ap])

    def v_ts(out_ap, in_ap, sc1, sc2, op0, op1, waits=(), inc=True):
        if op1 is None:
            return P.run("v", lambda e, o=out_ap, i=in_ap, s1=sc1, p0=op0:
                         e.tensor_scalar(o, i, s1, None, p0), waits=waits, inc=inc,
                         r=[in_ap, sc1], w=[out_ap])
        return P.run("v", lambda e, o=out_ap, i=in_ap, s1=sc1, s2=sc2, p0=op0, p1=op1:
                     e.tensor_scalar(o, i, s1, s2, p0, p1), waits=waits, inc=inc,
                     r=[in_ap, sc1, sc2], w=[out_ap])

    def v_stt(out_ap, in0_ap, sc, in1_ap, op0, op1, waits=(), inc=True):
        return P.run("v", lambda e, o=out_ap, i0=in0_ap, s=sc, i1=in1_ap, p0=op0, p1=op1:
                     e.scalar_tensor_tensor(o, i0, s, i1, p0, p1), waits=waits, inc=inc,
                     r=[in0_ap, sc, in1_ap], w=[out_ap])

    def v_copy(out_ap, in_ap, waits=(), inc=True):
        return P.run("v", lambda e, o=out_ap, i=in_ap: e.tensor_copy(o, i),
                     waits=waits, inc=inc, r=[in_ap], w=[out_ap])

    def v_scan(out_ap, d0, d1, init, op0, op1, waits=(), inc=True):
        return P.run("v", lambda e, o=out_ap, a=d0, b=d1, s=init, p0=op0, p1=op1:
                     e.tensor_tensor_scan(o, a, b, s, p0, p1), waits=waits, inc=inc,
                     r=[d0, d1, init], w=[out_ap])

    def v_memset(ap, val, waits=(), inc=False):
        return P.run("v", lambda e, a=ap, v=val: e.memset(a, v), waits=waits,
                     inc=inc, w=[ap])

    def v_reduce(out_ap, in_ap, op, waits=(), inc=True):
        return P.run("v", lambda e, o=out_ap, i=in_ap, p=op:
                     e.tensor_reduce(o, i, AX.X, p), waits=waits, inc=inc,
                     r=[in_ap], w=[out_ap])

    dbg_tok = {}
    _tap_prev = [None]

    def tap(name, ap, waits=()):
        if name in dbg:
            w = list(waits) + ([_tap_prev[0]] if _tap_prev[0] else [])
            dbg_tok[name] = y_q.dma(dbg[name].ap(), ap, s_dbg, waits=w)
            _tap_prev[0] = dbg_tok[name]

    def dtok(name):
        return [dbg_tok[name]] if name in dbg_tok else []

    def collective(in_t, out_t, waits=()):
        P.waits("g", waits)
        P.emit("g", lambda e, a=in_t, b=out_t: e.collective_compute(
            "AllGather", OP.bypass, replica_groups=[list(range(NCORES))],
            ins=[a.ap().opt()], outs=[b.ap().opt()]).then_inc(CC.h, 1))
        return (CC, CC.inc(1))

    tk_z = v_memset(zero1[:], 0.0, inc=True)
    t_agwz = y_q.dma(agw_in.ap(), zero1[0:16, :], s_agw, waits=[tk_z])
    collective(agw_in, agw_out, waits=[t_agwz])

    y_q.dma(cT_sb[:], cT.ap(), s_cst)
    y_q.dma(bt[:], b_gate_t.ap(), s_cst)
    y_q.dma(rms_t[:], rms_t_in.ap(), s_cst)
    t_cst_all = y_q.dma(cvec[:], cvec_in.ap(), s_cst)
    t_cT = t_bt = t_rms = t_cvec = t_cst_all
    g_q.dma(ident[:], ident_in.ap(), s_gcst)
    t_gcst_all = g_q.dma(gw_sb[:], gate_w_t.ap(), s_gcst)
    t_ident = t_gw = t_gcst_all

    v_memset(ones_m[:], 1.0 / 2048.0)
    v_memset(ones_r[:], 1.0 / 512.0)
    v_memset(onesrow[:], 1.0)
    v_memset(eps_ln[:], LN_EPS)
    v_memset(eps_rms[:], RMS_EPS)
    tk_ones = P.run("v", lambda e: e.memset(one1[:], 1.0), inc=True, w=[one1[:]])

    tk_bth = v_ts(bth[:], bt[:], 0.5, None, OP.mult, None, waits=[t_bt])
    tk_siluc = se_act(siluc[:], cT_sb[:], AF.Silu, waits=[t_cT])

    pi_mod, ps_mod, _ = pp.acquire()
    tok_mod = None
    for cm in range(12):
        si, slot, rdy = ws.load(wmod_slab[cm], 2048)
        for k in range(16):
            wts = [rdy, tk_siluc] if k == 0 else ()
            tok_mod = mm(ps_mod[:, cm * 2:(cm + 1) * 2],
                         slot[:, k * 128:(k + 1) * 128],
                         siluc[:, k * 2:(k + 1) * 2], start=(k == 0), stop=(k == 15),
                         waits=wts, inc=(k == 15))
        ws.free(si, tok_mod)
    tk_modl = v_copy(modl[:], ps_mod[:, 0:24], waits=[tok_mod])
    pp.release(pi_mod, tk_modl)
    t_ag1in = y_q.dma(ag1_in.ap(), modl[:], s_ag1i, waits=[tk_modl])
    tok_ag1 = collective(ag1_in, ag1_out, waits=[t_ag1in])

    t_MG = y_q.dma(bass.AP(MG, 0, [[192, 128], [1, 192]]),
                   bass.AP(ag1_out, 0, [[24, 128], [3072, 8], [1, 24]]),
                   s_mod, waits=[tok_ag1])
    mgb = lambda b: bass.AP(MG, b, [[192, 128], [24, 8], [2, 12]])
    vap = bass.AP(V, 0, [[96, 128], [12, 8], [1, 12]])
    v_ts(vap, mgb(0), cvec[:, 0:1], None, OP.mult, None,
         waits=[t_MG, t_cvec], inc=False)
    v_stt(vap, mgb(1), cvec[:, 1:2], vap, OP.mult, OP.add, inc=False)
    v_ts(V[:, 16:32], V[:, 16:32], 1.0, None, OP.add, None, inc=False)
    v_ts(V[:, 64:80], V[:, 64:80], 1.0, None, OP.add, None, inc=False)
    tk_V = None
    for c0 in (32, 80):
        gsl = V[:, c0:c0 + 16]
        v_ts(SCH[:, 0:16], gsl, 0.25, None, OP.max, None, inc=False)
        v_ts(SCH[:, 16:32], gsl, -0.25, None, OP.min, None, inc=False)
        v_ts(SCH[:, 32:48], gsl, 0.0, None, OP.is_ge, None, inc=False)
        v_tt(SCH[:, 48:64], SCH[:, 0:16], SCH[:, 16:32], OP.subtract, inc=False)
        v_tt(SCH[:, 48:64], SCH[:, 32:48], SCH[:, 48:64], OP.mult, inc=False)
        tk_V = v_tt(gsl, SCH[:, 48:64], SCH[:, 16:32], OP.add)
    tap("V", V[:], waits=[tk_V])

    def Vcol(c):
        return V[:, c:c + 1]

    xbf = BIGD
    xbf_grp = []
    for j in range(4):
        xbf_grp.append(g_q.dma(xbf[:, j * 4 * TOK:(j + 1) * 4 * TOK],
                               bass.AP(x_slab, j * 4 * 128 * TOK,
                                       [[TOK, 128], [128 * TOK, 4], [1, TOK]]),
                               s_xbf[j]))
    xbf_chunk = [xbf_grp[k // 4] for k in range(NCH)]

    def ln_stats(src_big, dep_toks, per_chunk=None):
        pc = per_chunk or [None] * NCH
        pi_m, ps_m, ptok_m = pp.acquire()
        pi_s, ps_s, ptok_s = pp.acquire()
        tokm = None
        for k in range(NCH):
            for hh in range(2):
                wts = (list(dep_toks) + [ptok_m, pc[k]]) if hh == 0 else ()
                tokm = mm(ps_m[:, hh * 512:(hh + 1) * 512], ones_m[:],
                          big(src_big, k)[:, hh * 512:(hh + 1) * 512],
                          start=(k == 0), stop=(k == NCH - 1), waits=wts,
                          inc=(k == NCH - 1 and hh == 1))
        toks = None
        mm_toks = []
        for k in range(NCH):
            sl = scrb[k % 2]
            w = [mm_toks[k - 2]] if k >= 2 else []
            tkq = v_tt(sl[:], big(src_big, k), big(src_big, k), OP.mult,
                       waits=(list(dep_toks) if k == 0 else []) + w + [pc[k]])
            for hh in range(2):
                wts = [tkq, ptok_s] if (k == 0 and hh == 0) else [tkq]
                toks = mm(ps_s[:, hh * 512:(hh + 1) * 512], ones_m[:],
                          sl[:, hh * 512:(hh + 1) * 512],
                          start=(k == 0), stop=(k == NCH - 1),
                          waits=wts, inc=(hh == 1))
            mm_toks.append(toks)
        return pi_m, ps_m, pi_s, ps_s, tokm, toks

    def ln_invm2(ps_m, ps_s, tok_m, tok_s, guard=()):
        tk1 = se_act(scr32[0][:], ps_m[:], AF.Square,
                     waits=[tok_m] + list(guard))
        tk2 = v_tt(scr32[1][:], ps_s[:], scr32[0][:], OP.subtract,
                   waits=[tok_s, tk1])
        tk3 = se_act(scr32[2][:], scr32[1][:], AF.Ln, bias=eps_ln[:], waits=[tk2, tk_ones])
        tk4 = se_act(inv_t[:], scr32[2][:], AF.Exp, scale=-0.5, waits=[tk3])
        tk5 = v_tt(m2_t[:], ps_m[:], inv_t[:], OP.mult, waits=[tk4])
        return tk4, tk5

    pi_m, ps_m, pi_s, ps_s, tok_mean, tok_sq = ln_stats(
        xbf, [], per_chunk=xbf_chunk)
    tok_inv, tok_m2 = ln_invm2(ps_m, ps_s, tok_mean, tok_sq)
    pp.release(pi_m, tok_m2)
    pp.release(pi_s, tok_m2)
    tap("inv1", inv_t[:], waits=[tok_inv])

    tok_h1_list = []
    for k in range(NCH):
        sl = scrb[2]
        v_tt(sl[:], big(xbf, k), inv_t[:], OP.mult,
             waits=[tok_inv] if k == 0 else (), inc=False)
        v_tt(sl[:], sl[:], m2_t[:], OP.subtract,
             waits=[tok_m2] if k == 0 else (), inc=False)
        tok_h1_list.append(v_ts(big(BIGB, k), sl[:], Vcol(16 + k), Vcol(0 + k),
                                OP.mult, OP.add,
                                waits=[tk_V] if k == 0 else ()))
    tok_h1 = tok_h1_list[-1]
    tap("h1", BIGB.ap(), waits=[tok_h1])

    tok_xp_evac = []

    def _xp_out(c, ps, tokm):
        tke = se_act(big(BIGC, c), ps[:], AF.Copy, bias=0.0, scale=1.0,
                     waits=[tokm])
        tok_xp_evac.append(tke)
        return tke

    mm_super([0, 1], w_in_slab, 2048, BIGB, tok_h1_list, _xp_out)
    mm_super([2, 3], w_in_slab, 2048, BIGB, tok_h1_list, _xp_out)
    for c in range(4, NCH):
        si, slot, rdy = ws.load(w_in_slab[c], 2048)
        pi, ps, ptok = pp.acquire()
        tokm = mm_group(ps, lambda k, s=slot: s[:, k * 128:(k + 1) * 128],
                        lambda k, hh: big(BIGB, k)[:, hh * 512:(hh + 1) * 512],
                        16, waits=[rdy, tok_h1, ptok])
        ws.free(si, tokm)
        tke = _xp_out(c, ps, tokm)
        pp.release(pi, tke)
    tok_xp = tok_xp_evac[-1]
    tap("xp", BIGC.ap(), waits=[tok_xp])

    prev_f = prev_u = None
    tok_last_scan = None
    for ch in range(NCH):
        pos = ch * 3
        f_sl = fbuf[ch % 2]
        u_sl = ubuf[ch % 2]
        si, slot, rdy = ws.load(w_gate_slab[pos], 2048)
        pi, ps, ptok = pp.acquire()
        tokm = mm_group(ps, lambda k, s=slot: s[:, k * 128:(k + 1) * 128],
                        lambda k, hh: big(BIGC, k)[:, hh * 512:(hh + 1) * 512],
                        16, waits=[rdy, tok_xp, ptok])
        ws.free(si, tokm)
        tk_th = se_act(scr32[0][:], ps[:], AF.Tanh, bias=bth[:, pos:pos + 1],
                       scale=0.5,
                       waits=[tokm, tk_bth] + ([prev_f] if prev_f else []))
        tk_f = v_ts(f_sl[:], scr32[0][:], 0.5, 0.5, OP.mult, OP.add,
                    waits=[tk_th])
        prev_f = tk_f
        pp.release(pi, tk_f)
        si, slot, rdy = ws.load(w_gate_slab[pos + 1], 2048)
        pi, ps, ptok = pp.acquire()
        tokm = mm_group(ps, lambda k, s=slot: s[:, k * 128:(k + 1) * 128],
                        lambda k, hh: big(BIGC, k)[:, hh * 512:(hh + 1) * 512],
                        16, waits=[rdy, ptok])
        ws.free(si, tokm)
        tk_i = se_act(scr32[1][:], ps[:], AF.Exp, bias=bt[:, pos + 1:pos + 2],
                      waits=[tokm] + ([prev_u] if prev_u else []))
        pp.release(pi, tk_i)
        si, slot, rdy = ws.load(w_gate_slab[pos + 2], 2048)
        pi, ps, ptok = pp.acquire()
        tokm = mm_group(ps, lambda k, s=slot: s[:, k * 128:(k + 1) * 128],
                        lambda k, hh: big(BIGC, k)[:, hh * 512:(hh + 1) * 512],
                        16, waits=[rdy, ptok])
        ws.free(si, tokm)
        tk_cd = se_act(scr32[2][:], ps[:], AF.Tanh, bias=bt[:, pos + 2:pos + 3],
                       waits=[tokm] + ([prev_u] if prev_u else []))
        tk_u = v_tt(u_sl[:], scr32[1][:], scr32[2][:], OP.mult,
                    waits=[tk_i, tk_cd])
        prev_u = tk_u
        pp.release(pi, tk_u)
        v_scan(big(BIGD, ch), f_sl[:], u_sl[:], zero1[:], OP.mult, OP.add,
               waits=[tk_f, tk_u, tk_ones], inc=False)
        tok_last_scan = v_scan(big(BIGE, ch), f_sl[:], u_sl[:], one1[:],
                               OP.mult, OP.bypass)

    tok_o = []
    for ch in range(NCH):
        pos = 48 + ch
        si, slot, rdy = ws.load(w_gate_slab[pos], 2048)
        pi, ps, ptok = pp.acquire()
        tokm = mm_group(ps, lambda k, s=slot: s[:, k * 128:(k + 1) * 128],
                        lambda k, hh: big(BIGC, k)[:, hh * 512:(hh + 1) * 512],
                        16, waits=[rdy, ptok])
        ws.free(si, tokm)
        tk_o = se_act(big(BIGB, ch), ps[:], AF.Sigmoid, bias=bt[:, pos:pos + 1],
                      waits=[tokm] + dtok("h1"))
        pp.release(pi, tk_o)
        tok_o.append(tk_o)

    tkA = v_copy(ABx[:, 0:16],
                 bass.AP(BIGE, TOK - 1, [[NCH * TOK, 128], [TOK, NCH]]),
                 waits=[tok_last_scan])
    tkB = v_copy(ABx[:, 16:32],
                 bass.AP(BIGD, TOK - 1, [[NCH * TOK, 128], [TOK, NCH]]))
    t_ag2in = y_q.dma(ag2_in.ap(), ABx[:], s_ag2i, waits=[tkB])
    tok_ag2 = collective(ag2_in, ag2_out, waits=[t_ag2in])
    tap("ctil", BIGD.ap(), waits=[tkB])
    tap("Pcum", BIGE.ap(), waits=[tkB])

    t_abg = y_q.dma(bass.AP(ABG, 0, [[256, 128], [1, 256]]),
                    bass.AP(ag2_out, 0, [[32, 128], [4096, 8], [1, 32]]),
                    s_abg, waits=[tok_ag2])

    def abg_A(r):
        return ABG[:, r * 32:r * 32 + 16]

    def abg_B(r):
        return ABG[:, r * 32 + 16:r * 32 + 32]

    def sch(r):
        return SCH[:, r * 16:(r + 1) * 16]

    v_copy(sch(0), abg_B(0), waits=[t_abg], inc=False)
    v_copy(sch(4), abg_B(4), inc=False)
    for r in (1, 2, 3, 5, 6, 7):
        v_tt(sch(r), abg_A(r), sch(r - 1), OP.mult, inc=False)
        v_tt(sch(r), sch(r), abg_B(r), OP.add, inc=False)
    v_memset(zci[:], 0.0)
    tk_ci = None
    for idx, r in enumerate((0, 1, 2, 4, 5, 6)):
        tk_ci = v_stt(zci[:], sch(r), cvec[:, 2 + idx:3 + idx], zci[:],
                      OP.mult, OP.add, waits=[t_cvec] if idx == 0 else ())
    tap("cin", zci[:], waits=[tk_ci])
    tok_fix = []
    for ch in range(NCH):
        tok_fix.append(v_stt(big(BIGD, ch), big(BIGE, ch), zci[:, ch:ch + 1],
                             big(BIGD, ch), OP.mult, OP.add,
                             waits=([tk_ci] + dtok("ctil") + dtok("Pcum"))
                             if ch == 0 else ()))
    tap("cfix", BIGD.ap(), waits=[tok_fix[-1]])

    rsq = [scrb[2 + h] for h in range(4)]
    tok_rsq = []
    csq_mm_tok = [None, None]
    for h in range(4):
        pi, ps, ptok = pp.acquire()
        tokm = None
        for q in range(4):
            ch = h * 4 + q
            sl = scrb[q % 2]
            w = [csq_mm_tok[q % 2]] if csq_mm_tok[q % 2] else []
            tkq = v_tt(sl[:], big(BIGD, ch), big(BIGD, ch), OP.mult,
                       waits=[tok_fix[ch]] + w)
            for hh in range(2):
                wts = [tkq, ptok] if (q == 0 and hh == 0) else [tkq]
                tokm = mm(ps[:, hh * 512:(hh + 1) * 512], ones_r[:],
                          sl[:, hh * 512:(hh + 1) * 512],
                          start=(q == 0), stop=(q == 3),
                          waits=wts, inc=(hh == 1))
            csq_mm_tok[q % 2] = tokm
        tk1 = se_act(scr32[0][:], ps[:], AF.Ln, bias=eps_rms[:], waits=[tokm, tk_ones])
        tk2 = se_act(rsq[h][:], scr32[0][:], AF.Exp, scale=-0.5, waits=[tk1])
        pp.release(pi, tk2)
        tok_rsq.append(tk2)
    fb_bf = [f.ap().bitcast(BF16) for f in fbuf]
    cns = [scrb[0][:], scrb[1][:], fb_bf[0][:, 0:TOK], fb_bf[1][:, 0:TOK]]
    ths = [ubuf[0][:], ubuf[1][:]]
    tkt_l = [None] * NCH
    tkh_l = [None] * NCH
    tkc_l = [None] * NCH

    def emit_tkc(ch):
        h = ch // 4
        cw = [tok_rsq[h]]
        if ch >= 4 and tkt_l[ch - 4] is not None:
            cw.append(tkt_l[ch - 4])
        tkc_l[ch] = v_tt(cns[ch % 4], big(BIGD, ch), rsq[h][:], OP.mult,
                         waits=cw)

    def emit_tkt(ch):
        q = ch % 4
        tw = [tkc_l[ch], t_rms]
        if ch >= 2 and tkh_l[ch - 2] is not None:
            tw.append(tkh_l[ch - 2])
        tkt_l[ch] = se_act(ths[ch % 2], cns[ch % 4], AF.Tanh,
                           scale=rms_t[:, q:q + 1], waits=tw)

    def emit_tkh(ch):
        tkh_l[ch] = v_tt(big(BIGB, ch), big(BIGB, ch), ths[ch % 2], OP.mult,
                         waits=[tkt_l[ch], tok_o[ch]])

    for ch in range(NCH):
        emit_tkc(ch)
        emit_tkt(ch)
        if ch >= 2:
            emit_tkh(ch - 2)
    emit_tkh(NCH - 2)
    emit_tkh(NCH - 1)
    tok_hout = tkh_l[NCH - 1]
    tap("hout", BIGB.ap(), waits=[tok_hout])

    tok_x2bf = []
    xsl_state = {}

    def _wout_out(dc, ps, tokm):
        xs = scr32[dc % 3]
        xw = [tok_fix[-1], tok_rsq[-1]] if dc < 3 else [tok_x2bf[dc - 3]]
        t_x = y_q.dma(xs[:], x_slab[dc], s_xsl[dc % 3], waits=xw)
        x2w = [tokm, tk_V, t_x]
        if dc == 0:
            x2w += dtok("cfix") + dtok("ctil") + dtok("Pcum") + dtok("hout")
        tk_x2 = v_stt(acc_chunk(dc), ps[:], Vcol(32 + dc), xs[:],
                      OP.mult, OP.add, waits=x2w)
        tok_x2bf.append(v_copy(big(BIGC, dc), acc_chunk(dc),
                               waits=[tk_x2] + dtok("xp")))
        return tk_x2

    mm_super([0, 1], w_out_slab, 2048, BIGB, tkh_l, _wout_out)
    mm_super([2, 3], w_out_slab, 2048, BIGB, tkh_l, _wout_out)
    for dc in range(4, NCH):
        si, slot, rdy = ws.load(w_out_slab[dc], 2048)
        pi, ps, ptok = pp.acquire()
        tokm = mm_group(ps, lambda k, s=slot: s[:, k * 128:(k + 1) * 128],
                        lambda k, hh: big(BIGB, k)[:, hh * 512:(hh + 1) * 512],
                        16, waits=[rdy, tkh_l[NCH - 1], ptok])
        ws.free(si, tokm)
        tke = _wout_out(dc, ps, tokm)
        pp.release(pi, tke)
    tok_acc = list(tok_x2bf)
    tok_x2bf_all = tok_x2bf[-1]
    tap("x2bf", BIGC.ap(), waits=[tok_x2bf_all])

    pi_m, ps_m, pi_s, ps_s, tok_mean2, tok_sq2 = ln_stats(
        BIGC, [], per_chunk=tok_x2bf)
    tok_inv2, tok_m22 = ln_invm2(ps_m, ps_s, tok_mean2, tok_sq2,
                                 guard=[tok_x2bf_all] + dtok("inv1"))
    pp.release(pi_m, tok_m22)
    pp.release(pi_s, tok_m22)

    tok_h2_list = []
    for k in range(NCH):
        sl = scrb[2]
        v_tt(sl[:], big(BIGC, k), inv_t[:], OP.mult,
             waits=[tok_inv2] if k == 0 else (), inc=False)
        v_tt(sl[:], sl[:], m2_t[:], OP.subtract,
             waits=[tok_m22] if k == 0 else (), inc=False)
        tok_h2_list.append(v_ts(big(BIGB, k), sl[:], Vcol(64 + k), Vcol(48 + k),
                                OP.mult, OP.add,
                                waits=dtok("hout") if k == 0 else ()))
    tok_h2 = tok_h2_list[-1]
    tap("h2", BIGB.ap(), waits=[tok_h2])

    pi_l, ps_l, ptok_l = pp.acquire()
    tok_lg = None
    for k in range(NCH):
        for hh in range(2):
            wts = [tok_h2_list[k], t_gw, ptok_l] if hh == 0 else ()
            tok_lg = mm(ps_l[0:8, hh * 512:(hh + 1) * 512],
                        gw_sb[:, k * 8:(k + 1) * 8],
                        big(BIGB, k)[:, hh * 512:(hh + 1) * 512],
                        start=(k == 0), stop=(k == NCH - 1), waits=wts,
                        inc=(k == NCH - 1 and hh == 1))
    tk_lgc = v_copy(LG[:], ps_l[0:8, 0:TOK], waits=[tok_lg])
    pp.release(pi_l, tk_lgc)
    tap("logits", LG.ap(), waits=[tk_lgc])

    pi_tr, ps_tr, ptok_tr = pp.acquire()
    pi_tr2, ps_tr2, ptok_tr2 = pp.acquire()
    psb_tr = ps_tr.ap().bitcast(BF16)
    psb_tr2 = ps_tr2.ap().bitcast(BF16)
    _g_state = {"tok_wem": None, "tk_tr2": None}

    def _gate_tt(tt):
        tk_tr = P.run("t", lambda e, o=psb_tr[:, tt * 8:(tt + 1) * 8],
                      i=LG[0:8, tt * 128:(tt + 1) * 128], idn=ident[0:8, 0:8]:
                      e.transpose(o, i, idn),
                      waits=([tk_lgc, t_ident, ptok_tr] if tt == 0 else ()),
                      inc=True)
        l_tm = LTM[:, tt * 8:(tt + 1) * 8]
        v_copy(l_tm, psb_tr[:, tt * 8:(tt + 1) * 8], waits=[tk_tr], inc=False)
        v_reduce(m1s[:], l_tm, OP.max, inc=False)
        v_ts(eqs[:], l_tm, m1s[:], None, OP.is_equal, None, inc=False)
        v_stt(els[:], eqs[:], -1e30, l_tm, OP.mult, OP.add, inc=False)
        v_reduce(m2s[:], els[:], OP.max, inc=False)
        tk_pre = v_ts(e2s[:], m2s[:], m1s[:], None, OP.subtract, None)
        tk_e = se_act(e2s[:], e2s[:], AF.Exp, waits=[tk_pre])
        v_ts(e2s[:], e2s[:], 1.0, None, OP.add, None, waits=[tk_e], inc=False)
        P.run("v", lambda e, o=rds[:], i=e2s[:]: e.reciprocal(o, i), inc=False,
              r=[e2s[:]], w=[rds[:]])
        tk_nm = v_ts(nm1[:], m1s[:], -1.0, None, OP.mult, None)
        tk_el = se_act(els[:], l_tm, AF.Exp, bias=nm1[:], waits=[tk_nm])
        v_ts(ges[:], l_tm, m2s[:], None, OP.is_ge, None, inc=False)
        v_ts(w1s[:], els[:], rds[:], None, OP.mult, None, waits=[tk_el],
             inc=False)
        tk_w = v_tt(WTM[:, tt * 8:(tt + 1) * 8], w1s[:], ges[:], OP.mult)
        tk_wb = v_copy(wtb[:, tt % 2:tt % 2 + 1].bitcast(BF16) if False else
                       wtb[:, (tt % 2) * 8:(tt % 2) * 8 + 8],
                       WTM[:, tt * 8:(tt + 1) * 8],
                       waits=[tk_w] + ([_g_state["tk_tr2"]] if _g_state["tk_tr2"] else []))
        _g_state["tk_wb"] = tk_wb

    def _gate_tt2(tt):
        tk_tr2 = P.run("t", lambda e,
                       o=psb_tr2[0:8, tt * 128:(tt + 1) * 128],
                       i=wtb[:, (tt % 2) * 8:(tt % 2) * 8 + 8], idn=ident[:]:
                       e.transpose(o, i, idn),
                       waits=[_g_state["tk_wb"]] + ([ptok_tr2] if tt == 0 else []),
                       inc=True)
        _g_state["tk_tr2"] = tk_tr2
        _g_state["tok_wem"] = v_copy(WEM[0:8, tt * 128:(tt + 1) * 128],
                                     psb_tr2[0:8, tt * 128:(tt + 1) * 128],
                                     waits=[tk_tr2])
        if tt == 7:
            pp.release(pi_tr, _g_state["tok_wem"])
            pp.release(pi_tr2, _g_state["tok_wem"])
            tap("wem", WEM.ap(), waits=[_g_state["tok_wem"]])

    def _gate_step(j):
        if 1 <= j <= 8:
            _gate_tt2(j - 1)
        if j <= 7:
            _gate_tt(j)

    gate_fillers = [lambda j=j: _gate_step(j) for j in range(9)]

    tok_pe_used = [None] * IC
    experts = [("s", 0)] + [("r", e) for e in range(NE)] + [("s", 1)]
    first = True
    tok_wbc = None
    prev_gp_read = None
    prev_sil_read = [None, None]
    for kind, e in experts:
        routed = kind == "r"
        if routed:
            t_wr = y_q.dma(WROW1[:], WEM[e:e + 1, :], s_wr,
                           waits=[_g_state["tok_wem"]] + ([tok_wbc] if tok_wbc else []))
            pi_b, ps_b, ptok_b = pp.acquire()
            tokb = None
            for hh in range(2):
                wts = [t_wr, ptok_b] if hh == 0 else ()
                tokb = mm(ps_b[:, hh * 512:(hh + 1) * 512], onesrow[:],
                          WROW1[0:1, hh * 512:(hh + 1) * 512],
                          start=True, stop=True, waits=wts, inc=(hh == 1))
            wbw = [tokb]
            if prev_gp_read is not None:
                wbw.append(prev_gp_read)
            tok_wbc = se_act(WBC[:], ps_b[:], AF.Copy, bias=0.0, scale=1.0,
                             waits=wbw)
            pp.release(pi_b, tok_wbc)
        tok_pe_last = None
        tok_pe_list = []
        for ic in range(IC):
            icg = ic if (routed or e == 0) else ic + IC
            w1src = w1_slab[e][ic] if routed else sw1_slab[icg]
            w3src = w3_slab[e][ic] if routed else sw3_slab[icg]
            si1, slot1, rdy1 = ws.load(w1src, 2048)
            pi_h, ps_h, ptok_h = pp.acquire()
            tok_hm = mm_group(ps_h, lambda k, s=slot1: s[:, k * 128:(k + 1) * 128],
                              lambda k, hh: big(BIGB, k)[:, hh * 512:(hh + 1) * 512],
                              16, waits=[rdy1, tok_h2_list[0], ptok_h],
                              per_k=tok_h2_list)
            ws.free(si1, tok_hm)
            si3, slot3, rdy3 = ws.load(w3src, 2048)
            pi_g, ps_g, ptok_g = pp.acquire()
            tok_gm = mm_group(ps_g, lambda k, s=slot3: s[:, k * 128:(k + 1) * 128],
                              lambda k, hh: big(BIGB, k)[:, hh * 512:(hh + 1) * 512],
                              16, waits=[rdy3, ptok_g])
            ws.free(si3, tok_gm)
            sil = scrb[ic % 2]
            silw = [tok_hm]
            if prev_sil_read[ic % 2] is not None:
                silw.append(prev_sil_read[ic % 2])
            tk_sil = se_act(sil[:], ps_h[:], AF.Silu, waits=silw)
            pp.release(pi_h, tk_sil)
            pe_guard = [tok_pe_used[ic]] if tok_pe_used[ic] else []
            if routed:
                gp = scrb[2 + ic % 2]
                tk_gp = v_tt(gp[:], ps_g[:], WBC[:], OP.mult,
                             waits=[tok_gm, tok_wbc])
                prev_gp_read = tk_gp
                tk_pe = v_tt(big(BIGC, ic), sil[:], gp[:], OP.mult,
                             waits=[tk_sil, tk_gp] + pe_guard + dtok("x2bf"))
            else:
                tk_pe = v_tt(big(BIGC, ic), sil[:], ps_g[:], OP.mult,
                             waits=[tk_sil, tok_gm] + pe_guard)
            prev_sil_read[ic % 2] = tk_pe
            pp.release(pi_g, tk_pe)
            tok_pe_used[ic] = None
            tok_pe_last = tk_pe
            tok_pe_list.append(tk_pe)
            if kind == "s" and e == 0 and ic < 9:
                gate_fillers[ic]()
        for dc in range(NCH):
            w2src = w2_slab[e][dc] if routed else sw2_slab[dc]
            ncols = 1408 if routed else 2816
            coff = 0 if (routed or e == 0) else 1408
            si2, slot2, rdy2 = ws.load(w2src, ncols)
            pi_y, ps_y, ptok_y = pp.acquire()
            toky = None
            for ic in range(IC):
                for hh in range(2):
                    wts = []
                    if hh == 0 and dc == 0:
                        wts.append(tok_pe_list[ic])
                    if ic == 0 and hh == 0:
                        wts += [rdy2, ptok_y]
                    toky = mm(ps_y[:, hh * 512:(hh + 1) * 512],
                              slot2[:, coff + ic * 128:coff + (ic + 1) * 128],
                              big(BIGC, ic)[:, hh * 512:(hh + 1) * 512],
                              start=(ic == 0), stop=(ic == IC - 1),
                              waits=wts, inc=(ic == IC - 1 and hh == 1))
            ws.free(si2, toky)
            if dc == NCH - 1:
                for ic2 in range(IC):
                    tok_pe_used[ic2] = toky
            tk_a = v_stt(acc_chunk(dc), ps_y[:], Vcol(80 + dc),
                         acc_chunk(dc), OP.mult, OP.add,
                         waits=[toky, tok_acc[dc]])
            tok_acc[dc] = tk_a
            pp.release(pi_y, tk_a)
        first = False

    for dc in range(NCH):
        y_q.dma(y_out[dc], acc_chunk(dc), s_out[0], waits=[tok_acc[dc]])
    P.emit("y", lambda e, v=s_out[0].n, h=s_out[0].h: e.wait_ge(h, v))
    if dbg:
        P.emit("y", lambda e, v=s_dbg.n: e.wait_ge(s_dbg.h, v))

    with ctx:
        with nc.Block() as block:
            @block.tensor
            def _(e):
                for fn in plan.ops["t"]:
                    fn(e)

            @block.vector
            def _(e):
                for fn in plan.ops["v"]:
                    fn(e)

            @block.scalar
            def _(e):
                for fn in plan.ops["s"]:
                    fn(e)

            @block.gpsimd
            def _(e):
                for fn in plan.ops["g"]:
                    fn(e)

            @block.sync
            def _(e):
                for fn in plan.ops["y"]:
                    fn(e)

    return nc


def _slab(WT):
    din, dout = WT.shape
    nk, C = din // 128, dout // 128
    return np.ascontiguousarray(
        WT.reshape(nk, 128, C, 128).transpose(2, 1, 0, 3).reshape(C, 128, nk * 128))


def prepare_inputs(inputs):
    f32 = lambda k: np.asarray(inputs[k], dtype=np.float32)
    x = f32("L0_attn_x")
    W_in = f32("L0_attn_W_in")
    W_gate = f32("L0_attn_W_gate")
    b_gate = f32("L0_attn_b_gate")
    rms_w = f32("L0_attn_rms_w")
    W_out = f32("L0_attn_W_out")
    gate_w = f32("L0_ffn_gate_w")
    w1 = f32("L0_ffn_w1")
    w3 = f32("L0_ffn_w3")
    w2 = f32("L0_ffn_w2")
    sw1 = f32("L0_ffn_sw1")
    sw3 = f32("L0_ffn_sw3")
    sw2 = f32("L0_ffn_sw2")
    cc = f32("L0_c")
    wmod = f32("L0_wmod")

    shared = {}
    shared["w_in_slab"] = _slab(np.ascontiguousarray(W_in.T))
    wg = _slab(np.ascontiguousarray(W_gate.T))
    shared["w_gate_slab"] = np.ascontiguousarray(wg[GATE_ORDER])
    b2 = np.ascontiguousarray(b_gate.reshape(64, 128).T)
    shared["b_gate_t"] = np.ascontiguousarray(b2[:, GATE_ORDER])
    shared["w_out_slab"] = _slab(np.ascontiguousarray(W_out.T))
    shared["w1_slab"] = np.stack([_slab(w1[e]) for e in range(NE)])
    shared["w3_slab"] = np.stack([_slab(w3[e]) for e in range(NE)])
    shared["w2_slab"] = np.stack([_slab(w2[e]) for e in range(NE)])
    shared["sw1_slab"] = _slab(sw1)
    shared["sw3_slab"] = _slab(sw3)
    shared["sw2_slab"] = _slab(sw2)
    shared["gate_w_t"] = np.ascontiguousarray(
        gate_w.T.reshape(16, 128, 8).transpose(1, 0, 2).reshape(128, 128))
    shared["rms_t"] = np.ascontiguousarray(rms_w.reshape(4, 128).T)
    shared["cT"] = np.ascontiguousarray(
        cc.T.reshape(16, 128, 2).transpose(1, 0, 2).reshape(128, 32))
    shared["ident"] = np.eye(128, dtype=np.float32)

    in_maps = []
    for core in range(NCORES):
        b, q = core // 4, core % 4
        m = dict(shared)
        xs = np.ascontiguousarray(x[b, q * TOK:(q + 1) * TOK, :].T)
        m["x_slab"] = np.ascontiguousarray(xs.reshape(NCH, 128, TOK))
        m["wmod_slab"] = _slab(np.ascontiguousarray(
            wmod[:, core * 1536:(core + 1) * 1536]))
        cv = np.zeros((128, 8), dtype=np.float32)
        cv[:, 0] = 1.0 - b
        cv[:, 1] = float(b)
        for idx, r in enumerate((0, 1, 2, 4, 5, 6)):
            if q != 0 and r == core - 1:
                cv[:, 2 + idx] = 1.0
        m["cvec"] = cv
        in_maps.append(m)
    return in_maps


def kernel(**inputs):
    in_maps = prepare_inputs(inputs)
    nc = build()
    res = run_bass_kernel_spmd(nc, in_maps, core_ids=list(range(NCORES)))
    out = np.empty((2, 4096, 2048), dtype=np.float32)
    for core in range(NCORES):
        b, q = core // 4, core % 4
        y = np.asarray(res.results[core]["y_slab"])
        out[b, q * TOK:(q + 1) * TOK, :] = y.reshape(2048, TOK).T
    return out


# version 08
# speedup vs baseline: 1.0248x; 1.0109x over previous
import numpy as np
from contextlib import ExitStack

import concourse.bass as bass
import concourse.mybir as mybir
from concourse.bass_utils import run_bass_kernel_spmd
from concourse.alu_op_type import AluOpType as OP

F32 = mybir.dt.float32
BF16 = mybir.dt.bfloat16
AF = mybir.ActivationFunctionType
AX = mybir.AxisListType

NCORES = 8
TOK = 1024
NCH = 16
LN_EPS = 1e-5
RMS_EPS = 1e-6
NE = 8
IC = 11
SIC = 22

GATE_ORDER = [h * 16 + ty * 4 + q for h in range(4) for q in range(4)
              for ty in (1, 0, 3)] + \
             [h * 16 + 2 * 4 + q for h in range(4) for q in range(4)]


class Sem:
    def __init__(self, ctx, nc, name):
        self.h = ctx.enter_context(nc.semaphore(name))
        self.n = 0

    def inc(self, k=1):
        self.n += k
        return self.n


class Plan:
    ENGS = ("t", "v", "s", "g", "y")

    def __init__(self, nc, ctx):
        self.nc = nc
        self.ops = {e: [] for e in self.ENGS}
        self.last_wait = {}
        self.sem = {e: Sem(ctx, nc, f"prog_{e}") for e in ("t", "v", "s")}
        self.cc = Sem(ctx, nc, "cc")
        self.dirty_w = {}
        self.dirty_r = {}

    def emit(self, eng, fn):
        self.ops[eng].append(fn)

    def wait(self, eng, tok):
        if tok is None:
            return
        sem, val = tok
        if sem is self.sem.get(eng):
            return
        key = (eng, id(sem))
        if self.last_wait.get(key, -1) >= val:
            return
        self.last_wait[key] = val
        self.emit(eng, lambda e, s=sem.h, v=val: e.wait_ge(s, v))

    def waits(self, eng, toks):
        for tk in toks:
            self.wait(eng, tk)

    @staticmethod
    def _names(aps):
        out = []
        for a in aps:
            if a is None or isinstance(a, (int, float)):
                continue
            n = getattr(a, "name", None)
            if n is not None:
                out.append(n)
        return out

    def run(self, eng, fn, waits=(), inc=False, r=(), w=()):
        self.waits(eng, waits)
        if eng in ("v", "s"):
            rn, wn = set(self._names(r)), set(self._names(w))
            dw = self.dirty_w.setdefault(eng, set())
            dr = self.dirty_r.setdefault(eng, set())
            if (rn & dw) or (wn & dw) or (wn & dr):
                self.emit(eng, lambda e: e.drain())
                dw.clear()
                dr.clear()
            dw |= wn
            dr |= rn
        if inc:
            sem = self.sem[eng]
            v = sem.inc(1)

            def wrapped(e, fn=fn, s=sem.h):
                fn(e).then_inc(s, 1)
            self.emit(eng, wrapped)
            return (sem, v)
        self.emit(eng, fn)
        return None


class DmaQueue:
    def __init__(self, plan, eng):
        self.plan = plan
        self.eng = eng

    def dma(self, dst, src, sem, waits=()):
        self.plan.waits(self.eng, waits)
        v = sem.inc(16)
        self.plan.emit(self.eng,
                       lambda e, d=dst, s=src, sm=sem.h: e.dma_start(d, s).then_inc(sm, 16))
        return (sem, v)


class WStream:

    def __init__(self, plan, ctx, nc, nslots=3, cols=2816):
        self.plan = plan
        self.slots = [ctx.enter_context(nc.sbuf_tensor(f"wslot{i}", [128, cols], BF16))
                      for i in range(nslots)]
        self.sems = [Sem(ctx, nc, f"wsem{i}") for i in range(nslots)]
        self.free_tok = [None] * nslots
        self.idx = 0

    def load(self, src_ap, ncols):
        i = self.idx % len(self.slots)
        self.idx += 1
        slot = self.slots[i]
        p = self.plan
        if self.free_tok[i] is not None:
            p.wait("g", self.free_tok[i])
        v = self.sems[i].inc(16)
        p.emit("g", lambda e, d=slot[:, 0:ncols], s=src_ap, sm=self.sems[i].h:
               e.dma_start(d, s).then_inc(sm, 16))
        return i, slot, (self.sems[i], v)

    def free(self, i, tok):
        self.free_tok[i] = tok


class PsumPool:
    def __init__(self, ctx, nc, n=4):
        self.ps = [ctx.enter_context(nc.psum_tensor(f"ps{i}", [128, TOK], F32))
                   for i in range(n)]
        self.free_tok = [None] * n
        self.held = [False] * n
        self.idx = 0

    def acquire(self):
        for _ in range(len(self.ps)):
            i = self.idx % len(self.ps)
            self.idx += 1
            if not self.held[i]:
                self.held[i] = True
                return i, self.ps[i], self.free_tok[i]
        raise RuntimeError("all psum pools held")

    def release(self, i, tok):
        assert self.held[i]
        self.held[i] = False
        self.free_tok[i] = tok


def build(debug_taps=None):
    debug_taps = debug_taps or {}
    nc = bass.Bass("TRN2", target_bir_lowering=False)
    ctx = ExitStack()

    def dram_in(name, shape):
        return nc.dram_tensor(name, list(shape), F32, kind="ExternalInput")

    x_slab = dram_in("x_slab", (NCH, 128, TOK))
    wmod_slab = dram_in("wmod_slab", (12, 128, 2048))
    cT = dram_in("cT", (128, 32))
    w_in_slab = dram_in("w_in_slab", (16, 128, 2048))
    w_gate_slab = dram_in("w_gate_slab", (64, 128, 2048))
    b_gate_t = dram_in("b_gate_t", (128, 64))
    w_out_slab = dram_in("w_out_slab", (16, 128, 2048))
    w1_slab = dram_in("w1_slab", (NE, IC, 128, 2048))
    w3_slab = dram_in("w3_slab", (NE, IC, 128, 2048))
    w2_slab = dram_in("w2_slab", (NE, 16, 128, 1408))
    sw1_slab = dram_in("sw1_slab", (SIC, 128, 2048))
    sw3_slab = dram_in("sw3_slab", (SIC, 128, 2048))
    sw2_slab = dram_in("sw2_slab", (16, 128, 2816))
    gate_w_t = dram_in("gate_w_t", (128, 128))
    rms_t_in = dram_in("rms_t", (128, 4))
    cvec_in = dram_in("cvec", (128, 8))
    ident_in = dram_in("ident", (128, 128))

    y_out = nc.dram_tensor("y_slab", [NCH, 128, TOK], F32, kind="ExternalOutput")

    ag1_in = nc.dram_tensor("ag1_in", [1, 24 * 128], F32)
    ag1_out = nc.dram_tensor("ag1_out", [1, 24 * 128 * NCORES], F32)
    ag2_in = nc.dram_tensor("ag2_in", [1, 32 * 128], F32)
    ag2_out = nc.dram_tensor("ag2_out", [1, 32 * 128 * NCORES], F32)
    agw_in = nc.dram_tensor("agw_in", [1, 16], F32)
    agw_out = nc.dram_tensor("agw_out", [1, 16 * NCORES], F32)
    x2s = nc.dram_tensor("x2s", [NCH, 128, TOK], F32)

    dbg = {}
    for name, (shape, dt) in debug_taps.items():
        dbg[name] = nc.dram_tensor("dbg_" + name, list(shape), dt,
                                   kind="ExternalOutput")

    sb = lambda n, s, d: ctx.enter_context(nc.sbuf_tensor(n, s, d))
    BIGB = sb("BIGB", [128, NCH * TOK], BF16)
    BIGC = sb("BIGC", [128, NCH * TOK], BF16)
    BIGD = sb("BIGD", [128, NCH * TOK], BF16)
    BIGE = sb("BIGE", [128, NCH * TOK], BF16)

    fbuf = [sb(f"fbuf{i}", [128, TOK], F32) for i in range(2)]
    ubuf = [sb(f"ubuf{i}", [128, TOK], BF16) for i in range(2)]
    scr32 = [sb(f"scr32_{i}", [128, TOK], F32) for i in range(4)]
    scrb = [sb(f"scrb_{i}", [128, TOK], BF16) for i in range(6)]
    inv_t = sb("inv_t", [128, TOK], BF16)
    m2_t = sb("m2_t", [128, TOK], BF16)

    modl = sb("modl", [128, 24], F32)
    MG = sb("MG", [128, 192], F32)
    V = sb("V", [128, 96], F32)
    ABx = sb("ABx", [128, 32], F32)
    ABG = sb("ABG", [128, 256], F32)
    SCH = sb("SCH", [128, 128], F32)
    zci = sb("zci", [128, 16], F32)
    bt = sb("bt", [128, 64], F32)
    bth = sb("bth", [128, 64], F32)
    rms_t = sb("rms_sb", [128, 4], F32)
    cvec = sb("cvec_sb", [128, 8], F32)
    cT_sb = sb("cT_sb", [128, 32], F32)
    siluc = sb("siluc", [128, 32], BF16)
    gw_sb = sb("gw_sb", [128, 128], BF16)
    ident = sb("ident_sb", [128, 128], BF16)
    ones_m = sb("ones_m", [128, 128], BF16)
    ones_r = sb("ones_r", [128, 128], BF16)
    onesrow = sb("onesrow", [1, 128], BF16)
    zero1 = sb("zero1", [128, 1], F32)
    one1 = sb("one1", [128, 1], F32)
    eps_ln = sb("eps_ln", [128, 1], F32)
    eps_rms = sb("eps_rms", [128, 1], F32)
    LG = sb("LG", [8, TOK], BF16)
    LTM = sb("LTM", [128, 64], F32)
    WTM = sb("WTM", [128, 64], F32)
    wtb = sb("wtb", [128, 16], BF16)
    WEM = sb("WEM", [8, TOK], BF16)
    WROW1 = sb("WROW1", [1, TOK], BF16)
    WBC = sb("WBC", [128, TOK], BF16)
    m1s = sb("m1s", [128, 1], F32)
    m2s = sb("m2s", [128, 1], F32)
    nm1 = sb("nm1", [128, 1], F32)
    e2s = sb("e2s", [128, 1], F32)
    rds = sb("rds", [128, 1], F32)
    eqs = sb("eqs", [128, 8], F32)
    els = sb("els", [128, 8], F32)
    ges = sb("ges", [128, 8], F32)
    w1s = sb("w1s", [128, 8], F32)

    accD = BIGD.ap().bitcast(F32)
    accE = BIGE.ap().bitcast(F32)

    def acc_chunk(dc):
        base = accD if dc < 8 else accE
        return base[:, (dc % 8) * TOK:(dc % 8 + 1) * TOK]

    def big(t, ch):
        return t[:, ch * TOK:(ch + 1) * TOK]

    plan = Plan(nc, ctx)
    P = plan
    CC = plan.cc

    g_q = DmaQueue(plan, "g")
    y_q = DmaQueue(plan, "y")
    s_cst = Sem(ctx, nc, "s_cst")
    s_gcst = Sem(ctx, nc, "s_gcst")
    s_xbf = [Sem(ctx, nc, f"s_xbf{i}") for i in range(4)]
    s_agw = Sem(ctx, nc, "s_agw")
    s_ag1i = Sem(ctx, nc, "s_ag1i")
    s_ag2i = Sem(ctx, nc, "s_ag2i")
    s_mod = Sem(ctx, nc, "s_mod")
    s_abg = Sem(ctx, nc, "s_abg")
    s_xsl = [Sem(ctx, nc, f"s_xsl{i}") for i in range(3)]
    s_x2w = [Sem(ctx, nc, f"s_x2w{i}") for i in range(2)]
    s_wr = Sem(ctx, nc, "s_wr")
    s_out = [Sem(ctx, nc, f"s_out{i}") for i in range(3)]
    s_dbg = Sem(ctx, nc, "s_dbg")

    ws = WStream(plan, ctx, nc, nslots=4, cols=2816)
    pp = PsumPool(ctx, nc, 4)

    def mm(psum_ap, w_ap, x_ap, start, stop, waits=(), inc=False):
        return P.run("t", lambda e, o=psum_ap, w=w_ap, x=x_ap, st=start, sp=stop:
                     e.matmul(o, w, x, start=st, stop=sp), waits=waits, inc=inc)

    def mm_group(ps, wk_slice, rhs_fn, nk, waits=(), per_k=None):
        tok = None
        for k in range(nk):
            w_ap = wk_slice(k)
            for hh in range(2):
                last = (k == nk - 1 and hh == 1)
                wts = list(waits) if (k == 0 and hh == 0) else []
                if per_k is not None and hh == 0:
                    wts.append(per_k[k])
                tok = mm(ps[:, hh * 512:(hh + 1) * 512], w_ap,
                         rhs_fn(k, hh), start=(k == 0), stop=(k == nk - 1),
                         waits=wts, inc=last)
        return tok

    def mm_super(cs, w_srcs, ncols, rhs_big, per_k, out_fn, extra_first=()):
        slots = []
        for c in cs:
            slots.append(ws.load(w_srcs[c], ncols))
        pools = [pp.acquire() for _ in cs]
        tokm = None
        for k in range(16):
            for j, c in enumerate(cs):
                si, slot, rdy = slots[j]
                pi, ps, ptok = pools[j]
                for hh in range(2):
                    wts = []
                    if hh == 0 and j == 0:
                        wts.append(per_k[k])
                    if k == 0 and hh == 0:
                        wts += [rdy, ptok] + (list(extra_first) if j == 0 else [])
                    tokm = mm(ps[:, hh * 512:(hh + 1) * 512],
                              slot[:, k * 128:(k + 1) * 128],
                              big(rhs_big, k)[:, hh * 512:(hh + 1) * 512],
                              start=(k == 0), stop=(k == 15), waits=wts,
                              inc=(k == 15 and hh == 1))
        for j, c in enumerate(cs):
            si, slot, rdy = slots[j]
            pi, ps, ptok = pools[j]
            ws.free(si, tokm)
            tke = out_fn(c, ps, tokm)
            pp.release(pi, tke)

    def se_act(out_ap, in_ap, func, bias=0.0, scale=1.0, waits=(), inc=True):
        return P.run("s", lambda e, o=out_ap, i=in_ap, f=func, b=bias, sc=scale:
                     e.activation(o, i, f, bias=b, scale=sc), waits=waits, inc=inc,
                     r=[in_ap, bias, scale], w=[out_ap])

    def v_tt(out_ap, a_ap, b_ap, op, waits=(), inc=True):
        return P.run("v", lambda e, o=out_ap, a=a_ap, b=b_ap, p=op:
                     e.tensor_tensor(o, a, b, p), waits=waits, inc=inc,
                     r=[a_ap, b_ap], w=[out_ap])

    def v_ts(out_ap, in_ap, sc1, sc2, op0, op1, waits=(), inc=True):
        if op1 is None:
            return P.run("v", lambda e, o=out_ap, i=in_ap, s1=sc1, p0=op0:
                         e.tensor_scalar(o, i, s1, None, p0), waits=waits, inc=inc,
                         r=[in_ap, sc1], w=[out_ap])
        return P.run("v", lambda e, o=out_ap, i=in_ap, s1=sc1, s2=sc2, p0=op0, p1=op1:
                     e.tensor_scalar(o, i, s1, s2, p0, p1), waits=waits, inc=inc,
                     r=[in_ap, sc1, sc2], w=[out_ap])

    def v_stt(out_ap, in0_ap, sc, in1_ap, op0, op1, waits=(), inc=True):
        return P.run("v", lambda e, o=out_ap, i0=in0_ap, s=sc, i1=in1_ap, p0=op0, p1=op1:
                     e.scalar_tensor_tensor(o, i0, s, i1, p0, p1), waits=waits, inc=inc,
                     r=[in0_ap, sc, in1_ap], w=[out_ap])

    def v_copy(out_ap, in_ap, waits=(), inc=True):
        return P.run("v", lambda e, o=out_ap, i=in_ap: e.tensor_copy(o, i),
                     waits=waits, inc=inc, r=[in_ap], w=[out_ap])

    def v_scan(out_ap, d0, d1, init, op0, op1, waits=(), inc=True):
        return P.run("v", lambda e, o=out_ap, a=d0, b=d1, s=init, p0=op0, p1=op1:
                     e.tensor_tensor_scan(o, a, b, s, p0, p1), waits=waits, inc=inc,
                     r=[d0, d1, init], w=[out_ap])

    def v_memset(ap, val, waits=(), inc=False):
        return P.run("v", lambda e, a=ap, v=val: e.memset(a, v), waits=waits,
                     inc=inc, w=[ap])

    def v_reduce(out_ap, in_ap, op, waits=(), inc=True):
        return P.run("v", lambda e, o=out_ap, i=in_ap, p=op:
                     e.tensor_reduce(o, i, AX.X, p), waits=waits, inc=inc,
                     r=[in_ap], w=[out_ap])

    dbg_tok = {}
    _tap_prev = [None]

    def tap(name, ap, waits=()):
        if name in dbg:
            w = list(waits) + ([_tap_prev[0]] if _tap_prev[0] else [])
            dbg_tok[name] = y_q.dma(dbg[name].ap(), ap, s_dbg, waits=w)
            _tap_prev[0] = dbg_tok[name]

    def dtok(name):
        return [dbg_tok[name]] if name in dbg_tok else []

    def collective(in_t, out_t, waits=()):
        P.waits("g", waits)
        P.emit("g", lambda e, a=in_t, b=out_t: e.collective_compute(
            "AllGather", OP.bypass, replica_groups=[list(range(NCORES))],
            ins=[a.ap().opt()], outs=[b.ap().opt()]).then_inc(CC.h, 1))
        return (CC, CC.inc(1))

    tk_z = v_memset(zero1[:], 0.0, inc=True)
    t_agwz = y_q.dma(agw_in.ap(), zero1[0:16, :], s_agw, waits=[tk_z])
    collective(agw_in, agw_out, waits=[t_agwz])

    y_q.dma(cT_sb[:], cT.ap(), s_cst)
    y_q.dma(bt[:], b_gate_t.ap(), s_cst)
    y_q.dma(rms_t[:], rms_t_in.ap(), s_cst)
    t_cst_all = y_q.dma(cvec[:], cvec_in.ap(), s_cst)
    t_cT = t_bt = t_rms = t_cvec = t_cst_all
    g_q.dma(ident[:], ident_in.ap(), s_gcst)
    t_gcst_all = g_q.dma(gw_sb[:], gate_w_t.ap(), s_gcst)
    t_ident = t_gw = t_gcst_all

    v_memset(ones_m[:], 1.0 / 2048.0)
    v_memset(ones_r[:], 1.0 / 512.0)
    v_memset(onesrow[:], 1.0)
    v_memset(eps_ln[:], LN_EPS)
    v_memset(eps_rms[:], RMS_EPS)
    tk_ones = P.run("v", lambda e: e.memset(one1[:], 1.0), inc=True, w=[one1[:]])

    tk_bth = v_ts(bth[:], bt[:], 0.5, None, OP.mult, None, waits=[t_bt])
    tk_siluc = se_act(siluc[:], cT_sb[:], AF.Silu, waits=[t_cT])

    pi_mod, ps_mod, _ = pp.acquire()
    tok_mod = None
    for cm in range(12):
        si, slot, rdy = ws.load(wmod_slab[cm], 2048)
        for k in range(16):
            wts = [rdy, tk_siluc] if k == 0 else ()
            tok_mod = mm(ps_mod[:, cm * 2:(cm + 1) * 2],
                         slot[:, k * 128:(k + 1) * 128],
                         siluc[:, k * 2:(k + 1) * 2], start=(k == 0), stop=(k == 15),
                         waits=wts, inc=(k == 15))
        ws.free(si, tok_mod)
    tk_modl = v_copy(modl[:], ps_mod[:, 0:24], waits=[tok_mod])
    pp.release(pi_mod, tk_modl)
    t_ag1in = y_q.dma(ag1_in.ap(), modl[:], s_ag1i, waits=[tk_modl])
    tok_ag1 = collective(ag1_in, ag1_out, waits=[t_ag1in])

    t_MG = y_q.dma(bass.AP(MG, 0, [[192, 128], [1, 192]]),
                   bass.AP(ag1_out, 0, [[24, 128], [3072, 8], [1, 24]]),
                   s_mod, waits=[tok_ag1])
    mgb = lambda b: bass.AP(MG, b, [[192, 128], [24, 8], [2, 12]])
    vap = bass.AP(V, 0, [[96, 128], [12, 8], [1, 12]])
    v_ts(vap, mgb(0), cvec[:, 0:1], None, OP.mult, None,
         waits=[t_MG, t_cvec], inc=False)
    v_stt(vap, mgb(1), cvec[:, 1:2], vap, OP.mult, OP.add, inc=False)
    v_ts(V[:, 16:32], V[:, 16:32], 1.0, None, OP.add, None, inc=False)
    v_ts(V[:, 64:80], V[:, 64:80], 1.0, None, OP.add, None, inc=False)
    tk_V = None
    for c0 in (32, 80):
        gsl = V[:, c0:c0 + 16]
        v_ts(SCH[:, 0:16], gsl, 0.25, None, OP.max, None, inc=False)
        v_ts(SCH[:, 16:32], gsl, -0.25, None, OP.min, None, inc=False)
        v_ts(SCH[:, 32:48], gsl, 0.0, None, OP.is_ge, None, inc=False)
        v_tt(SCH[:, 48:64], SCH[:, 0:16], SCH[:, 16:32], OP.subtract, inc=False)
        v_tt(SCH[:, 48:64], SCH[:, 32:48], SCH[:, 48:64], OP.mult, inc=False)
        tk_V = v_tt(gsl, SCH[:, 48:64], SCH[:, 16:32], OP.add)
    tap("V", V[:], waits=[tk_V])

    def Vcol(c):
        return V[:, c:c + 1]

    xbf = BIGD
    xbf_grp = []
    for j in range(4):
        xbf_grp.append(g_q.dma(xbf[:, j * 4 * TOK:(j + 1) * 4 * TOK],
                               bass.AP(x_slab, j * 4 * 128 * TOK,
                                       [[TOK, 128], [128 * TOK, 4], [1, TOK]]),
                               s_xbf[j]))
    xbf_chunk = [xbf_grp[k // 4] for k in range(NCH)]

    def ln_stats(src_big, dep_toks, per_chunk=None):
        pc = per_chunk or [None] * NCH
        pi_m, ps_m, ptok_m = pp.acquire()
        pi_s, ps_s, ptok_s = pp.acquire()
        tokm = None
        for k in range(NCH):
            for hh in range(2):
                wts = (list(dep_toks) + [ptok_m, pc[k]]) if hh == 0 else ()
                tokm = mm(ps_m[:, hh * 512:(hh + 1) * 512], ones_m[:],
                          big(src_big, k)[:, hh * 512:(hh + 1) * 512],
                          start=(k == 0), stop=(k == NCH - 1), waits=wts,
                          inc=(k == NCH - 1 and hh == 1))
        toks = None
        mm_toks = []
        for k in range(NCH):
            sl = scrb[k % 2]
            w = [mm_toks[k - 2]] if k >= 2 else []
            tkq = v_tt(sl[:], big(src_big, k), big(src_big, k), OP.mult,
                       waits=(list(dep_toks) if k == 0 else []) + w + [pc[k]])
            for hh in range(2):
                wts = [tkq, ptok_s] if (k == 0 and hh == 0) else [tkq]
                toks = mm(ps_s[:, hh * 512:(hh + 1) * 512], ones_m[:],
                          sl[:, hh * 512:(hh + 1) * 512],
                          start=(k == 0), stop=(k == NCH - 1),
                          waits=wts, inc=(hh == 1))
            mm_toks.append(toks)
        return pi_m, ps_m, pi_s, ps_s, tokm, toks

    def ln_invm2(ps_m, ps_s, tok_m, tok_s, guard=()):
        tk1 = se_act(scr32[0][:], ps_m[:], AF.Square,
                     waits=[tok_m] + list(guard))
        tk2 = v_tt(scr32[1][:], ps_s[:], scr32[0][:], OP.subtract,
                   waits=[tok_s, tk1])
        tk3 = se_act(scr32[2][:], scr32[1][:], AF.Ln, bias=eps_ln[:], waits=[tk2, tk_ones])
        tk4 = se_act(inv_t[:], scr32[2][:], AF.Exp, scale=-0.5, waits=[tk3])
        tk5 = v_tt(m2_t[:], ps_m[:], inv_t[:], OP.mult, waits=[tk4])
        return tk4, tk5

    pi_m, ps_m, pi_s, ps_s, tok_mean, tok_sq = ln_stats(
        xbf, [], per_chunk=xbf_chunk)
    tok_inv, tok_m2 = ln_invm2(ps_m, ps_s, tok_mean, tok_sq)
    pp.release(pi_m, tok_m2)
    pp.release(pi_s, tok_m2)
    tap("inv1", inv_t[:], waits=[tok_inv])

    tok_h1_list = []
    for k in range(NCH):
        sl = scrb[2]
        v_tt(sl[:], big(xbf, k), inv_t[:], OP.mult,
             waits=[tok_inv] if k == 0 else (), inc=False)
        tok_xn = v_tt(big(BIGB, k), sl[:], m2_t[:], OP.subtract,
                      waits=[tok_m2] if k == 0 else ())
        tok_h1_list.append(se_act(big(BIGB, k), big(BIGB, k), AF.Identity,
                                  bias=Vcol(0 + k), scale=Vcol(16 + k),
                                  waits=[tok_xn] + ([tk_V] if k == 0 else [])))
    tok_h1 = tok_h1_list[-1]
    tap("h1", BIGB.ap(), waits=[tok_h1])

    tok_xp_evac = []

    def _xp_out(c, ps, tokm):
        tke = se_act(big(BIGC, c), ps[:], AF.Copy, bias=0.0, scale=1.0,
                     waits=[tokm])
        tok_xp_evac.append(tke)
        return tke

    mm_super([0, 1], w_in_slab, 2048, BIGB, tok_h1_list, _xp_out)
    mm_super([2, 3], w_in_slab, 2048, BIGB, tok_h1_list, _xp_out)
    for c in range(4, NCH):
        si, slot, rdy = ws.load(w_in_slab[c], 2048)
        pi, ps, ptok = pp.acquire()
        tokm = mm_group(ps, lambda k, s=slot: s[:, k * 128:(k + 1) * 128],
                        lambda k, hh: big(BIGB, k)[:, hh * 512:(hh + 1) * 512],
                        16, waits=[rdy, tok_h1, ptok])
        ws.free(si, tokm)
        tke = _xp_out(c, ps, tokm)
        pp.release(pi, tke)
    tok_xp = tok_xp_evac[-1]
    tap("xp", BIGC.ap(), waits=[tok_xp])

    prev_f = prev_u = None
    tok_last_scan = None
    for ch in range(NCH):
        pos = ch * 3
        f_sl = fbuf[ch % 2]
        u_sl = ubuf[ch % 2]
        si, slot, rdy = ws.load(w_gate_slab[pos], 2048)
        pi, ps, ptok = pp.acquire()
        tokm = mm_group(ps, lambda k, s=slot: s[:, k * 128:(k + 1) * 128],
                        lambda k, hh: big(BIGC, k)[:, hh * 512:(hh + 1) * 512],
                        16, waits=[rdy, tok_xp, ptok])
        ws.free(si, tokm)
        tk_th = se_act(scr32[0][:], ps[:], AF.Tanh, bias=bth[:, pos:pos + 1],
                       scale=0.5,
                       waits=[tokm, tk_bth] + ([prev_f] if prev_f else []))
        tk_f = v_ts(f_sl[:], scr32[0][:], 0.5, 0.5, OP.mult, OP.add,
                    waits=[tk_th])
        prev_f = tk_f
        pp.release(pi, tk_f)
        si, slot, rdy = ws.load(w_gate_slab[pos + 1], 2048)
        pi, ps, ptok = pp.acquire()
        tokm = mm_group(ps, lambda k, s=slot: s[:, k * 128:(k + 1) * 128],
                        lambda k, hh: big(BIGC, k)[:, hh * 512:(hh + 1) * 512],
                        16, waits=[rdy, ptok])
        ws.free(si, tokm)
        tk_i = se_act(scr32[1][:], ps[:], AF.Exp, bias=bt[:, pos + 1:pos + 2],
                      waits=[tokm] + ([prev_u] if prev_u else []))
        pp.release(pi, tk_i)
        si, slot, rdy = ws.load(w_gate_slab[pos + 2], 2048)
        pi, ps, ptok = pp.acquire()
        tokm = mm_group(ps, lambda k, s=slot: s[:, k * 128:(k + 1) * 128],
                        lambda k, hh: big(BIGC, k)[:, hh * 512:(hh + 1) * 512],
                        16, waits=[rdy, ptok])
        ws.free(si, tokm)
        tk_cd = se_act(scr32[2][:], ps[:], AF.Tanh, bias=bt[:, pos + 2:pos + 3],
                       waits=[tokm] + ([prev_u] if prev_u else []))
        tk_u = v_tt(u_sl[:], scr32[1][:], scr32[2][:], OP.mult,
                    waits=[tk_i, tk_cd])
        prev_u = tk_u
        pp.release(pi, tk_u)
        v_scan(big(BIGD, ch), f_sl[:], u_sl[:], zero1[:], OP.mult, OP.add,
               waits=[tk_f, tk_u, tk_ones], inc=False)
        tok_last_scan = v_scan(big(BIGE, ch), f_sl[:], u_sl[:], one1[:],
                               OP.mult, OP.bypass)

    tok_o = []
    for ch in range(NCH):
        pos = 48 + ch
        si, slot, rdy = ws.load(w_gate_slab[pos], 2048)
        pi, ps, ptok = pp.acquire()
        tokm = mm_group(ps, lambda k, s=slot: s[:, k * 128:(k + 1) * 128],
                        lambda k, hh: big(BIGC, k)[:, hh * 512:(hh + 1) * 512],
                        16, waits=[rdy, ptok])
        ws.free(si, tokm)
        tk_o = se_act(big(BIGB, ch), ps[:], AF.Sigmoid, bias=bt[:, pos:pos + 1],
                      waits=[tokm] + dtok("h1"))
        pp.release(pi, tk_o)
        tok_o.append(tk_o)

    tkA = v_copy(ABx[:, 0:16],
                 bass.AP(BIGE, TOK - 1, [[NCH * TOK, 128], [TOK, NCH]]),
                 waits=[tok_last_scan])
    tkB = v_copy(ABx[:, 16:32],
                 bass.AP(BIGD, TOK - 1, [[NCH * TOK, 128], [TOK, NCH]]))
    t_ag2in = y_q.dma(ag2_in.ap(), ABx[:], s_ag2i, waits=[tkB])
    tok_ag2 = collective(ag2_in, ag2_out, waits=[t_ag2in])
    tap("ctil", BIGD.ap(), waits=[tkB])
    tap("Pcum", BIGE.ap(), waits=[tkB])

    t_abg = y_q.dma(bass.AP(ABG, 0, [[256, 128], [1, 256]]),
                    bass.AP(ag2_out, 0, [[32, 128], [4096, 8], [1, 32]]),
                    s_abg, waits=[tok_ag2])

    def abg_A(r):
        return ABG[:, r * 32:r * 32 + 16]

    def abg_B(r):
        return ABG[:, r * 32 + 16:r * 32 + 32]

    def sch(r):
        return SCH[:, r * 16:(r + 1) * 16]

    v_copy(sch(0), abg_B(0), waits=[t_abg], inc=False)
    v_copy(sch(4), abg_B(4), inc=False)
    for r in (1, 2, 3, 5, 6, 7):
        v_tt(sch(r), abg_A(r), sch(r - 1), OP.mult, inc=False)
        v_tt(sch(r), sch(r), abg_B(r), OP.add, inc=False)
    v_memset(zci[:], 0.0)
    tk_ci = None
    for idx, r in enumerate((0, 1, 2, 4, 5, 6)):
        tk_ci = v_stt(zci[:], sch(r), cvec[:, 2 + idx:3 + idx], zci[:],
                      OP.mult, OP.add, waits=[t_cvec] if idx == 0 else ())
    tap("cin", zci[:], waits=[tk_ci])
    tok_fix = []
    for ch in range(NCH):
        tok_fix.append(v_stt(big(BIGD, ch), big(BIGE, ch), zci[:, ch:ch + 1],
                             big(BIGD, ch), OP.mult, OP.add,
                             waits=([tk_ci] + dtok("ctil") + dtok("Pcum"))
                             if ch == 0 else ()))
    tap("cfix", BIGD.ap(), waits=[tok_fix[-1]])

    rsq = [scrb[2 + h] for h in range(4)]
    tok_rsq = []
    csq_mm_tok = [None, None]
    for h in range(4):
        pi, ps, ptok = pp.acquire()
        tokm = None
        for q in range(4):
            ch = h * 4 + q
            sl = scrb[q % 2]
            w = [csq_mm_tok[q % 2]] if csq_mm_tok[q % 2] else []
            tkq = v_tt(sl[:], big(BIGD, ch), big(BIGD, ch), OP.mult,
                       waits=[tok_fix[ch]] + w)
            for hh in range(2):
                wts = [tkq, ptok] if (q == 0 and hh == 0) else [tkq]
                tokm = mm(ps[:, hh * 512:(hh + 1) * 512], ones_r[:],
                          sl[:, hh * 512:(hh + 1) * 512],
                          start=(q == 0), stop=(q == 3),
                          waits=wts, inc=(hh == 1))
            csq_mm_tok[q % 2] = tokm
        tk1 = se_act(scr32[0][:], ps[:], AF.Ln, bias=eps_rms[:], waits=[tokm, tk_ones])
        tk2 = se_act(rsq[h][:], scr32[0][:], AF.Exp, scale=-0.5, waits=[tk1])
        pp.release(pi, tk2)
        tok_rsq.append(tk2)
    fb_bf = [f.ap().bitcast(BF16) for f in fbuf]
    cns = [scrb[0][:], scrb[1][:], fb_bf[0][:, 0:TOK], fb_bf[1][:, 0:TOK]]
    ths = [ubuf[0][:], ubuf[1][:]]
    tkt_l = [None] * NCH
    tkh_l = [None] * NCH
    tkc_l = [None] * NCH

    def emit_tkc(ch):
        h = ch // 4
        cw = [tok_rsq[h]]
        if ch >= 4 and tkt_l[ch - 4] is not None:
            cw.append(tkt_l[ch - 4])
        tkc_l[ch] = v_tt(cns[ch % 4], big(BIGD, ch), rsq[h][:], OP.mult,
                         waits=cw)

    def emit_tkt(ch):
        q = ch % 4
        tw = [tkc_l[ch], t_rms]
        if ch >= 2 and tkh_l[ch - 2] is not None:
            tw.append(tkh_l[ch - 2])
        tkt_l[ch] = se_act(ths[ch % 2], cns[ch % 4], AF.Tanh,
                           scale=rms_t[:, q:q + 1], waits=tw)

    def emit_tkh(ch):
        tkh_l[ch] = v_tt(big(BIGB, ch), big(BIGB, ch), ths[ch % 2], OP.mult,
                         waits=[tkt_l[ch], tok_o[ch]])

    for ch in range(NCH):
        emit_tkc(ch)
        emit_tkt(ch)
        if ch >= 2:
            emit_tkh(ch - 2)
    emit_tkh(NCH - 2)
    emit_tkh(NCH - 1)
    tok_hout = tkh_l[NCH - 1]
    tap("hout", BIGB.ap(), waits=[tok_hout])

    tok_x2bf = []
    xsl_state = {}

    def _wout_out(dc, ps, tokm):
        xs = scr32[dc % 3]
        xw = [tok_fix[-1], tok_rsq[-1]] if dc < 3 else [tok_x2bf[dc - 3]]
        t_x = y_q.dma(xs[:], x_slab[dc], s_xsl[dc % 3], waits=xw)
        x2w = [tokm, tk_V, t_x]
        if dc == 0:
            x2w += dtok("cfix") + dtok("ctil") + dtok("Pcum") + dtok("hout")
        tk_x2 = v_stt(acc_chunk(dc), ps[:], Vcol(32 + dc), xs[:],
                      OP.mult, OP.add, waits=x2w)
        tok_x2bf.append(v_copy(big(BIGC, dc), acc_chunk(dc),
                               waits=[tk_x2] + dtok("xp")))
        return tk_x2

    mm_super([0, 1], w_out_slab, 2048, BIGB, tkh_l, _wout_out)
    mm_super([2, 3], w_out_slab, 2048, BIGB, tkh_l, _wout_out)
    for dc in range(4, NCH):
        si, slot, rdy = ws.load(w_out_slab[dc], 2048)
        pi, ps, ptok = pp.acquire()
        tokm = mm_group(ps, lambda k, s=slot: s[:, k * 128:(k + 1) * 128],
                        lambda k, hh: big(BIGB, k)[:, hh * 512:(hh + 1) * 512],
                        16, waits=[rdy, tkh_l[NCH - 1], ptok])
        ws.free(si, tokm)
        tke = _wout_out(dc, ps, tokm)
        pp.release(pi, tke)
    tok_acc = list(tok_x2bf)
    tok_x2bf_all = tok_x2bf[-1]
    tap("x2bf", BIGC.ap(), waits=[tok_x2bf_all])

    pi_m, ps_m, pi_s, ps_s, tok_mean2, tok_sq2 = ln_stats(
        BIGC, [], per_chunk=tok_x2bf)
    tok_inv2, tok_m22 = ln_invm2(ps_m, ps_s, tok_mean2, tok_sq2,
                                 guard=[tok_x2bf_all] + dtok("inv1"))
    pp.release(pi_m, tok_m22)
    pp.release(pi_s, tok_m22)

    tok_h2_list = []
    for k in range(NCH):
        sl = scrb[2]
        v_tt(sl[:], big(BIGC, k), inv_t[:], OP.mult,
             waits=[tok_inv2] if k == 0 else (), inc=False)
        tok_xn2 = v_tt(big(BIGB, k), sl[:], m2_t[:], OP.subtract,
                       waits=([tok_m22] if k == 0 else []) +
                       (dtok("hout") if k == 0 else []))
        tok_h2_list.append(se_act(big(BIGB, k), big(BIGB, k), AF.Identity,
                                  bias=Vcol(48 + k), scale=Vcol(64 + k),
                                  waits=[tok_xn2]))
    tok_h2 = tok_h2_list[-1]
    tap("h2", BIGB.ap(), waits=[tok_h2])

    pi_l, ps_l, ptok_l = pp.acquire()
    tok_lg = None
    for k in range(NCH):
        for hh in range(2):
            wts = [tok_h2_list[k], t_gw, ptok_l] if hh == 0 else ()
            tok_lg = mm(ps_l[0:8, hh * 512:(hh + 1) * 512],
                        gw_sb[:, k * 8:(k + 1) * 8],
                        big(BIGB, k)[:, hh * 512:(hh + 1) * 512],
                        start=(k == 0), stop=(k == NCH - 1), waits=wts,
                        inc=(k == NCH - 1 and hh == 1))
    tk_lgc = v_copy(LG[:], ps_l[0:8, 0:TOK], waits=[tok_lg])
    pp.release(pi_l, tk_lgc)
    tap("logits", LG.ap(), waits=[tk_lgc])

    pi_tr, ps_tr, ptok_tr = pp.acquire()
    pi_tr2, ps_tr2, ptok_tr2 = pp.acquire()
    psb_tr = ps_tr.ap().bitcast(BF16)
    psb_tr2 = ps_tr2.ap().bitcast(BF16)
    _g_state = {"tok_wem": None, "tk_tr2": None}

    def _gate_tt(tt):
        tk_tr = P.run("t", lambda e, o=psb_tr[:, tt * 8:(tt + 1) * 8],
                      i=LG[0:8, tt * 128:(tt + 1) * 128], idn=ident[0:8, 0:8]:
                      e.transpose(o, i, idn),
                      waits=([tk_lgc, t_ident, ptok_tr] if tt == 0 else ()),
                      inc=True)
        l_tm = LTM[:, tt * 8:(tt + 1) * 8]
        v_copy(l_tm, psb_tr[:, tt * 8:(tt + 1) * 8], waits=[tk_tr], inc=False)
        v_reduce(m1s[:], l_tm, OP.max, inc=False)
        v_ts(eqs[:], l_tm, m1s[:], None, OP.is_equal, None, inc=False)
        v_stt(els[:], eqs[:], -1e30, l_tm, OP.mult, OP.add, inc=False)
        v_reduce(m2s[:], els[:], OP.max, inc=False)
        tk_pre = v_ts(e2s[:], m2s[:], m1s[:], None, OP.subtract, None)
        tk_e = se_act(e2s[:], e2s[:], AF.Exp, waits=[tk_pre])
        v_ts(e2s[:], e2s[:], 1.0, None, OP.add, None, waits=[tk_e], inc=False)
        P.run("v", lambda e, o=rds[:], i=e2s[:]: e.reciprocal(o, i), inc=False,
              r=[e2s[:]], w=[rds[:]])
        tk_nm = v_ts(nm1[:], m1s[:], -1.0, None, OP.mult, None)
        tk_el = se_act(els[:], l_tm, AF.Exp, bias=nm1[:], waits=[tk_nm])
        v_ts(ges[:], l_tm, m2s[:], None, OP.is_ge, None, inc=False)
        v_ts(w1s[:], els[:], rds[:], None, OP.mult, None, waits=[tk_el],
             inc=False)
        tk_w = v_tt(WTM[:, tt * 8:(tt + 1) * 8], w1s[:], ges[:], OP.mult)
        tk_wb = v_copy(wtb[:, tt % 2:tt % 2 + 1].bitcast(BF16) if False else
                       wtb[:, (tt % 2) * 8:(tt % 2) * 8 + 8],
                       WTM[:, tt * 8:(tt + 1) * 8],
                       waits=[tk_w] + ([_g_state["tk_tr2"]] if _g_state["tk_tr2"] else []))
        _g_state["tk_wb"] = tk_wb

    def _gate_tt2(tt):
        tk_tr2 = P.run("t", lambda e,
                       o=psb_tr2[0:8, tt * 128:(tt + 1) * 128],
                       i=wtb[:, (tt % 2) * 8:(tt % 2) * 8 + 8], idn=ident[:]:
                       e.transpose(o, i, idn),
                       waits=[_g_state["tk_wb"]] + ([ptok_tr2] if tt == 0 else []),
                       inc=True)
        _g_state["tk_tr2"] = tk_tr2
        _g_state["tok_wem"] = v_copy(WEM[0:8, tt * 128:(tt + 1) * 128],
                                     psb_tr2[0:8, tt * 128:(tt + 1) * 128],
                                     waits=[tk_tr2])
        if tt == 7:
            pp.release(pi_tr, _g_state["tok_wem"])
            pp.release(pi_tr2, _g_state["tok_wem"])
            tap("wem", WEM.ap(), waits=[_g_state["tok_wem"]])

    def _gate_step(j):
        if 1 <= j <= 8:
            _gate_tt2(j - 1)
        if j <= 7:
            _gate_tt(j)

    gate_fillers = [lambda j=j: _gate_step(j) for j in range(9)]

    tok_pe_used = [None] * IC
    experts = [("s", 0)] + [("r", e) for e in range(NE)] + [("s", 1)]
    first = True
    tok_wbc = None
    prev_gp_read = None
    prev_sil_read = [None, None]
    for kind, e in experts:
        routed = kind == "r"
        if routed:
            t_wr = y_q.dma(WROW1[:], WEM[e:e + 1, :], s_wr,
                           waits=[_g_state["tok_wem"]] + ([tok_wbc] if tok_wbc else []))
            pi_b, ps_b, ptok_b = pp.acquire()
            tokb = None
            for hh in range(2):
                wts = [t_wr, ptok_b] if hh == 0 else ()
                tokb = mm(ps_b[:, hh * 512:(hh + 1) * 512], onesrow[:],
                          WROW1[0:1, hh * 512:(hh + 1) * 512],
                          start=True, stop=True, waits=wts, inc=(hh == 1))
            wbw = [tokb]
            if prev_gp_read is not None:
                wbw.append(prev_gp_read)
            tok_wbc = se_act(WBC[:], ps_b[:], AF.Copy, bias=0.0, scale=1.0,
                             waits=wbw)
            pp.release(pi_b, tok_wbc)
        tok_pe_last = None
        tok_pe_list = []
        for ic in range(IC):
            icg = ic if (routed or e == 0) else ic + IC
            w1src = w1_slab[e][ic] if routed else sw1_slab[icg]
            w3src = w3_slab[e][ic] if routed else sw3_slab[icg]
            si1, slot1, rdy1 = ws.load(w1src, 2048)
            pi_h, ps_h, ptok_h = pp.acquire()
            tok_hm = mm_group(ps_h, lambda k, s=slot1: s[:, k * 128:(k + 1) * 128],
                              lambda k, hh: big(BIGB, k)[:, hh * 512:(hh + 1) * 512],
                              16, waits=[rdy1, tok_h2_list[0], ptok_h],
                              per_k=tok_h2_list)
            ws.free(si1, tok_hm)
            si3, slot3, rdy3 = ws.load(w3src, 2048)
            pi_g, ps_g, ptok_g = pp.acquire()
            tok_gm = mm_group(ps_g, lambda k, s=slot3: s[:, k * 128:(k + 1) * 128],
                              lambda k, hh: big(BIGB, k)[:, hh * 512:(hh + 1) * 512],
                              16, waits=[rdy3, ptok_g])
            ws.free(si3, tok_gm)
            sil = scrb[ic % 2]
            silw = [tok_hm]
            if prev_sil_read[ic % 2] is not None:
                silw.append(prev_sil_read[ic % 2])
            tk_sil = se_act(sil[:], ps_h[:], AF.Silu, waits=silw)
            pp.release(pi_h, tk_sil)
            pe_guard = [tok_pe_used[ic]] if tok_pe_used[ic] else []
            if routed:
                gp = scrb[2 + ic % 2]
                tk_gp = v_tt(gp[:], ps_g[:], WBC[:], OP.mult,
                             waits=[tok_gm, tok_wbc])
                prev_gp_read = tk_gp
                tk_pe = v_tt(big(BIGC, ic), sil[:], gp[:], OP.mult,
                             waits=[tk_sil, tk_gp] + pe_guard + dtok("x2bf"))
            else:
                tk_pe = v_tt(big(BIGC, ic), sil[:], ps_g[:], OP.mult,
                             waits=[tk_sil, tok_gm] + pe_guard)
            prev_sil_read[ic % 2] = tk_pe
            pp.release(pi_g, tk_pe)
            tok_pe_used[ic] = None
            tok_pe_last = tk_pe
            tok_pe_list.append(tk_pe)
            if kind == "s" and e == 0 and ic < 9:
                gate_fillers[ic]()
        for dc in range(NCH):
            w2src = w2_slab[e][dc] if routed else sw2_slab[dc]
            ncols = 1408 if routed else 2816
            coff = 0 if (routed or e == 0) else 1408
            si2, slot2, rdy2 = ws.load(w2src, ncols)
            pi_y, ps_y, ptok_y = pp.acquire()
            toky = None
            for ic in range(IC):
                for hh in range(2):
                    wts = []
                    if hh == 0 and dc == 0:
                        wts.append(tok_pe_list[ic])
                    if ic == 0 and hh == 0:
                        wts += [rdy2, ptok_y]
                    toky = mm(ps_y[:, hh * 512:(hh + 1) * 512],
                              slot2[:, coff + ic * 128:coff + (ic + 1) * 128],
                              big(BIGC, ic)[:, hh * 512:(hh + 1) * 512],
                              start=(ic == 0), stop=(ic == IC - 1),
                              waits=wts, inc=(ic == IC - 1 and hh == 1))
            ws.free(si2, toky)
            if dc == NCH - 1:
                for ic2 in range(IC):
                    tok_pe_used[ic2] = toky
            tk_a = v_stt(acc_chunk(dc), ps_y[:], Vcol(80 + dc),
                         acc_chunk(dc), OP.mult, OP.add,
                         waits=[toky, tok_acc[dc]])
            tok_acc[dc] = tk_a
            pp.release(pi_y, tk_a)
        first = False

    for dc in range(NCH):
        y_q.dma(y_out[dc], acc_chunk(dc), s_out[0], waits=[tok_acc[dc]])
    P.emit("y", lambda e, v=s_out[0].n, h=s_out[0].h: e.wait_ge(h, v))
    if dbg:
        P.emit("y", lambda e, v=s_dbg.n: e.wait_ge(s_dbg.h, v))

    with ctx:
        with nc.Block() as block:
            @block.tensor
            def _(e):
                for fn in plan.ops["t"]:
                    fn(e)

            @block.vector
            def _(e):
                for fn in plan.ops["v"]:
                    fn(e)

            @block.scalar
            def _(e):
                for fn in plan.ops["s"]:
                    fn(e)

            @block.gpsimd
            def _(e):
                for fn in plan.ops["g"]:
                    fn(e)

            @block.sync
            def _(e):
                for fn in plan.ops["y"]:
                    fn(e)

    return nc


def _slab(WT):
    din, dout = WT.shape
    nk, C = din // 128, dout // 128
    return np.ascontiguousarray(
        WT.reshape(nk, 128, C, 128).transpose(2, 1, 0, 3).reshape(C, 128, nk * 128))


def prepare_inputs(inputs):
    f32 = lambda k: np.asarray(inputs[k], dtype=np.float32)
    x = f32("L0_attn_x")
    W_in = f32("L0_attn_W_in")
    W_gate = f32("L0_attn_W_gate")
    b_gate = f32("L0_attn_b_gate")
    rms_w = f32("L0_attn_rms_w")
    W_out = f32("L0_attn_W_out")
    gate_w = f32("L0_ffn_gate_w")
    w1 = f32("L0_ffn_w1")
    w3 = f32("L0_ffn_w3")
    w2 = f32("L0_ffn_w2")
    sw1 = f32("L0_ffn_sw1")
    sw3 = f32("L0_ffn_sw3")
    sw2 = f32("L0_ffn_sw2")
    cc = f32("L0_c")
    wmod = f32("L0_wmod")

    shared = {}
    shared["w_in_slab"] = _slab(np.ascontiguousarray(W_in.T))
    wg = _slab(np.ascontiguousarray(W_gate.T))
    shared["w_gate_slab"] = np.ascontiguousarray(wg[GATE_ORDER])
    b2 = np.ascontiguousarray(b_gate.reshape(64, 128).T)
    shared["b_gate_t"] = np.ascontiguousarray(b2[:, GATE_ORDER])
    shared["w_out_slab"] = _slab(np.ascontiguousarray(W_out.T))
    shared["w1_slab"] = np.stack([_slab(w1[e]) for e in range(NE)])
    shared["w3_slab"] = np.stack([_slab(w3[e]) for e in range(NE)])
    shared["w2_slab"] = np.stack([_slab(w2[e]) for e in range(NE)])
    shared["sw1_slab"] = _slab(sw1)
    shared["sw3_slab"] = _slab(sw3)
    shared["sw2_slab"] = _slab(sw2)
    shared["gate_w_t"] = np.ascontiguousarray(
        gate_w.T.reshape(16, 128, 8).transpose(1, 0, 2).reshape(128, 128))
    shared["rms_t"] = np.ascontiguousarray(rms_w.reshape(4, 128).T)
    shared["cT"] = np.ascontiguousarray(
        cc.T.reshape(16, 128, 2).transpose(1, 0, 2).reshape(128, 32))
    shared["ident"] = np.eye(128, dtype=np.float32)

    in_maps = []
    for core in range(NCORES):
        b, q = core // 4, core % 4
        m = dict(shared)
        xs = np.ascontiguousarray(x[b, q * TOK:(q + 1) * TOK, :].T)
        m["x_slab"] = np.ascontiguousarray(xs.reshape(NCH, 128, TOK))
        m["wmod_slab"] = _slab(np.ascontiguousarray(
            wmod[:, core * 1536:(core + 1) * 1536]))
        cv = np.zeros((128, 8), dtype=np.float32)
        cv[:, 0] = 1.0 - b
        cv[:, 1] = float(b)
        for idx, r in enumerate((0, 1, 2, 4, 5, 6)):
            if q != 0 and r == core - 1:
                cv[:, 2 + idx] = 1.0
        m["cvec"] = cv
        in_maps.append(m)
    return in_maps


def kernel(**inputs):
    in_maps = prepare_inputs(inputs)
    nc = build()
    res = run_bass_kernel_spmd(nc, in_maps, core_ids=list(range(NCORES)))
    out = np.empty((2, 4096, 2048), dtype=np.float32)
    for core in range(NCORES):
        b, q = core // 4, core % 4
        y = np.asarray(res.results[core]["y_slab"])
        out[b, q * TOK:(q + 1) * TOK, :] = y.reshape(2048, TOK).T
    return out


# version 09
# speedup vs baseline: 1.0251x; 1.0003x over previous
import numpy as np
from contextlib import ExitStack

import concourse.bass as bass
import concourse.mybir as mybir
from concourse.bass_utils import run_bass_kernel_spmd
from concourse.alu_op_type import AluOpType as OP

F32 = mybir.dt.float32
BF16 = mybir.dt.bfloat16
AF = mybir.ActivationFunctionType
AX = mybir.AxisListType

NCORES = 8
TOK = 1024
NCH = 16
LN_EPS = 1e-5
RMS_EPS = 1e-6
NE = 8
IC = 11
SIC = 22

GATE_ORDER = [h * 16 + ty * 4 + q for h in range(4) for q in range(4)
              for ty in (1, 0, 3)] + \
             [h * 16 + 2 * 4 + q for h in range(4) for q in range(4)]


class Sem:
    def __init__(self, ctx, nc, name):
        self.h = ctx.enter_context(nc.semaphore(name))
        self.n = 0

    def inc(self, k=1):
        self.n += k
        return self.n


class Plan:
    ENGS = ("t", "v", "s", "g", "y")

    def __init__(self, nc, ctx):
        self.nc = nc
        self.ops = {e: [] for e in self.ENGS}
        self.last_wait = {}
        self.sem = {e: Sem(ctx, nc, f"prog_{e}") for e in ("t", "v", "s")}
        self.cc = Sem(ctx, nc, "cc")
        self.dirty_w = {}
        self.dirty_r = {}

    def emit(self, eng, fn):
        self.ops[eng].append(fn)

    def wait(self, eng, tok):
        if tok is None:
            return
        sem, val = tok
        if sem is self.sem.get(eng):
            return
        key = (eng, id(sem))
        if self.last_wait.get(key, -1) >= val:
            return
        self.last_wait[key] = val
        self.emit(eng, lambda e, s=sem.h, v=val: e.wait_ge(s, v))

    def waits(self, eng, toks):
        for tk in toks:
            self.wait(eng, tk)

    @staticmethod
    def _names(aps):
        out = []
        for a in aps:
            if a is None or isinstance(a, (int, float)):
                continue
            n = getattr(a, "name", None)
            if n is not None:
                out.append(n)
        return out

    def run(self, eng, fn, waits=(), inc=False, r=(), w=()):
        self.waits(eng, waits)
        if eng in ("v", "s"):
            rn, wn = set(self._names(r)), set(self._names(w))
            dw = self.dirty_w.setdefault(eng, set())
            dr = self.dirty_r.setdefault(eng, set())
            if (rn & dw) or (wn & dw) or (wn & dr):
                self.emit(eng, lambda e: e.drain())
                dw.clear()
                dr.clear()
            dw |= wn
            dr |= rn
        if inc:
            sem = self.sem[eng]
            v = sem.inc(1)

            def wrapped(e, fn=fn, s=sem.h):
                fn(e).then_inc(s, 1)
            self.emit(eng, wrapped)
            return (sem, v)
        self.emit(eng, fn)
        return None


class DmaQueue:
    def __init__(self, plan, eng):
        self.plan = plan
        self.eng = eng

    def dma(self, dst, src, sem, waits=()):
        self.plan.waits(self.eng, waits)
        v = sem.inc(16)
        self.plan.emit(self.eng,
                       lambda e, d=dst, s=src, sm=sem.h: e.dma_start(d, s).then_inc(sm, 16))
        return (sem, v)


class WStream:

    def __init__(self, plan, ctx, nc, nslots=3, cols=2816):
        self.plan = plan
        self.slots = [ctx.enter_context(nc.sbuf_tensor(f"wslot{i}", [128, cols], BF16))
                      for i in range(nslots)]
        self.sems = [Sem(ctx, nc, f"wsem{i}") for i in range(nslots)]
        self.free_tok = [None] * nslots
        self.idx = 0

    def load(self, src_ap, ncols):
        i = self.idx % len(self.slots)
        self.idx += 1
        slot = self.slots[i]
        p = self.plan
        if self.free_tok[i] is not None:
            p.wait("g", self.free_tok[i])
        v = self.sems[i].inc(16)
        p.emit("g", lambda e, d=slot[:, 0:ncols], s=src_ap, sm=self.sems[i].h:
               e.dma_start(d, s).then_inc(sm, 16))
        return i, slot, (self.sems[i], v)

    def free(self, i, tok):
        self.free_tok[i] = tok


class PsumPool:
    def __init__(self, ctx, nc, n=4):
        self.ps = [ctx.enter_context(nc.psum_tensor(f"ps{i}", [128, TOK], F32))
                   for i in range(n)]
        self.free_tok = [None] * n
        self.held = [False] * n
        self.idx = 0

    def acquire(self):
        for _ in range(len(self.ps)):
            i = self.idx % len(self.ps)
            self.idx += 1
            if not self.held[i]:
                self.held[i] = True
                return i, self.ps[i], self.free_tok[i]
        raise RuntimeError("all psum pools held")

    def release(self, i, tok):
        assert self.held[i]
        self.held[i] = False
        self.free_tok[i] = tok


def build(debug_taps=None):
    debug_taps = debug_taps or {}
    nc = bass.Bass("TRN2", target_bir_lowering=False)
    ctx = ExitStack()

    def dram_in(name, shape):
        return nc.dram_tensor(name, list(shape), F32, kind="ExternalInput")

    x_slab = dram_in("x_slab", (NCH, 128, TOK))
    wmod_slab = dram_in("wmod_slab", (12, 128, 2048))
    cT = dram_in("cT", (128, 32))
    w_in_slab = dram_in("w_in_slab", (16, 128, 2048))
    w_gate_slab = dram_in("w_gate_slab", (64, 128, 2048))
    b_gate_t = dram_in("b_gate_t", (128, 64))
    w_out_slab = dram_in("w_out_slab", (16, 128, 2048))
    w1_slab = dram_in("w1_slab", (NE, IC, 128, 2048))
    w3_slab = dram_in("w3_slab", (NE, IC, 128, 2048))
    w2_slab = dram_in("w2_slab", (NE, 16, 128, 1408))
    sw1_slab = dram_in("sw1_slab", (SIC, 128, 2048))
    sw3_slab = dram_in("sw3_slab", (SIC, 128, 2048))
    sw2_slab = dram_in("sw2_slab", (16, 128, 2816))
    gate_w_t = dram_in("gate_w_t", (128, 128))
    rms_t_in = dram_in("rms_t", (128, 4))
    cvec_in = dram_in("cvec", (128, 8))
    ident_in = dram_in("ident", (128, 128))

    y_out = nc.dram_tensor("y_slab", [NCH, 128, TOK], F32, kind="ExternalOutput")

    ag1_in = nc.dram_tensor("ag1_in", [1, 24 * 128], F32)
    ag1_out = nc.dram_tensor("ag1_out", [1, 24 * 128 * NCORES], F32)
    ag2_in = nc.dram_tensor("ag2_in", [1, 32 * 128], F32)
    ag2_out = nc.dram_tensor("ag2_out", [1, 32 * 128 * NCORES], F32)
    agw_in = nc.dram_tensor("agw_in", [1, 16], F32)
    agw_out = nc.dram_tensor("agw_out", [1, 16 * NCORES], F32)
    x2s = nc.dram_tensor("x2s", [NCH, 128, TOK], F32)

    dbg = {}
    for name, (shape, dt) in debug_taps.items():
        dbg[name] = nc.dram_tensor("dbg_" + name, list(shape), dt,
                                   kind="ExternalOutput")

    sb = lambda n, s, d: ctx.enter_context(nc.sbuf_tensor(n, s, d))
    BIGB = sb("BIGB", [128, NCH * TOK], BF16)
    BIGC = sb("BIGC", [128, NCH * TOK], BF16)
    BIGD = sb("BIGD", [128, NCH * TOK], BF16)
    BIGE = sb("BIGE", [128, NCH * TOK], BF16)

    fbuf = [sb(f"fbuf{i}", [128, TOK], F32) for i in range(2)]
    ubuf = [sb(f"ubuf{i}", [128, TOK], BF16) for i in range(2)]
    scr32 = [sb(f"scr32_{i}", [128, TOK], F32) for i in range(4)]
    scrb = [sb(f"scrb_{i}", [128, TOK], BF16) for i in range(6)]
    inv_t = sb("inv_t", [128, TOK], BF16)
    m2_t = sb("m2_t", [128, TOK], BF16)

    modl = sb("modl", [128, 24], F32)
    MG = sb("MG", [128, 192], F32)
    V = sb("V", [128, 96], F32)
    ABx = sb("ABx", [128, 32], F32)
    ABG = sb("ABG", [128, 256], F32)
    SCH = sb("SCH", [128, 128], F32)
    zci = sb("zci", [128, 16], F32)
    bt = sb("bt", [128, 64], F32)
    bth = sb("bth", [128, 64], F32)
    rms_t = sb("rms_sb", [128, 4], F32)
    cvec = sb("cvec_sb", [128, 8], F32)
    cT_sb = sb("cT_sb", [128, 32], F32)
    siluc = sb("siluc", [128, 32], BF16)
    gw_sb = sb("gw_sb", [128, 128], BF16)
    ident = sb("ident_sb", [128, 128], BF16)
    ones_m = sb("ones_m", [128, 128], BF16)
    ones_r = sb("ones_r", [128, 128], BF16)
    onesrow = sb("onesrow", [1, 128], BF16)
    zero1 = sb("zero1", [128, 1], F32)
    one1 = sb("one1", [128, 1], F32)
    eps_ln = sb("eps_ln", [128, 1], F32)
    eps_rms = sb("eps_rms", [128, 1], F32)
    LG = sb("LG", [8, TOK], BF16)
    LTM = sb("LTM", [128, 64], F32)
    WTM = sb("WTM", [128, 64], F32)
    wtb = sb("wtb", [128, 16], BF16)
    WEM = sb("WEM", [8, TOK], BF16)
    WROW1 = sb("WROW1", [1, TOK], BF16)
    WBC = sb("WBC", [128, TOK], BF16)
    m1s = sb("m1s", [128, 1], F32)
    m2s = sb("m2s", [128, 1], F32)
    nm1 = sb("nm1", [128, 1], F32)
    e2s = sb("e2s", [128, 1], F32)
    rds = sb("rds", [128, 1], F32)
    eqs = sb("eqs", [128, 8], F32)
    els = sb("els", [128, 8], F32)
    ges = sb("ges", [128, 8], F32)
    w1s = sb("w1s", [128, 8], F32)

    accD = BIGD.ap().bitcast(F32)
    accE = BIGE.ap().bitcast(F32)

    def acc_chunk(dc):
        base = accD if dc < 8 else accE
        return base[:, (dc % 8) * TOK:(dc % 8 + 1) * TOK]

    def big(t, ch):
        return t[:, ch * TOK:(ch + 1) * TOK]

    plan = Plan(nc, ctx)
    P = plan
    CC = plan.cc

    g_q = DmaQueue(plan, "g")
    y_q = DmaQueue(plan, "y")
    s_cst = Sem(ctx, nc, "s_cst")
    s_gcst = Sem(ctx, nc, "s_gcst")
    s_xbf = [Sem(ctx, nc, f"s_xbf{i}") for i in range(4)]
    s_agw = Sem(ctx, nc, "s_agw")
    s_ag1i = Sem(ctx, nc, "s_ag1i")
    s_ag2i = Sem(ctx, nc, "s_ag2i")
    s_mod = Sem(ctx, nc, "s_mod")
    s_abg = Sem(ctx, nc, "s_abg")
    s_xsl = [Sem(ctx, nc, f"s_xsl{i}") for i in range(3)]
    s_x2w = [Sem(ctx, nc, f"s_x2w{i}") for i in range(2)]
    s_wr = Sem(ctx, nc, "s_wr")
    s_out = [Sem(ctx, nc, f"s_out{i}") for i in range(3)]
    s_dbg = Sem(ctx, nc, "s_dbg")

    ws = WStream(plan, ctx, nc, nslots=4, cols=2816)
    pp = PsumPool(ctx, nc, 4)

    def mm(psum_ap, w_ap, x_ap, start, stop, waits=(), inc=False):
        return P.run("t", lambda e, o=psum_ap, w=w_ap, x=x_ap, st=start, sp=stop:
                     e.matmul(o, w, x, start=st, stop=sp), waits=waits, inc=inc)

    def mm_group(ps, wk_slice, rhs_fn, nk, waits=(), per_k=None):
        tok = None
        for k in range(nk):
            w_ap = wk_slice(k)
            for hh in range(2):
                last = (k == nk - 1 and hh == 1)
                wts = list(waits) if (k == 0 and hh == 0) else []
                if per_k is not None and hh == 0:
                    wts.append(per_k[k])
                tok = mm(ps[:, hh * 512:(hh + 1) * 512], w_ap,
                         rhs_fn(k, hh), start=(k == 0), stop=(k == nk - 1),
                         waits=wts, inc=last)
        return tok

    def mm_super(cs, w_srcs, ncols, rhs_big, per_k, out_fn, extra_first=()):
        slots = []
        for c in cs:
            slots.append(ws.load(w_srcs[c], ncols))
        pools = [pp.acquire() for _ in cs]
        tokm = None
        for k in range(16):
            for j, c in enumerate(cs):
                si, slot, rdy = slots[j]
                pi, ps, ptok = pools[j]
                for hh in range(2):
                    wts = []
                    if hh == 0 and j == 0:
                        wts.append(per_k[k])
                    if k == 0 and hh == 0:
                        wts += [rdy, ptok] + (list(extra_first) if j == 0 else [])
                    tokm = mm(ps[:, hh * 512:(hh + 1) * 512],
                              slot[:, k * 128:(k + 1) * 128],
                              big(rhs_big, k)[:, hh * 512:(hh + 1) * 512],
                              start=(k == 0), stop=(k == 15), waits=wts,
                              inc=(k == 15 and hh == 1))
        for j, c in enumerate(cs):
            si, slot, rdy = slots[j]
            pi, ps, ptok = pools[j]
            ws.free(si, tokm)
            tke = out_fn(c, ps, tokm)
            pp.release(pi, tke)

    def se_act(out_ap, in_ap, func, bias=0.0, scale=1.0, waits=(), inc=True):
        return P.run("s", lambda e, o=out_ap, i=in_ap, f=func, b=bias, sc=scale:
                     e.activation(o, i, f, bias=b, scale=sc), waits=waits, inc=inc,
                     r=[in_ap, bias, scale], w=[out_ap])

    def v_tt(out_ap, a_ap, b_ap, op, waits=(), inc=True):
        return P.run("v", lambda e, o=out_ap, a=a_ap, b=b_ap, p=op:
                     e.tensor_tensor(o, a, b, p), waits=waits, inc=inc,
                     r=[a_ap, b_ap], w=[out_ap])

    def v_ts(out_ap, in_ap, sc1, sc2, op0, op1, waits=(), inc=True):
        if op1 is None:
            return P.run("v", lambda e, o=out_ap, i=in_ap, s1=sc1, p0=op0:
                         e.tensor_scalar(o, i, s1, None, p0), waits=waits, inc=inc,
                         r=[in_ap, sc1], w=[out_ap])
        return P.run("v", lambda e, o=out_ap, i=in_ap, s1=sc1, s2=sc2, p0=op0, p1=op1:
                     e.tensor_scalar(o, i, s1, s2, p0, p1), waits=waits, inc=inc,
                     r=[in_ap, sc1, sc2], w=[out_ap])

    def v_stt(out_ap, in0_ap, sc, in1_ap, op0, op1, waits=(), inc=True):
        return P.run("v", lambda e, o=out_ap, i0=in0_ap, s=sc, i1=in1_ap, p0=op0, p1=op1:
                     e.scalar_tensor_tensor(o, i0, s, i1, p0, p1), waits=waits, inc=inc,
                     r=[in0_ap, sc, in1_ap], w=[out_ap])

    def v_copy(out_ap, in_ap, waits=(), inc=True):
        return P.run("v", lambda e, o=out_ap, i=in_ap: e.tensor_copy(o, i),
                     waits=waits, inc=inc, r=[in_ap], w=[out_ap])

    def v_scan(out_ap, d0, d1, init, op0, op1, waits=(), inc=True):
        return P.run("v", lambda e, o=out_ap, a=d0, b=d1, s=init, p0=op0, p1=op1:
                     e.tensor_tensor_scan(o, a, b, s, p0, p1), waits=waits, inc=inc,
                     r=[d0, d1, init], w=[out_ap])

    def v_memset(ap, val, waits=(), inc=False):
        return P.run("v", lambda e, a=ap, v=val: e.memset(a, v), waits=waits,
                     inc=inc, w=[ap])

    def v_reduce(out_ap, in_ap, op, waits=(), inc=True):
        return P.run("v", lambda e, o=out_ap, i=in_ap, p=op:
                     e.tensor_reduce(o, i, AX.X, p), waits=waits, inc=inc,
                     r=[in_ap], w=[out_ap])

    dbg_tok = {}
    _tap_prev = [None]

    def tap(name, ap, waits=()):
        if name in dbg:
            w = list(waits) + ([_tap_prev[0]] if _tap_prev[0] else [])
            dbg_tok[name] = y_q.dma(dbg[name].ap(), ap, s_dbg, waits=w)
            _tap_prev[0] = dbg_tok[name]

    def dtok(name):
        return [dbg_tok[name]] if name in dbg_tok else []

    def collective(in_t, out_t, waits=()):
        P.waits("g", waits)
        P.emit("g", lambda e, a=in_t, b=out_t: e.collective_compute(
            "AllGather", OP.bypass, replica_groups=[list(range(NCORES))],
            ins=[a.ap().opt()], outs=[b.ap().opt()]).then_inc(CC.h, 1))
        return (CC, CC.inc(1))

    tk_z = v_memset(zero1[:], 0.0, inc=True)
    t_agwz = y_q.dma(agw_in.ap(), zero1[0:16, :], s_agw, waits=[tk_z])
    collective(agw_in, agw_out, waits=[t_agwz])

    y_q.dma(cT_sb[:], cT.ap(), s_cst)
    y_q.dma(bt[:], b_gate_t.ap(), s_cst)
    y_q.dma(rms_t[:], rms_t_in.ap(), s_cst)
    t_cst_all = y_q.dma(cvec[:], cvec_in.ap(), s_cst)
    t_cT = t_bt = t_rms = t_cvec = t_cst_all
    g_q.dma(ident[:], ident_in.ap(), s_gcst)
    t_gcst_all = g_q.dma(gw_sb[:], gate_w_t.ap(), s_gcst)
    t_ident = t_gw = t_gcst_all

    v_memset(ones_m[:], 1.0 / 2048.0)
    v_memset(ones_r[:], 1.0 / 512.0)
    v_memset(onesrow[:], 1.0)
    v_memset(eps_ln[:], LN_EPS)
    v_memset(eps_rms[:], RMS_EPS)
    tk_ones = P.run("v", lambda e: e.memset(one1[:], 1.0), inc=True, w=[one1[:]])

    tk_bth = v_ts(bth[:], bt[:], 0.5, None, OP.mult, None, waits=[t_bt])
    tk_siluc = se_act(siluc[:], cT_sb[:], AF.Silu, waits=[t_cT])

    pi_mod, ps_mod, _ = pp.acquire()
    tok_mod = None
    for cm in range(12):
        si, slot, rdy = ws.load(wmod_slab[cm], 2048)
        for k in range(16):
            wts = [rdy, tk_siluc] if k == 0 else ()
            tok_mod = mm(ps_mod[:, cm * 2:(cm + 1) * 2],
                         slot[:, k * 128:(k + 1) * 128],
                         siluc[:, k * 2:(k + 1) * 2], start=(k == 0), stop=(k == 15),
                         waits=wts, inc=(k == 15))
        ws.free(si, tok_mod)
    tk_modl = v_copy(modl[:], ps_mod[:, 0:24], waits=[tok_mod])
    pp.release(pi_mod, tk_modl)
    t_ag1in = y_q.dma(ag1_in.ap(), modl[:], s_ag1i, waits=[tk_modl])
    tok_ag1 = collective(ag1_in, ag1_out, waits=[t_ag1in])

    t_MG = y_q.dma(bass.AP(MG, 0, [[192, 128], [1, 192]]),
                   bass.AP(ag1_out, 0, [[24, 128], [3072, 8], [1, 24]]),
                   s_mod, waits=[tok_ag1])
    mgb = lambda b: bass.AP(MG, b, [[192, 128], [24, 8], [2, 12]])
    vap = bass.AP(V, 0, [[96, 128], [12, 8], [1, 12]])
    v_ts(vap, mgb(0), cvec[:, 0:1], None, OP.mult, None,
         waits=[t_MG, t_cvec], inc=False)
    v_stt(vap, mgb(1), cvec[:, 1:2], vap, OP.mult, OP.add, inc=False)
    v_ts(V[:, 16:32], V[:, 16:32], 1.0, None, OP.add, None, inc=False)
    v_ts(V[:, 64:80], V[:, 64:80], 1.0, None, OP.add, None, inc=False)
    tk_V = None
    for c0 in (32, 80):
        gsl = V[:, c0:c0 + 16]
        v_ts(SCH[:, 0:16], gsl, 0.25, None, OP.max, None, inc=False)
        v_ts(SCH[:, 16:32], gsl, -0.25, None, OP.min, None, inc=False)
        v_ts(SCH[:, 32:48], gsl, 0.0, None, OP.is_ge, None, inc=False)
        v_tt(SCH[:, 48:64], SCH[:, 0:16], SCH[:, 16:32], OP.subtract, inc=False)
        v_tt(SCH[:, 48:64], SCH[:, 32:48], SCH[:, 48:64], OP.mult, inc=False)
        tk_V = v_tt(gsl, SCH[:, 48:64], SCH[:, 16:32], OP.add)
    tap("V", V[:], waits=[tk_V])

    def Vcol(c):
        return V[:, c:c + 1]

    xbf = BIGD
    xbf_grp = []
    for j in range(4):
        xbf_grp.append(g_q.dma(xbf[:, j * 4 * TOK:(j + 1) * 4 * TOK],
                               bass.AP(x_slab, j * 4 * 128 * TOK,
                                       [[TOK, 128], [128 * TOK, 4], [1, TOK]]),
                               s_xbf[j]))
    xbf_chunk = [xbf_grp[k // 4] for k in range(NCH)]

    def ln_stats(src_big, dep_toks, per_chunk=None):
        pc = per_chunk or [None] * NCH
        pi_m, ps_m, ptok_m = pp.acquire()
        pi_s, ps_s, ptok_s = pp.acquire()
        tokm = None
        for k in range(NCH):
            for hh in range(2):
                wts = (list(dep_toks) + [ptok_m, pc[k]]) if hh == 0 else ()
                tokm = mm(ps_m[:, hh * 512:(hh + 1) * 512], ones_m[:],
                          big(src_big, k)[:, hh * 512:(hh + 1) * 512],
                          start=(k == 0), stop=(k == NCH - 1), waits=wts,
                          inc=(k == NCH - 1 and hh == 1))
        toks = None
        mm_toks = []
        for k in range(NCH):
            sl = scrb[k % 2]
            w = [mm_toks[k - 2]] if k >= 2 else []
            tkq = v_tt(sl[:], big(src_big, k), big(src_big, k), OP.mult,
                       waits=(list(dep_toks) if k == 0 else []) + w + [pc[k]])
            for hh in range(2):
                wts = [tkq, ptok_s] if (k == 0 and hh == 0) else [tkq]
                toks = mm(ps_s[:, hh * 512:(hh + 1) * 512], ones_m[:],
                          sl[:, hh * 512:(hh + 1) * 512],
                          start=(k == 0), stop=(k == NCH - 1),
                          waits=wts, inc=(hh == 1))
            mm_toks.append(toks)
        return pi_m, ps_m, pi_s, ps_s, tokm, toks

    def ln_invm2(ps_m, ps_s, tok_m, tok_s, guard=()):
        tk1 = se_act(scr32[0][:], ps_m[:], AF.Square,
                     waits=[tok_m] + list(guard))
        tk2 = v_tt(scr32[1][:], ps_s[:], scr32[0][:], OP.subtract,
                   waits=[tok_s, tk1])
        tk3 = se_act(scr32[2][:], scr32[1][:], AF.Ln, bias=eps_ln[:], waits=[tk2, tk_ones])
        tk4 = se_act(inv_t[:], scr32[2][:], AF.Exp, scale=-0.5, waits=[tk3])
        tk5 = v_tt(m2_t[:], ps_m[:], inv_t[:], OP.mult, waits=[tk4])
        return tk4, tk5

    pi_m, ps_m, pi_s, ps_s, tok_mean, tok_sq = ln_stats(
        xbf, [], per_chunk=xbf_chunk)
    tok_inv, tok_m2 = ln_invm2(ps_m, ps_s, tok_mean, tok_sq)
    pp.release(pi_m, tok_m2)
    pp.release(pi_s, tok_m2)
    tap("inv1", inv_t[:], waits=[tok_inv])

    tok_h1_list = []
    for k in range(NCH):
        sl = scrb[2]
        v_tt(sl[:], big(xbf, k), inv_t[:], OP.mult,
             waits=[tok_inv] if k == 0 else (), inc=False)
        tok_xn = v_tt(big(BIGB, k), sl[:], m2_t[:], OP.subtract,
                      waits=[tok_m2] if k == 0 else ())
        tok_h1_list.append(se_act(big(BIGB, k), big(BIGB, k), AF.Identity,
                                  bias=Vcol(0 + k), scale=Vcol(16 + k),
                                  waits=[tok_xn] + ([tk_V] if k == 0 else [])))
    tok_h1 = tok_h1_list[-1]
    tap("h1", BIGB.ap(), waits=[tok_h1])

    tok_xp_evac = []

    def _xp_out(c, ps, tokm):
        tke = se_act(big(BIGC, c), ps[:], AF.Copy, bias=0.0, scale=1.0,
                     waits=[tokm])
        tok_xp_evac.append(tke)
        return tke

    mm_super([0, 1], w_in_slab, 2048, BIGB, tok_h1_list, _xp_out)
    mm_super([2, 3], w_in_slab, 2048, BIGB, tok_h1_list, _xp_out)
    for c in range(4, NCH):
        si, slot, rdy = ws.load(w_in_slab[c], 2048)
        pi, ps, ptok = pp.acquire()
        tokm = mm_group(ps, lambda k, s=slot: s[:, k * 128:(k + 1) * 128],
                        lambda k, hh: big(BIGB, k)[:, hh * 512:(hh + 1) * 512],
                        16, waits=[rdy, tok_h1, ptok])
        ws.free(si, tokm)
        tke = _xp_out(c, ps, tokm)
        pp.release(pi, tke)
    tok_xp = tok_xp_evac[-1]
    tap("xp", BIGC.ap(), waits=[tok_xp])

    prev_f = prev_u = None
    tok_last_scan = None
    for ch in range(NCH):
        pos = ch * 3
        f_sl = fbuf[ch % 2]
        u_sl = ubuf[ch % 2]
        si, slot, rdy = ws.load(w_gate_slab[pos], 2048)
        pi, ps, ptok = pp.acquire()
        tokm = mm_group(ps, lambda k, s=slot: s[:, k * 128:(k + 1) * 128],
                        lambda k, hh: big(BIGC, k)[:, hh * 512:(hh + 1) * 512],
                        16, waits=[rdy, tok_xp, ptok])
        ws.free(si, tokm)
        tk_th = se_act(scr32[0][:], ps[:], AF.Tanh, bias=bth[:, pos:pos + 1],
                       scale=0.5,
                       waits=[tokm, tk_bth] + ([prev_f] if prev_f else []))
        tk_f = v_ts(f_sl[:], scr32[0][:], 0.5, 0.5, OP.mult, OP.add,
                    waits=[tk_th])
        prev_f = tk_f
        pp.release(pi, tk_f)
        si, slot, rdy = ws.load(w_gate_slab[pos + 1], 2048)
        pi, ps, ptok = pp.acquire()
        tokm = mm_group(ps, lambda k, s=slot: s[:, k * 128:(k + 1) * 128],
                        lambda k, hh: big(BIGC, k)[:, hh * 512:(hh + 1) * 512],
                        16, waits=[rdy, ptok])
        ws.free(si, tokm)
        tk_i = se_act(scr32[1][:], ps[:], AF.Exp, bias=bt[:, pos + 1:pos + 2],
                      waits=[tokm] + ([prev_u] if prev_u else []))
        pp.release(pi, tk_i)
        si, slot, rdy = ws.load(w_gate_slab[pos + 2], 2048)
        pi, ps, ptok = pp.acquire()
        tokm = mm_group(ps, lambda k, s=slot: s[:, k * 128:(k + 1) * 128],
                        lambda k, hh: big(BIGC, k)[:, hh * 512:(hh + 1) * 512],
                        16, waits=[rdy, ptok])
        ws.free(si, tokm)
        tk_cd = se_act(scr32[2][:], ps[:], AF.Tanh, bias=bt[:, pos + 2:pos + 3],
                       waits=[tokm] + ([prev_u] if prev_u else []))
        tk_u = v_tt(u_sl[:], scr32[1][:], scr32[2][:], OP.mult,
                    waits=[tk_i, tk_cd])
        prev_u = tk_u
        pp.release(pi, tk_u)
        v_scan(big(BIGD, ch), f_sl[:], u_sl[:], zero1[:], OP.mult, OP.add,
               waits=[tk_f, tk_u, tk_ones], inc=False)
        tok_last_scan = v_scan(big(BIGE, ch), f_sl[:], u_sl[:], one1[:],
                               OP.mult, OP.bypass)

    tok_o = []
    for ch in range(NCH):
        pos = 48 + ch
        si, slot, rdy = ws.load(w_gate_slab[pos], 2048)
        pi, ps, ptok = pp.acquire()
        tokm = mm_group(ps, lambda k, s=slot: s[:, k * 128:(k + 1) * 128],
                        lambda k, hh: big(BIGC, k)[:, hh * 512:(hh + 1) * 512],
                        16, waits=[rdy, ptok])
        ws.free(si, tokm)
        tk_o = se_act(big(BIGB, ch), ps[:], AF.Sigmoid, bias=bt[:, pos:pos + 1],
                      waits=[tokm] + dtok("h1"))
        pp.release(pi, tk_o)
        tok_o.append(tk_o)

    tkA = v_copy(ABx[:, 0:16],
                 bass.AP(BIGE, TOK - 1, [[NCH * TOK, 128], [TOK, NCH]]),
                 waits=[tok_last_scan])
    tkB = v_copy(ABx[:, 16:32],
                 bass.AP(BIGD, TOK - 1, [[NCH * TOK, 128], [TOK, NCH]]))
    t_ag2in = y_q.dma(ag2_in.ap(), ABx[:], s_ag2i, waits=[tkB])
    tok_ag2 = collective(ag2_in, ag2_out, waits=[t_ag2in])
    tap("ctil", BIGD.ap(), waits=[tkB])
    tap("Pcum", BIGE.ap(), waits=[tkB])

    t_abg = y_q.dma(bass.AP(ABG, 0, [[256, 128], [1, 256]]),
                    bass.AP(ag2_out, 0, [[32, 128], [4096, 8], [1, 32]]),
                    s_abg, waits=[tok_ag2])

    def abg_A(r):
        return ABG[:, r * 32:r * 32 + 16]

    def abg_B(r):
        return ABG[:, r * 32 + 16:r * 32 + 32]

    def sch(r):
        return SCH[:, r * 16:(r + 1) * 16]

    v_copy(sch(0), abg_B(0), waits=[t_abg], inc=False)
    v_copy(sch(4), abg_B(4), inc=False)
    for r in (1, 2, 3, 5, 6, 7):
        v_tt(sch(r), abg_A(r), sch(r - 1), OP.mult, inc=False)
        v_tt(sch(r), sch(r), abg_B(r), OP.add, inc=False)
    v_memset(zci[:], 0.0)
    tk_ci = None
    for idx, r in enumerate((0, 1, 2, 4, 5, 6)):
        tk_ci = v_stt(zci[:], sch(r), cvec[:, 2 + idx:3 + idx], zci[:],
                      OP.mult, OP.add, waits=[t_cvec] if idx == 0 else ())
    tap("cin", zci[:], waits=[tk_ci])
    tok_fix = []
    for ch in range(NCH):
        tok_fix.append(v_stt(big(BIGD, ch), big(BIGE, ch), zci[:, ch:ch + 1],
                             big(BIGD, ch), OP.mult, OP.add,
                             waits=([tk_ci] + dtok("ctil") + dtok("Pcum"))
                             if ch == 0 else ()))
    tap("cfix", BIGD.ap(), waits=[tok_fix[-1]])

    rsq = [scrb[2 + h] for h in range(4)]
    tok_rsq = []
    csq_mm_tok = [None, None]
    for h in range(4):
        pi, ps, ptok = pp.acquire()
        tokm = None
        for q in range(4):
            ch = h * 4 + q
            sl = scrb[q % 2]
            w = [csq_mm_tok[q % 2]] if csq_mm_tok[q % 2] else []
            tkq = v_tt(sl[:], big(BIGD, ch), big(BIGD, ch), OP.mult,
                       waits=[tok_fix[ch]] + w)
            for hh in range(2):
                wts = [tkq, ptok] if (q == 0 and hh == 0) else [tkq]
                tokm = mm(ps[:, hh * 512:(hh + 1) * 512], ones_r[:],
                          sl[:, hh * 512:(hh + 1) * 512],
                          start=(q == 0), stop=(q == 3),
                          waits=wts, inc=(hh == 1))
            csq_mm_tok[q % 2] = tokm
        tk1 = se_act(scr32[0][:], ps[:], AF.Ln, bias=eps_rms[:], waits=[tokm, tk_ones])
        tk2 = se_act(rsq[h][:], scr32[0][:], AF.Exp, scale=-0.5, waits=[tk1])
        pp.release(pi, tk2)
        tok_rsq.append(tk2)
    fb_bf = [f.ap().bitcast(BF16) for f in fbuf]
    s3_bf = scr32[3].ap().bitcast(BF16)
    cns = [fb_bf[0][:, 0:TOK], fb_bf[1][:, 0:TOK],
           s3_bf[:, 0:TOK], s3_bf[:, TOK:2 * TOK]]
    ths = [ubuf[0][:], ubuf[1][:]]
    tkt_l = [None] * NCH
    tkh_l = [None] * NCH
    tkc_l = [None] * NCH

    def emit_tkc(ch):
        h = ch // 4
        cw = [tok_rsq[h]]
        if ch >= 4 and tkt_l[ch - 4] is not None:
            cw.append(tkt_l[ch - 4])
        tkc_l[ch] = v_tt(cns[ch % 4], big(BIGD, ch), rsq[h][:], OP.mult,
                         waits=cw)

    def emit_tkt(ch):
        q = ch % 4
        tw = [tkc_l[ch], t_rms]
        if ch >= 2 and tkh_l[ch - 2] is not None:
            tw.append(tkh_l[ch - 2])
        tkt_l[ch] = se_act(ths[ch % 2], cns[ch % 4], AF.Tanh,
                           scale=rms_t[:, q:q + 1], waits=tw)

    def emit_tkh(ch):
        tkh_l[ch] = v_tt(big(BIGB, ch), big(BIGB, ch), ths[ch % 2], OP.mult,
                         waits=[tkt_l[ch], tok_o[ch]])

    for ch in range(NCH):
        emit_tkc(ch)
        emit_tkt(ch)
        if ch >= 2:
            emit_tkh(ch - 2)
    emit_tkh(NCH - 2)
    emit_tkh(NCH - 1)
    tok_hout = tkh_l[NCH - 1]
    tap("hout", BIGB.ap(), waits=[tok_hout])

    tok_x2bf = []
    xsl_state = {}

    def _wout_out(dc, ps, tokm):
        xs = scr32[dc % 3]
        xw = [tok_fix[-1], tok_rsq[-1]] if dc < 3 else [tok_x2bf[dc - 3]]
        t_x = y_q.dma(xs[:], x_slab[dc], s_xsl[dc % 3], waits=xw)
        x2w = [tokm, tk_V, t_x]
        if dc == 0:
            x2w += dtok("cfix") + dtok("ctil") + dtok("Pcum") + dtok("hout")
        tk_x2 = v_stt(acc_chunk(dc), ps[:], Vcol(32 + dc), xs[:],
                      OP.mult, OP.add, waits=x2w)
        tok_x2bf.append(v_copy(big(BIGC, dc), acc_chunk(dc),
                               waits=[tk_x2] + dtok("xp")))
        return tk_x2

    mm_super([0, 1], w_out_slab, 2048, BIGB, tkh_l, _wout_out)
    mm_super([2, 3], w_out_slab, 2048, BIGB, tkh_l, _wout_out)
    for dc in range(4, NCH):
        si, slot, rdy = ws.load(w_out_slab[dc], 2048)
        pi, ps, ptok = pp.acquire()
        tokm = mm_group(ps, lambda k, s=slot: s[:, k * 128:(k + 1) * 128],
                        lambda k, hh: big(BIGB, k)[:, hh * 512:(hh + 1) * 512],
                        16, waits=[rdy, tkh_l[NCH - 1], ptok])
        ws.free(si, tokm)
        tke = _wout_out(dc, ps, tokm)
        pp.release(pi, tke)
    tok_acc = list(tok_x2bf)
    tok_x2bf_all = tok_x2bf[-1]
    tap("x2bf", BIGC.ap(), waits=[tok_x2bf_all])

    pi_m, ps_m, pi_s, ps_s, tok_mean2, tok_sq2 = ln_stats(
        BIGC, [], per_chunk=tok_x2bf)
    tok_inv2, tok_m22 = ln_invm2(ps_m, ps_s, tok_mean2, tok_sq2,
                                 guard=[tok_x2bf_all] + dtok("inv1"))
    pp.release(pi_m, tok_m22)
    pp.release(pi_s, tok_m22)

    tok_h2_list = []
    for k in range(NCH):
        sl = scrb[2]
        v_tt(sl[:], big(BIGC, k), inv_t[:], OP.mult,
             waits=[tok_inv2] if k == 0 else (), inc=False)
        tok_xn2 = v_tt(big(BIGB, k), sl[:], m2_t[:], OP.subtract,
                       waits=([tok_m22] if k == 0 else []) +
                       (dtok("hout") if k == 0 else []))
        tok_h2_list.append(se_act(big(BIGB, k), big(BIGB, k), AF.Identity,
                                  bias=Vcol(48 + k), scale=Vcol(64 + k),
                                  waits=[tok_xn2]))
    tok_h2 = tok_h2_list[-1]
    tap("h2", BIGB.ap(), waits=[tok_h2])

    pi_l, ps_l, ptok_l = pp.acquire()
    tok_lg = None
    for k in range(NCH):
        for hh in range(2):
            wts = [tok_h2_list[k], t_gw, ptok_l] if hh == 0 else ()
            tok_lg = mm(ps_l[0:8, hh * 512:(hh + 1) * 512],
                        gw_sb[:, k * 8:(k + 1) * 8],
                        big(BIGB, k)[:, hh * 512:(hh + 1) * 512],
                        start=(k == 0), stop=(k == NCH - 1), waits=wts,
                        inc=(k == NCH - 1 and hh == 1))
    tk_lgc = v_copy(LG[:], ps_l[0:8, 0:TOK], waits=[tok_lg])
    pp.release(pi_l, tk_lgc)
    tap("logits", LG.ap(), waits=[tk_lgc])

    pi_tr, ps_tr, ptok_tr = pp.acquire()
    pi_tr2, ps_tr2, ptok_tr2 = pp.acquire()
    psb_tr = ps_tr.ap().bitcast(BF16)
    psb_tr2 = ps_tr2.ap().bitcast(BF16)
    _g_state = {"tok_wem": None, "tk_tr2": None}

    def _gate_tt(tt):
        tk_tr = P.run("t", lambda e, o=psb_tr[:, tt * 8:(tt + 1) * 8],
                      i=LG[0:8, tt * 128:(tt + 1) * 128], idn=ident[0:8, 0:8]:
                      e.transpose(o, i, idn),
                      waits=([tk_lgc, t_ident, ptok_tr] if tt == 0 else ()),
                      inc=True)
        l_tm = LTM[:, tt * 8:(tt + 1) * 8]
        v_copy(l_tm, psb_tr[:, tt * 8:(tt + 1) * 8], waits=[tk_tr], inc=False)
        v_reduce(m1s[:], l_tm, OP.max, inc=False)
        v_ts(eqs[:], l_tm, m1s[:], None, OP.is_equal, None, inc=False)
        v_stt(els[:], eqs[:], -1e30, l_tm, OP.mult, OP.add, inc=False)
        v_reduce(m2s[:], els[:], OP.max, inc=False)
        tk_pre = v_ts(e2s[:], m2s[:], m1s[:], None, OP.subtract, None)
        tk_e = se_act(e2s[:], e2s[:], AF.Exp, waits=[tk_pre])
        v_ts(e2s[:], e2s[:], 1.0, None, OP.add, None, waits=[tk_e], inc=False)
        P.run("v", lambda e, o=rds[:], i=e2s[:]: e.reciprocal(o, i), inc=False,
              r=[e2s[:]], w=[rds[:]])
        tk_nm = v_ts(nm1[:], m1s[:], -1.0, None, OP.mult, None)
        tk_el = se_act(els[:], l_tm, AF.Exp, bias=nm1[:], waits=[tk_nm])
        v_ts(ges[:], l_tm, m2s[:], None, OP.is_ge, None, inc=False)
        v_ts(w1s[:], els[:], rds[:], None, OP.mult, None, waits=[tk_el],
             inc=False)
        tk_w = v_tt(WTM[:, tt * 8:(tt + 1) * 8], w1s[:], ges[:], OP.mult)
        tk_wb = v_copy(wtb[:, tt % 2:tt % 2 + 1].bitcast(BF16) if False else
                       wtb[:, (tt % 2) * 8:(tt % 2) * 8 + 8],
                       WTM[:, tt * 8:(tt + 1) * 8],
                       waits=[tk_w] + ([_g_state["tk_tr2"]] if _g_state["tk_tr2"] else []))
        _g_state["tk_wb"] = tk_wb

    def _gate_tt2(tt):
        tk_tr2 = P.run("t", lambda e,
                       o=psb_tr2[0:8, tt * 128:(tt + 1) * 128],
                       i=wtb[:, (tt % 2) * 8:(tt % 2) * 8 + 8], idn=ident[:]:
                       e.transpose(o, i, idn),
                       waits=[_g_state["tk_wb"]] + ([ptok_tr2] if tt == 0 else []),
                       inc=True)
        _g_state["tk_tr2"] = tk_tr2
        _g_state["tok_wem"] = v_copy(WEM[0:8, tt * 128:(tt + 1) * 128],
                                     psb_tr2[0:8, tt * 128:(tt + 1) * 128],
                                     waits=[tk_tr2])
        if tt == 7:
            pp.release(pi_tr, _g_state["tok_wem"])
            pp.release(pi_tr2, _g_state["tok_wem"])
            tap("wem", WEM.ap(), waits=[_g_state["tok_wem"]])

    def _gate_step(j):
        if 1 <= j <= 8:
            _gate_tt2(j - 1)
        if j <= 7:
            _gate_tt(j)

    gate_fillers = [lambda j=j: _gate_step(j) for j in range(9)]

    tok_pe_used = [None] * IC
    experts = [("s", 0)] + [("r", e) for e in range(NE)] + [("s", 1)]
    first = True
    tok_wbc = None
    prev_gp_read = None
    prev_sil_read = [None, None]
    for kind, e in experts:
        routed = kind == "r"
        if routed:
            t_wr = y_q.dma(WROW1[:], WEM[e:e + 1, :], s_wr,
                           waits=[_g_state["tok_wem"]] + ([tok_wbc] if tok_wbc else []))
            pi_b, ps_b, ptok_b = pp.acquire()
            tokb = None
            for hh in range(2):
                wts = [t_wr, ptok_b] if hh == 0 else ()
                tokb = mm(ps_b[:, hh * 512:(hh + 1) * 512], onesrow[:],
                          WROW1[0:1, hh * 512:(hh + 1) * 512],
                          start=True, stop=True, waits=wts, inc=(hh == 1))
            wbw = [tokb]
            if prev_gp_read is not None:
                wbw.append(prev_gp_read)
            tok_wbc = se_act(WBC[:], ps_b[:], AF.Copy, bias=0.0, scale=1.0,
                             waits=wbw)
            pp.release(pi_b, tok_wbc)
        tok_pe_last = None
        tok_pe_list = []
        for ic in range(IC):
            icg = ic if (routed or e == 0) else ic + IC
            w1src = w1_slab[e][ic] if routed else sw1_slab[icg]
            w3src = w3_slab[e][ic] if routed else sw3_slab[icg]
            si1, slot1, rdy1 = ws.load(w1src, 2048)
            pi_h, ps_h, ptok_h = pp.acquire()
            tok_hm = mm_group(ps_h, lambda k, s=slot1: s[:, k * 128:(k + 1) * 128],
                              lambda k, hh: big(BIGB, k)[:, hh * 512:(hh + 1) * 512],
                              16, waits=[rdy1, tok_h2_list[0], ptok_h],
                              per_k=tok_h2_list)
            ws.free(si1, tok_hm)
            si3, slot3, rdy3 = ws.load(w3src, 2048)
            pi_g, ps_g, ptok_g = pp.acquire()
            tok_gm = mm_group(ps_g, lambda k, s=slot3: s[:, k * 128:(k + 1) * 128],
                              lambda k, hh: big(BIGB, k)[:, hh * 512:(hh + 1) * 512],
                              16, waits=[rdy3, ptok_g])
            ws.free(si3, tok_gm)
            sil = scrb[ic % 2]
            silw = [tok_hm]
            if prev_sil_read[ic % 2] is not None:
                silw.append(prev_sil_read[ic % 2])
            tk_sil = se_act(sil[:], ps_h[:], AF.Silu, waits=silw)
            pp.release(pi_h, tk_sil)
            pe_guard = [tok_pe_used[ic]] if tok_pe_used[ic] else []
            if routed:
                gp = scrb[2 + ic % 2]
                tk_gp = v_tt(gp[:], ps_g[:], WBC[:], OP.mult,
                             waits=[tok_gm, tok_wbc])
                prev_gp_read = tk_gp
                tk_pe = v_tt(big(BIGC, ic), sil[:], gp[:], OP.mult,
                             waits=[tk_sil, tk_gp] + pe_guard + dtok("x2bf"))
            else:
                tk_pe = v_tt(big(BIGC, ic), sil[:], ps_g[:], OP.mult,
                             waits=[tk_sil, tok_gm] + pe_guard)
            prev_sil_read[ic % 2] = tk_pe
            pp.release(pi_g, tk_pe)
            tok_pe_used[ic] = None
            tok_pe_last = tk_pe
            tok_pe_list.append(tk_pe)
            if kind == "s" and e == 0 and ic < 9:
                gate_fillers[ic]()
        for dc in range(NCH):
            w2src = w2_slab[e][dc] if routed else sw2_slab[dc]
            ncols = 1408 if routed else 2816
            coff = 0 if (routed or e == 0) else 1408
            si2, slot2, rdy2 = ws.load(w2src, ncols)
            pi_y, ps_y, ptok_y = pp.acquire()
            toky = None
            for ic in range(IC):
                for hh in range(2):
                    wts = []
                    if hh == 0 and dc == 0:
                        wts.append(tok_pe_list[ic])
                    if ic == 0 and hh == 0:
                        wts += [rdy2, ptok_y]
                    toky = mm(ps_y[:, hh * 512:(hh + 1) * 512],
                              slot2[:, coff + ic * 128:coff + (ic + 1) * 128],
                              big(BIGC, ic)[:, hh * 512:(hh + 1) * 512],
                              start=(ic == 0), stop=(ic == IC - 1),
                              waits=wts, inc=(ic == IC - 1 and hh == 1))
            ws.free(si2, toky)
            if dc == NCH - 1:
                for ic2 in range(IC):
                    tok_pe_used[ic2] = toky
            tk_a = v_stt(acc_chunk(dc), ps_y[:], Vcol(80 + dc),
                         acc_chunk(dc), OP.mult, OP.add,
                         waits=[toky, tok_acc[dc]])
            tok_acc[dc] = tk_a
            pp.release(pi_y, tk_a)
        first = False

    for dc in range(NCH):
        y_q.dma(y_out[dc], acc_chunk(dc), s_out[0], waits=[tok_acc[dc]])
    P.emit("y", lambda e, v=s_out[0].n, h=s_out[0].h: e.wait_ge(h, v))
    if dbg:
        P.emit("y", lambda e, v=s_dbg.n: e.wait_ge(s_dbg.h, v))

    with ctx:
        with nc.Block() as block:
            @block.tensor
            def _(e):
                for fn in plan.ops["t"]:
                    fn(e)

            @block.vector
            def _(e):
                for fn in plan.ops["v"]:
                    fn(e)

            @block.scalar
            def _(e):
                for fn in plan.ops["s"]:
                    fn(e)

            @block.gpsimd
            def _(e):
                for fn in plan.ops["g"]:
                    fn(e)

            @block.sync
            def _(e):
                for fn in plan.ops["y"]:
                    fn(e)

    return nc


def _slab(WT):
    din, dout = WT.shape
    nk, C = din // 128, dout // 128
    return np.ascontiguousarray(
        WT.reshape(nk, 128, C, 128).transpose(2, 1, 0, 3).reshape(C, 128, nk * 128))


def prepare_inputs(inputs):
    f32 = lambda k: np.asarray(inputs[k], dtype=np.float32)
    x = f32("L0_attn_x")
    W_in = f32("L0_attn_W_in")
    W_gate = f32("L0_attn_W_gate")
    b_gate = f32("L0_attn_b_gate")
    rms_w = f32("L0_attn_rms_w")
    W_out = f32("L0_attn_W_out")
    gate_w = f32("L0_ffn_gate_w")
    w1 = f32("L0_ffn_w1")
    w3 = f32("L0_ffn_w3")
    w2 = f32("L0_ffn_w2")
    sw1 = f32("L0_ffn_sw1")
    sw3 = f32("L0_ffn_sw3")
    sw2 = f32("L0_ffn_sw2")
    cc = f32("L0_c")
    wmod = f32("L0_wmod")

    shared = {}
    shared["w_in_slab"] = _slab(np.ascontiguousarray(W_in.T))
    wg = _slab(np.ascontiguousarray(W_gate.T))
    shared["w_gate_slab"] = np.ascontiguousarray(wg[GATE_ORDER])
    b2 = np.ascontiguousarray(b_gate.reshape(64, 128).T)
    shared["b_gate_t"] = np.ascontiguousarray(b2[:, GATE_ORDER])
    shared["w_out_slab"] = _slab(np.ascontiguousarray(W_out.T))
    shared["w1_slab"] = np.stack([_slab(w1[e]) for e in range(NE)])
    shared["w3_slab"] = np.stack([_slab(w3[e]) for e in range(NE)])
    shared["w2_slab"] = np.stack([_slab(w2[e]) for e in range(NE)])
    shared["sw1_slab"] = _slab(sw1)
    shared["sw3_slab"] = _slab(sw3)
    shared["sw2_slab"] = _slab(sw2)
    shared["gate_w_t"] = np.ascontiguousarray(
        gate_w.T.reshape(16, 128, 8).transpose(1, 0, 2).reshape(128, 128))
    shared["rms_t"] = np.ascontiguousarray(rms_w.reshape(4, 128).T)
    shared["cT"] = np.ascontiguousarray(
        cc.T.reshape(16, 128, 2).transpose(1, 0, 2).reshape(128, 32))
    shared["ident"] = np.eye(128, dtype=np.float32)

    in_maps = []
    for core in range(NCORES):
        b, q = core // 4, core % 4
        m = dict(shared)
        xs = np.ascontiguousarray(x[b, q * TOK:(q + 1) * TOK, :].T)
        m["x_slab"] = np.ascontiguousarray(xs.reshape(NCH, 128, TOK))
        m["wmod_slab"] = _slab(np.ascontiguousarray(
            wmod[:, core * 1536:(core + 1) * 1536]))
        cv = np.zeros((128, 8), dtype=np.float32)
        cv[:, 0] = 1.0 - b
        cv[:, 1] = float(b)
        for idx, r in enumerate((0, 1, 2, 4, 5, 6)):
            if q != 0 and r == core - 1:
                cv[:, 2 + idx] = 1.0
        m["cvec"] = cv
        in_maps.append(m)
    return in_maps


def kernel(**inputs):
    in_maps = prepare_inputs(inputs)
    nc = build()
    res = run_bass_kernel_spmd(nc, in_maps, core_ids=list(range(NCORES)))
    out = np.empty((2, 4096, 2048), dtype=np.float32)
    for core in range(NCORES):
        b, q = core // 4, core % 4
        y = np.asarray(res.results[core]["y_slab"])
        out[b, q * TOK:(q + 1) * TOK, :] = y.reshape(2048, TOK).T
    return out
